# version 11
# baseline (speedup 1.0000x reference)
import contextlib
import numpy as np
import ml_dtypes
import concourse.bass as bass
import concourse.mybir as mybir
from concourse.bass_utils import run_bass_kernel_spmd

F32 = mybir.dt.float32
BF16 = mybir.dt.bfloat16
I32 = mybir.dt.int32
AF = mybir.ActivationFunctionType
ALU = mybir.AluOpType
AX = mybir.AxisListType
BF = ml_dtypes.bfloat16

D = 1024
H = 16
HD = 64
LH = 4
LHD = 64
LD = 256
CH = 32
INNER = 4096
WIN = 128

SEG_EL = 8192
RING_SLOTS = 2


def make_cfg(B=2, S=2048):
    T = 512 if S >= 512 else S
    return dict(B=B, S=S, T=T, NT=S // T, QG=T // 128, NLAT=S // CH, STI=S // 128)



def build_consts(cfg):
    T, STI, NLAT = cfg["T"], cfg["STI"], cfg["NLAT"]
    c = {}
    c["ident"] = np.eye(128, dtype=BF)
    c["identf32"] = np.eye(128, dtype=np.float32)
    qi = np.arange(128)[:, None]
    kj = np.arange(256)[None, :]
    binB = ((kj >= qi) & (kj <= qi + 127)).astype(BF)
    binA = ((kj >= np.maximum(qi, 128)) & (kj <= qi + 127)).astype(BF)
    c["binB"] = binB
    c["binA"] = binA
    lat = np.zeros((128, STI * NLAT), dtype=BF)
    for sti in range(STI):
        p = np.arange(128)[:, None]
        l = np.arange(NLAT)[None, :]
        lat[:, sti * NLAT:(sti + 1) * NLAT] = (l < (sti * 4 + p // 32)).astype(BF)
    c["latbin"] = lat
    P4 = np.zeros((128, 4), dtype=np.float32)
    P4[np.arange(128), np.arange(128) // 32] = 1.0 / 32.0
    c["P4"] = P4.astype(BF)
    sel = np.zeros((16, 8 * 128), dtype=BF)
    for ci in range(8):
        for p in range(128):
            sel[2 * ci + p // 64, ci * 128 + p] = 1.0
    c["sel"] = sel
    return c



def _pack_ws(w, och_count, k_count):
    d_in, d_out = w.shape
    assert d_in == k_count * 128 and d_out == och_count * 128
    t = w.reshape(k_count, 128, och_count, 128)
    t = np.transpose(t, (1, 2, 0, 3))
    return np.ascontiguousarray(t.reshape(128, och_count * k_count * 128))


def _pack_rhs(w, k_count):
    d_in, d_out = w.shape
    assert d_in == k_count * 128
    t = w.reshape(k_count, 128, d_out)
    t = np.transpose(t, (1, 0, 2))
    return np.ascontiguousarray(t.reshape(128, k_count * d_out))


def prep_core_inputs(ci, n_cores, cfg, inp):
    B, S = cfg["B"], cfg["S"]
    x = np.ascontiguousarray(inp["L0_attn_x"][ci * B:(ci + 1) * B])
    qkv_w = np.asarray(inp["L0_attn_qkv_w"])
    m = {}
    m["xin"] = x
    m["tq"] = _pack_ws(qkv_w[:, 0:1024], 8, 8)
    m["tk"] = _pack_ws(qkv_w[:, 1024:2048], 8, 8)
    m["tv"] = _pack_rhs(qkv_w[:, 2048:3072], 8)
    m["trq"] = _pack_ws(np.asarray(inp["L0_attn_rq_w"]), 2, 8)
    m["trout"] = _pack_ws(np.asarray(inp["L0_attn_rout_w"]), 8, 2)
    m["tow"] = _pack_rhs(np.asarray(inp["L0_attn_out_w"]), 8)
    m["trkv"] = _pack_ws(np.asarray(inp["L0_attn_rkv_w"]), 4, 8)
    wg = np.ascontiguousarray(np.asarray(inp["L0_ffn_w_gate"]).T)
    wu = np.ascontiguousarray(np.asarray(inp["L0_ffn_w_up"]).T)
    wd = np.ascontiguousarray(np.asarray(inp["L0_ffn_w_down"]).T)
    gw = wg.reshape(8, 128, 32, 128)
    uw = wu.reshape(8, 128, 32, 128)
    gu = np.empty((128, 64 * 8 * 128), dtype=np.float32)
    for og in range(64):
        src = gw if og % 2 == 0 else uw
        oc = og // 2
        blk = np.transpose(src[:, :, oc, :], (1, 0, 2)).reshape(128, 8 * 128)
        gu[:, og * 1024:(og + 1) * 1024] = blk
    m["tgu"] = np.ascontiguousarray(gu)
    m["tdn"] = _pack_rhs(wd, 32)
    qkv_b = np.asarray(inp["L0_attn_qkv_b"])
    m["bcol_qk"] = np.ascontiguousarray(qkv_b[0:2048].reshape(16, 128).T)
    m["bcol_rq"] = np.ascontiguousarray(np.asarray(inp["L0_attn_rq_b"]).reshape(2, 128).T)
    m["bcol_rkv"] = np.ascontiguousarray(np.asarray(inp["L0_attn_rkv_b"]).reshape(4, 128).T)
    m["bcol_rout"] = np.ascontiguousarray(np.asarray(inp["L0_attn_rout_b"]).reshape(8, 128).T)
    m["brow_v"] = np.ascontiguousarray(qkv_b[2048:3072].reshape(1, 1024))
    m["brow_out"] = np.ascontiguousarray(np.asarray(inp["L0_attn_out_b"]).reshape(1, 1024))
    m["gl"] = np.ascontiguousarray(np.asarray(inp["L0_attn_gl_w"]).reshape(64, 1))
    m["gr"] = np.ascontiguousarray(np.asarray(inp["L0_attn_gr_w"]).reshape(64, 1))
    return m



ENGS = ("sync", "tensor", "vector", "scalar", "gpsimd")


class Em:
    def __init__(self):
        self.q = {e: [] for e in ENGS}
        self.written = {e: set() for e in ENGS}
        self.readden = {e: set() for e in ENGS}
        self.counts = {}
        self.ranges = {}

    def _overlap(self, s1, s2):
        for a in s1:
            for b in s2:
                if a == b:
                    return True
                ra, rb = self.ranges.get(a), self.ranges.get(b)
                if ra and rb and ra[0] < rb[1] and rb[0] < ra[1]:
                    return True
        return False

    def _names(self, aps):
        out = set()
        for a in aps:
            if a is None or isinstance(a, (int, float)):
                continue
            out.add(a.tensor.name)
        return out

    def op(self, eng, fn, reads=(), writes=()):
        if getattr(self, "frozen", False):
            return
        if eng in ("vector", "scalar", "gpsimd"):
            rd = self._names(reads)
            wr = self._names(writes)
            if (self._overlap(wr, self.written[eng] | self.readden[eng])
                    or self._overlap(rd, self.written[eng])):
                self.q[eng].append(("drain",))
                self.written[eng] = set()
                self.readden[eng] = set()
            self.written[eng] |= wr
            self.readden[eng] |= rd
        self.q[eng].append(("op", fn))

    def inc(self, eng, sem, n=1):
        if getattr(self, "frozen", False):
            return self.counts.get(sem, 0)
        self.counts[sem] = self.counts.get(sem, 0) + n
        q = self.q[eng]
        i = len(q) - 1
        while i >= 0 and q[i][0] == "wait":
            i -= 1
        if i >= 0 and q[i][0] == "inc":
            q.append(("op", lambda e: e.nop(nofuse=True)))
        q.append(("inc", sem, n))
        return self.counts[sem]

    def wait(self, eng, sem, n=None):
        if getattr(self, "frozen", False):
            return
        if n is None:
            n = self.counts.get(sem, 0)
        if n > 0:
            self.q[eng].append(("wait", sem, n))

    def mark(self, eng):
        q = self.q[eng]
        i = len(q) - 1
        while i >= 0 and q[i][0] == "wait":
            i -= 1
        if i >= 0 and q[i][0] == "inc" and q[i][1] == "ms_" + eng:
            return self.counts["ms_" + eng]
        return self.inc(eng, "ms_" + eng, 1)

    def join(self, eng, src):
        self.wait(eng, "ms_" + src)

    def cnt(self, sem):
        return self.counts.get(sem, 0)

    def tt(self, eng, out, a, b, aop):
        self.op(eng, lambda e: e.tensor_tensor(out, a, b, aop), reads=[a, b], writes=[out])

    def ts(self, eng, out, a, s1, s2=None, op0=ALU.mult, op1=ALU.bypass):
        rd = [a] + [x for x in (s1, s2) if hasattr(x, "tensor")]
        self.op(eng, lambda e: e.tensor_scalar(out, a, s1, s2, op0, op1), reads=rd, writes=[out])

    def stt(self, eng, out, in0, scalar, in1, op0, op1):
        rd = [in0, in1] + ([scalar] if hasattr(scalar, "tensor") else [])
        self.op(eng, lambda e: e.scalar_tensor_tensor(out, in0, scalar, in1, op0, op1),
                reads=rd, writes=[out])

    def copy(self, eng, out, a):
        self.op(eng, lambda e: e.tensor_copy(out, a), reads=[a], writes=[out])

    def memset(self, eng, ap, v):
        self.op(eng, lambda e: e.memset(ap, v), reads=[], writes=[ap])

    def red(self, eng, out, a, rop, absval=False, neg=False):
        if rop == "max":
            self.op(eng, lambda e: e.reduce_max(out, a, AX.X, apply_absolute_value=absval,
                                                negate=neg), reads=[a], writes=[out])
        else:
            self.op(eng, lambda e: e.reduce_sum(out, a, AX.X, apply_absolute_value=absval,
                                                negate=neg), reads=[a], writes=[out])

    def recip(self, eng, out, a):
        self.op(eng, lambda e: e.reciprocal(out, a), reads=[a], writes=[out])

    def act(self, out, a, func, bias=0.0, scale=1.0, accum=None):
        rd = [a] + [x for x in (bias, scale) if hasattr(x, "tensor")]
        wr = [out] + ([accum] if accum is not None else [])
        self.op("scalar", lambda e: e.activation(out, a, func, bias=bias, scale=scale,
                                                 accum_out=accum), reads=rd, writes=wr)

    def mm(self, out, lhsT, rhs, start, stop):
        self.op("tensor", lambda e: e.matmul(out, lhsT, rhs, start=start, stop=stop))

    def trans(self, out, a, ident):
        self.op("tensor", lambda e: e.transpose(out, a, ident))

    def dma(self, eng, out, a, sem):
        self.op(eng, lambda e: e.dma_start(out=out, in_=a), reads=[a], writes=[out])
        return self.inc(eng, sem, 16)

    def table(self, set_name):
        if getattr(self, "frozen", False):
            return
        def fn(e):
            from concourse.hw_specs import get_activation_tables
            for idx, (nm, funcs) in enumerate(get_activation_tables(e.bass.m.arch).items()):
                if nm == set_name:
                    return e.add_instruction(mybir.InstLoadActFuncSet(
                        name=e.bass.get_next_instruction_name(), act_func_set_id=idx,
                        ins=[], outs=[]))
            raise ValueError(set_name)
        self.q["scalar"].append(("op", fn))

    def replay(self, block, sems):
        def section(engname):
            def body(engobj):
                last = None
                for item in self.q[engname]:
                    k = item[0]
                    if k == "op":
                        last = item[1](engobj)
                    elif k == "inc":
                        assert last is not None, f"inc with no prior op on {engname}"
                        last.then_inc(sems[item[1]], item[2])
                    elif k == "wait":
                        engobj.wait_ge(sems[item[1]], item[2])
                    elif k == "drain":
                        engobj.drain()
            return body
        block.sync(section("sync"))
        block.tensor(section("tensor"))
        block.vector(section("vector"))
        block.scalar(section("scalar"))
        block.gpsimd(section("gpsimd"))



class StopEmission(Exception):
    pass


def build_program(cfg, stop=None, dump=None):
    B, S, T, NT, QG, NLAT, STI = (cfg["B"], cfg["S"], cfg["T"], cfg["NT"], cfg["QG"],
                                  cfg["NLAT"], cfg["STI"])
    NSEQT = S // 128
    consts = build_consts(cfg)

    nc = bass.Bass("TRN2", target_bir_lowering=False)

    xin = nc.dram_tensor("xin", [B, S, D], F32, kind="ExternalInput").ap()
    yout = nc.dram_tensor("yout", [B, S, D], F32, kind="ExternalOutput").ap()
    hv = {}
    for nm, shape in [("tq", [128, 8192]), ("tk", [128, 8192]), ("tv", [128, 8192]),
                      ("trq", [128, 2048]), ("trout", [128, 2048]), ("tow", [128, 8192]),
                      ("trkv", [128, 4096]), ("tgu", [128, 65536]), ("tdn", [128, 32768]),
                      ("bcol_qk", [128, 16]), ("bcol_rq", [128, 2]), ("bcol_rkv", [128, 4]),
                      ("bcol_rout", [128, 8]), ("brow_v", [1, 1024]), ("brow_out", [1, 1024]),
                      ("gl", [64, 1]), ("gr", [64, 1])]:
        hv[nm] = nc.dram_tensor(nm, shape, F32, kind="ExternalInput").ap()
    cin = {k: nc.inline_tensor(v, "c_" + k).ap() for k, v in consts.items()}
    NSEG_TILE = 5 + 8 + 4
    tape = nc.dram_tensor("tape", [128, NSEG_TILE * SEG_EL], BF16, kind="Internal").ap()

    es = contextlib.ExitStack()
    sb = {}

    def alloc(name, shape, dtype):
        sb[name] = nc.alloc_sbuf_tensor(name, shape, dtype)
        return sb[name]

    alloc("x", [128, 4096], F32)
    alloc("y1n", [128, 4096], F32)
    alloc("y2", [128, 4096], F32)
    alloc("xqT", [128, 8 * T], BF16)
    alloc("kcarry", [128, 1024], BF16)
    alloc("vcarry", [128, 1024], BF16)
    alloc("ring", [128, RING_SLOTS * SEG_EL], BF16)
    alloc("rowpack", [128, 2176], F32)
    alloc("ident", [128, 128], BF16)
    alloc("identf32", [128, 128], F32)
    alloc("binAB", [128, 512], BF16)
    alloc("latbin", [128, STI * NLAT], BF16)
    alloc("P4", [128, 4], BF16)
    alloc("selc", [16, 1024], BF16)
    alloc("glgr", [128, 256], BF16)
    alloc("gl2", [128, 2], F32)
    alloc("rkres", [128, B * 2 * LHD], BF16)
    alloc("rvres", [64, B * LH * LHD], BF16)
    alloc("bcols", [128, 64], F32)
    alloc("stats", [128, 64], F32)
    alloc("mcols", [128, 8], F32)
    alloc("sc1", [1, 16], F32)
    alloc("outst", [128, 1024], F32)
    alloc("onescol", [128, 1], F32)

    AR = {}
    em = Em()
    ar_base = [None]

    def ar(name, shape, dtype, off):
        t = nc.alloc_sbuf_tensor_at(name, shape, dtype, offset=ar_base[0] + off)
        sb[name] = t
        esz = {F32: 4, BF16: 2, I32: 4}[dtype]
        em.ranges[t.name] = (off, off + shape[1] * esz)
        return t


    T2 = 2 * T
    T8 = 8 * T
    ARENA_BYTES = 0x17A00
    ar_base[0] = (nc.sbuf_base + 63) // 64 * 64
    assert nc.sbuf_top - ar_base[0] >= ARENA_BYTES, (
        f"SBUF over budget: base {nc.sbuf_base} top {nc.sbuf_top}")

    ar("xbTM", [128, 4096], BF16, 0x0000)
    ar("xhT", [128, T8], BF16, 0x2000)
    ar("qT", [128, T8], BF16, 0x4000)
    ar("kT", [128, T8], BF16, 0x6000)
    ar("vT", [128, 4096], BF16, 0x8000)
    ar("rqT", [128, T2], BF16, 0xA000)
    ar("probsB", [128, 2048], BF16, 0xA800)
    ar("probsTB", [128, 2048], BF16, 0xB800)
    ar("localT", [128, T8], BF16, 0xC800)
    ar("latoT", [128, T2], BF16, 0xE800)
    ar("latpr", [128, 256], BF16, 0xF000)
    ar("lattnT", [64, 512], BF16, 0xF200)
    ar("remT", [128, T8], BF16, 0xF600)
    ar("mixT", [128, T8], BF16, 0x11600)
    ar("dtmp", [128, 512], F32, 0x13600)
    ar("gatesf", [128, 512], F32, 0x13E00)
    ar("gatesb", [128, 512], BF16, 0x14600)
    ar("aq1i", [128, 1024], I32, 0x10000)
    ar("xqTM", [128, 4096], BF16, 0x15A00)
    ar("inner", [128, 32 * T], F32, 0x0000)
    ar("sg", [128, 1024], F32, 0x10000)
    ar("q2i", [128, 512], I32, 0x11000)
    ar("q2ch", [128, 1024], BF16, 0x11800)
    ar("mpos", [128, 512], F32, 0x12000)
    ar("mneg", [128, 512], F32, 0x12800)
    ar("sxbc", [128, 512], F32, 0x13000)
    ar("sxrow", [1, 512], F32, 0x13800)
    ar("sqscr", [128, 1024], F32, 0x14000)
    ar("outst2", [128, 1024], F32, 0x16A00)
    ar("wslab", [128, 8192], F32, 0x0000)
    ar("oslab", [128, 8192], BF16, 0x8000)
    ar("islab", [128, 4096], I32, 0xC000)
    ar("xbP", [128, 4096], BF16, 0x10000)
    ar("pooledP", [128, 512], BF16, 0x12000)
    ar("acc", [128, 8], F32, 0x12400)
    ar("abscr", [128, 1024], F32, 0x12500)
    ar("rvtmp", [128, 128], BF16, 0x13600)
    ar("w1slab", [128, 4096], F32, 0x13800)

    ps_l = [nc.alloc_psum_tensor("ps%d" % i, [128, 512], F32) for i in range(8)]
    psb_l = [tt.bitcast(BF16) for tt in ps_l]

    def pf(i, lo=0, hi=512, p0=0, p1=128):
        return ps_l[i].ap()[p0:p1, lo:hi]

    def pb(i, lo=0, hi=1024, p0=0, p1=128):
        return psb_l[i].ap()[p0:p1, lo:hi]

    A = lambda n: sb[n].ap()
    stage_no = [0]

    def ck():
        stage_no[0] += 1
        if stop is not None and stage_no[0] >= stop:
            em.frozen = True

    ones_row = lambda w: A("rowpack")[0:1, 2048:2048 + w]

    ring_state = dict(next_load=0, loaded_off={}, done_at={}, load_sem={})

    def seg_load(idx):
        slot = idx % RING_SLOTS
        prev = idx - RING_SLOTS
        if prev >= 0:
            em.wait("sync", "ms_tensor", ring_state["done_at"][prev])
        for sem, cntv in ring_state["tape_w"].get(idx % NSEG_TILE, []):
            em.wait("sync", sem, cntv)
        w = 4096 if (idx % NSEG_TILE) == 3 else SEG_EL
        dst = A("ring")[:, slot * SEG_EL:slot * SEG_EL + w]
        src = tape[:, (idx % NSEG_TILE) * SEG_EL:(idx % NSEG_TILE) * SEG_EL + w]
        n = em.dma("sync", dst, src, "s_tape%d" % slot)
        ring_state["load_sem"][idx] = ("s_tape%d" % slot, n)
        ring_state["loaded_off"][idx] = slot * SEG_EL

    def seg_ap(idx, lo, hi):
        off = ring_state["loaded_off"][idx]
        return A("ring")[:, off + lo:off + hi]

    def seg_wait_pe(idx):
        sem, n = ring_state["load_sem"][idx]
        em.wait("tensor", sem, n)

    def seg_done(idx):
        em.mark("tensor")
        ring_state["done_at"][idx] = em.cnt("ms_tensor")
        nl = ring_state["next_load"]
        if nl == idx + RING_SLOTS:
            pass
        want = idx + RING_SLOTS
        if want < ring_state["total"]:
            seg_load(want)
            ring_state["next_load"] = want + 1

    em.dma("sync", A("ident"), cin["ident"], "s_c")
    em.dma("sync", A("identf32"), cin["identf32"], "s_c")
    em.dma("sync", A("binAB")[:, 0:256], cin["binA"], "s_c")
    em.dma("sync", A("binAB")[:, 256:512], cin["binB"], "s_c")
    em.dma("sync", A("latbin"), cin["latbin"], "s_c")
    em.dma("sync", A("P4"), cin["P4"], "s_c")
    em.dma("sync", A("selc")[0:16, 0:1024], cin["sel"], "s_c")
    em.dma("sync", A("bcols")[:, 0:16], hv["bcol_qk"], "s_c")
    em.dma("sync", A("bcols")[:, 16:18], hv["bcol_rq"], "s_c")
    em.dma("sync", A("bcols")[:, 18:22], hv["bcol_rkv"], "s_c")
    em.dma("sync", A("bcols")[:, 22:30], hv["bcol_rout"], "s_c")
    em.dma("sync", A("rowpack")[0:1, 0:1024], hv["brow_v"], "s_c")
    em.dma("sync", A("rowpack")[0:1, 1024:2048], hv["brow_out"], "s_c")
    em.dma("sync", A("gl2")[0:64, 0:1], hv["gl"], "s_c")
    em.dma("sync", A("gl2")[64:128, 0:1], hv["gl"], "s_c")
    em.dma("sync", A("gl2")[0:64, 1:2], hv["gr"], "s_c")
    em.dma("sync", A("gl2")[64:128, 1:2], hv["gr"], "s_c")
    n_c = em.cnt("s_c")

    ck()
    em.table("natural_log_exp_and_others")

    em.wait("vector", "s_c", n_c)
    em.memset("vector", A("rowpack")[0:1, 2048:2176], 1.0)
    em.memset("vector", A("onescol"), 1.0)
    em.memset("vector", A("stats")[:, 60:61], 1e-5)
    em.memset("vector", A("stats")[:, 62:63], 0.2)
    em.ts("vector", A("aq1i")[:, 0:1], A("stats")[:, 62:63], 1.0, 256.5, ALU.mult, ALU.add)
    em.ts("vector", A("stats")[:, 63:64], A("aq1i")[:, 0:1], -256.0, None, ALU.add)
    em.ts("vector", A("mcols")[:, 7:8], A("stats")[:, 63:64], -0.5, 256.5, ALU.mult, ALU.add)
    em.ts("vector", A("bcols")[:, 32:40], A("bcols")[:, 0:8], 0.125)
    em.ts("vector", A("bcols")[:, 40:42], A("bcols")[:, 16:18], 0.125)
    em.memset("vector", A("glgr"), 0.0)
    for cidx in range(8):
        em.copy("vector", A("glgr")[0:64, cidx * 16 + 2 * cidx:cidx * 16 + 2 * cidx + 1],
                A("gl2")[0:64, 0:1])
        em.copy("vector", A("glgr")[64:128, cidx * 16 + 2 * cidx + 1:cidx * 16 + 2 * cidx + 2],
                A("gl2")[64:128, 0:1])
        em.copy("vector", A("glgr")[0:64, 128 + cidx * 16 + 2 * cidx:128 + cidx * 16 + 2 * cidx + 1],
                A("gl2")[0:64, 1:2])
        em.copy("vector",
                A("glgr")[64:128, 128 + cidx * 16 + 2 * cidx + 1:128 + cidx * 16 + 2 * cidx + 2],
                A("gl2")[64:128, 1:2])
    em.mark("vector")

    ck()
    ring_state["tape_w"] = {}
    SEGQ, SEGK, SEGV, SEGRQRO, SEGOW = 0, 1, 2, 3, 4
    tape_steps = []
    for seg, nm, dsto, tot in [(SEGQ, "tq", 0, 8192), (SEGK, "tk", 0, 8192),
                               (SEGV, "tv", 0, 8192), (SEGRQRO, "trq", 0, 2048),
                               (SEGRQRO, "trout", 2048, 2048), (SEGOW, "tow", 0, 8192)]:
        for half in range(0, tot, 4096):
            tape_steps.append((seg, nm, dsto, half, min(4096, tot - half)))
    cast_m = {}
    tw_m = {}

    def tape_step(si):
        seg, nm, dsto, half, w = tape_steps[si]
        sl = si % 2
        wap = A("wslab")[:, sl * 4096:sl * 4096 + w]
        oap = A("oslab")[:, sl * 4096:sl * 4096 + w]
        if si >= 2:
            em.wait("sync", "ms_vector", cast_m[si - 2])
        nl = em.dma("sync", wap, hv[nm][:, half:half + w], "s_wp%d" % sl)
        em.wait("vector", "s_wp%d" % sl, nl)
        if si >= 2:
            em.wait("vector", "s_tw%d" % sl, tw_m[si - 2])
        em.copy("vector", oap, wap)
        em.mark("vector")
        cast_m[si] = em.cnt("ms_vector")
        em.wait("sync", "ms_vector", cast_m[si])
        em.dma("sync", tape[:, seg * SEG_EL + dsto + half:seg * SEG_EL + dsto + half + w],
               oap, "s_tw%d" % sl)
        tw_m[si] = em.cnt("s_tw%d" % sl)
        ring_state["tape_w"].setdefault(seg, []).append(("s_tw%d" % sl, tw_m[si]))

    em.memset("vector", A("acc")[:, 0:8], 0.0)
    em.memset("vector", A("rvtmp"), 0.0)
    em.mark("vector")
    p1_steps = []
    for nm, nsl in (("tgu", 32), ("tdn", 16)):
        for sidx in range(nsl):
            p1_steps.append((nm, sidx))
    abs_m = {}
    addm = {}

    def p1_step(si):
        nm, sidx = p1_steps[si]
        sl = si % 2
        wap = A("w1slab")[:, sl * 2048:(sl + 1) * 2048]
        if si >= 2:
            em.wait("gpsimd", "ms_scalar", abs_m[si - 2])
        nl = em.dma("gpsimd", wap, hv[nm][:, sidx * 2048:(sidx + 1) * 2048], "s_gp%d" % sl)
        em.wait("scalar", "s_gp%d" % sl, nl)
        if si >= 1:
            em.wait("scalar", "ms_vector", addm.get(si - 1, 0))
        for j in range(2):
            em.act(A("abscr"), wap[:, j * 1024:(j + 1) * 1024], AF.Abs,
                   accum=A("stats")[:, 56 + 2 * sl + j:57 + 2 * sl + j])
        em.mark("scalar")
        abs_m[si] = em.cnt("ms_scalar")
        em.wait("vector", "ms_scalar", abs_m[si])
        for j in range(2):
            if nm == "tgu":
                og = sidx * 2 + j
                colidx = 0 if og % 2 == 0 else 1
            else:
                colidx = 6
            em.tt("vector", A("acc")[:, colidx:colidx + 1], A("acc")[:, colidx:colidx + 1],
                  A("stats")[:, 56 + 2 * sl + j:57 + 2 * sl + j], ALU.add)
        em.mark("vector")
        addm[si] = em.cnt("ms_vector")

    for si in range(len(p1_steps)):
        if si < len(tape_steps):
            tape_step(si)
        p1_step(si)
    em.mark("vector")

    em.join("tensor", "vector")
    for i, colidx in enumerate((0, 1, 6)):
        em.mm(pf(5, i, i + 1, 0, 1), A("acc")[:, colidx:colidx + 1],
              A("onescol"), start=True, stop=True)
    em.mark("tensor")
    em.join("vector", "tensor")
    NW = 4096 * 1024
    em.ts("vector", A("sc1")[0:1, 0:3], pf(5, 0, 3, 0, 1), 1.0 / NW, 1e-5, ALU.mult, ALU.max)
    em.recip("vector", A("sc1")[0:1, 4:7], A("sc1")[0:1, 0:3])
    em.ts("vector", A("sc1")[0:1, 8:9], A("sc1")[0:1, 5:6], 1e-5)
    em.tt("vector", A("sc1")[0:1, 9:10], A("sc1")[0:1, 2:3], A("sc1")[0:1, 1:2], ALU.mult)
    em.ts("vector", A("sc1")[0:1, 9:10], A("sc1")[0:1, 9:10], 1.0 / 127.0)
    em.mark("vector")
    em.join("tensor", "vector")
    bc_list = [(0, 0), (1, 1), (2, 8), (3, 9), (4, 4), (5, 5), (6, 6)]
    for mcol, scol in bc_list:
        em.mm(pf(6, mcol, mcol + 1), A("rowpack")[0:1, 2048:2176],
              A("sc1")[0:1, scol:scol + 1], start=True, stop=True)
    em.mark("tensor")
    em.join("vector", "tensor")
    em.copy("vector", A("mcols")[:, 0:7], pf(6, 0, 7))
    em.mark("vector")

    GU0 = 5
    DN0 = 13
    q2_m = {}
    wq_m = {}

    def p2_step(si):
        nm, sidx = p1_steps[si]
        sl = si % 2
        wap = A("wslab")[:, sl * 2048:(sl + 1) * 2048]
        oap = A("oslab")[:, sl * 2048:(sl + 1) * 2048]
        if si >= 2:
            em.wait("gpsimd", "ms_vector", q2_m[si - 2])
        else:
            em.wait("gpsimd", "ms_vector", em.cnt("ms_vector"))
            em.wait("gpsimd", "s_gp%d" % sl, em.cnt("s_gp%d" % sl))
        nl = em.dma("gpsimd", wap, hv[nm][:, sidx * 2048:(sidx + 1) * 2048], "s_g2p%d" % sl)
        em.wait("vector", "s_g2p%d" % sl, nl)
        if si >= 2:
            em.wait("vector", "s_twg%d" % sl, wq_m[si - 2])
        else:
            em.wait("vector", "s_tw0", em.cnt("s_tw0"))
            em.wait("vector", "s_tw1", em.cnt("s_tw1"))
        for j in range(2):
            if nm == "tgu":
                og = sidx * 2 + j
                sw = A("mcols")[:, 4:5] if og % 2 == 0 else A("mcols")[:, 5:6]
            else:
                sw = A("mcols")[:, 6:7]
            blk = wap[:, j * 1024:(j + 1) * 1024]
            fb = sb["islab"].bitcast(F32).ap()[:, (sl * 2 + j) * 1024:(sl * 2 + j + 1) * 1024]
            em.ts("vector", blk, blk, sw, None, ALU.mult)
            em.ts("vector", fb, blk, 0.5, None, ALU.is_gt)
            em.ts("vector", blk, blk, -0.5, None, ALU.is_lt)
            em.tt("vector", oap[:, j * 1024:(j + 1) * 1024], fb, blk, ALU.subtract)
        em.mark("vector")
        q2_m[si] = em.cnt("ms_vector")
        em.wait("gpsimd", "ms_vector", q2_m[si])
        if nm == "tgu":
            seg = GU0 + sidx // 4
            off = (sidx % 4) * 2048
        else:
            seg = DN0 + sidx // 4
            off = (sidx % 4) * 2048
        em.dma("gpsimd", tape[:, seg * SEG_EL + off:seg * SEG_EL + off + 2048], oap,
               "s_twg%d" % sl)
        wq_m[si] = em.cnt("s_twg%d" % sl)
        ring_state["tape_w"].setdefault(seg, []).append(("s_twg%d" % sl, wq_m[si]))

    rkvb = sb["w1slab"].bitcast(BF16).ap()
    em.wait("gpsimd", "ms_scalar", em.cnt("ms_scalar"))
    nl = em.dma("gpsimd", rkvb[:, 0:4096], hv["trkv"], "s_gp0")
    rkv_ld = nl
    p2i = [0]

    def pump2(n=2):
        for _ in range(n):
            if p2i[0] < len(p1_steps):
                p2_step(p2i[0])
                p2i[0] += 1

    for b in range(B):
        ck()
        pump2()
        ngr = (S // 128 + 3) // 4
        for grp in range(ngr):
            t0 = grp * 4
            ntl = min(4, S // 128 - t0)
            em.wait("sync", "ms_gpsimd", em.cnt("ms_gpsimd"))
            nl = em.dma("sync", A("x").rearrange("p (g d) -> p g d", d=1024)[:, 0:ntl],
                        xin[b][t0 * 128:(t0 + ntl) * 128, :].rearrange("(g p) d -> p g d", p=128),
                        "s_xp")
            em.wait("gpsimd", "s_xp", nl)
            em.join("gpsimd", "tensor")
            em.copy("gpsimd", A("xbP")[:, 0:ntl * 1024], A("x")[:, 0:ntl * 1024])
            em.mark("gpsimd")
            em.join("tensor", "gpsimd")
            for tl in range(ntl):
                tg = t0 + tl
                for kk in range(8):
                    em.mm(pf(kk, tg * 4, tg * 4 + 4),
                          A("xbP")[:, tl * 1024 + kk * 128:tl * 1024 + (kk + 1) * 128],
                          A("P4"), start=True, stop=True)
            em.mark("tensor")
            pump2()
        ck()
        em.join("scalar", "tensor")
        for kk in range(8):
            em.act(A("pooledP")[:, kk * NLAT:(kk + 1) * NLAT], pf(kk, 0, NLAT), AF.Identity)
        em.mark("scalar")
        ck()
        em.join("tensor", "scalar")
        em.wait("tensor", "s_gp0", rkv_ld)
        for och in range(4):
            for kk in range(8):
                em.mm(pf(och, 0, NLAT),
                      rkvb[:, (och * 8 + kk) * 128:(och * 8 + kk + 1) * 128],
                      A("pooledP")[:, kk * NLAT:(kk + 1) * NLAT], start=(kk == 0), stop=(kk == 7))
        em.mark("tensor")
        ck()
        pump2()
        em.join("scalar", "tensor")
        for och in range(2):
            em.act(A("rkres")[:, (b * 2 + och) * 64:(b * 2 + och) * 64 + NLAT],
                   pf(och, 0, NLAT), AF.Identity, bias=A("bcols")[:, 18 + och:19 + och])
        for och in range(2, 4):
            em.act(A("rvtmp")[:, 0:NLAT] if och == 2 else A("rvtmp")[:, 64:64 + NLAT],
                   pf(och, 0, NLAT), AF.Identity, bias=A("bcols")[:, 18 + och:19 + och])
        em.mark("scalar")
        ck()
        em.join("tensor", "scalar")
        for half in range(2):
            em.trans(pb(4, half * 256, half * 256 + 128, 0, 64),
                     A("rvtmp")[:, half * 64:(half + 1) * 64], A("ident"))
        em.mark("tensor")
        ck()
        em.join("scalar", "tensor")
        for half in range(2):
            em.act(A("rvres")[0:NLAT, (b * 2 + half) * 128:(b * 2 + half + 1) * 128],
                   pb(4, half * 256, half * 256 + 128, 0, NLAT), AF.Identity)
        em.mark("scalar")
        ck()
        em.join("tensor", "scalar")
        pump2()
    pump2(len(p1_steps))
    ck()
    ring_state["total"] = NSEG_TILE * B * NT
    ring_state["done_at"] = {}
    seg_load(0)
    seg_load(1)
    ring_state["next_load"] = 2

    x_loaded = [False]
    prev_stats = [None]

    def out_rows(b, t, tg):
        return yout[b][t * T + tg * 128:t * T + (tg + 1) * 128, :]

    stc = {"nmax": 0, "rsum": 8, "rr": 16, "negmean": 24, "varp": 28, "rstd": 32,
           "rmax": 36, "sx": 40, "invsx": 44, "scale2": 48, "sx2": 52, "tmp": 56}

    def SC(name, i, w=1):
        return A("stats")[:, stc[name] + i:stc[name] + i + w]

    try:
      for b in range(B):
        for t in range(NT):
            tile_idx = b * NT + t
            seg0 = tile_idx * NSEG_TILE

            em.table("natural_log_exp_and_others")

            if not x_loaded[0]:
                em.wait("sync", "ms_gpsimd", em.cnt("ms_gpsimd"))
                nl = em.dma("sync", A("x").rearrange("p (g d) -> p g d", d=1024),
                            xin[b][t * T:(t + 1) * T, :].rearrange("(g p) d -> p g d", p=128),
                            "s_x")
            x_loaded[0] = False
            em.wait("gpsimd", "s_x", em.cnt("s_x"))
            if tile_idx == 0:
                for s in ("s_twg0", "s_twg1", "s_gp0", "s_gp1", "s_g2p0", "s_g2p1",
                          "s_wp0", "s_wp1", "s_xp"):
                    em.wait("gpsimd", s, em.cnt(s))
            cast_m = {}
            for c in range((QG + 3) // 4):
                lo, hi = 4 * c * 1024, min(4 * c + 4, QG) * 1024
                em.copy("gpsimd", A("xbTM")[:, lo:hi], A("x")[:, lo:hi])
                em.mark("gpsimd")
                cast_m[c] = em.cnt("ms_gpsimd")

            if t == 0:
                em.join("gpsimd", "tensor")
                em.memset("gpsimd", A("kcarry"), 0.0)
                em.memset("gpsimd", A("vcarry"), 0.0)
                em.mark("gpsimd")
                em.join("tensor", "gpsimd")

            ck()
            em.join("tensor", "vector")
            a2m = {}
            for tg in range(QG):
                bk = 6 + tg % 2
                em.wait("tensor", "ms_gpsimd", cast_m[tg // 4])
                if tg >= 2:
                    em.wait("tensor", "ms_scalar", a2m[tg - 2])
                for kk in range(8):
                    em.trans(pb(bk, kk * 128, (kk + 1) * 128),
                             A("xbTM")[:, tg * 1024 + kk * 128:tg * 1024 + (kk + 1) * 128],
                             A("ident"))
                em.mark("tensor")
                em.join("scalar", "tensor")
                dst = A("xhT").rearrange("p (k t) -> p k t", k=8)[:, :, tg * 128:(tg + 1) * 128]
                em.act(dst, pb(bk).rearrange("p (k c) -> p k c", k=8), AF.Identity)
                em.mark("scalar")
                a2m[tg] = em.cnt("ms_scalar")
            em.join("tensor", "scalar")

            ck()
            if prev_stats[0] is not None:
                pb_, pt_ = prev_stats[0]
                em.join("vector", "gpsimd")
                for tg in range(QG):
                    em.act(SC("rstd", tg), SC("varp", tg), AF.Ln)
                    em.act(SC("rstd", tg), SC("rstd", tg), AF.Exp, scale=-0.5)
                em.mark("scalar")
                em.join("vector", "scalar")
                outdma = {}
                for tg in range(QG):
                    ob = A("outst") if tg % 2 == 0 else A("outst2")
                    sem = "s_out%d" % (tg % 2)
                    if tg >= 2:
                        em.wait("vector", sem, outdma[tg - 2])
                    em.ts("vector", ob, A("y2")[:, tg * 1024:(tg + 1) * 1024],
                          SC("negmean", tg), SC("rstd", tg), ALU.add, ALU.mult)
                    em.mark("vector")
                    em.join("scalar", "vector")
                    if dump is None:
                        em.dma("scalar", out_rows(pb_, pt_, tg), ob, sem)
                    outdma[tg] = em.cnt(sem)

            seg_wait_pe(seg0 + SEGQ)
            wave_evac = {}

            def proj_wave(name, segi, ochs, banks, dst, dstoff, bias0, scale):
                prev = wave_evac.get(banks[0] // 4)
                if prev is not None:
                    em.wait("tensor", "ms_scalar", prev)
                for j, och in enumerate(ochs):
                    for kk in range(8):
                        em.mm(pf(banks[j]),
                              seg_ap(segi, (och * 8 + kk) * 128, (och * 8 + kk + 1) * 128),
                              A("xhT")[:, kk * T:(kk + 1) * T], start=(kk == 0), stop=(kk == 7))
                em.mark("tensor")
                mmk = em.cnt("ms_tensor")
                em.wait("scalar", "ms_tensor", mmk)
                for j, och in enumerate(ochs):
                    em.act(A(dst)[:, (dstoff + och) * T:(dstoff + och + 1) * T], pf(banks[j]),
                           AF.Identity, bias=A("bcols")[:, bias0 + och:bias0 + och + 1],
                           scale=scale)
                em.mark("scalar")
                wave_evac[banks[0] // 4] = em.cnt("ms_scalar")

            proj_wave("qT", seg0 + SEGQ, range(0, 4), [0, 1, 2, 3], "qT", 0, 32, 0.125)
            proj_wave("qT", seg0 + SEGQ, range(4, 8), [4, 5, 6, 7], "qT", 0, 32, 0.125)
            seg_wait_pe(seg0 + SEGK)
            proj_wave("kT", seg0 + SEGK, range(0, 4), [0, 1, 2, 3], "kT", 0, 8, 1.0)
            proj_wave("kT", seg0 + SEGK, range(4, 8), [4, 5, 6, 7], "kT", 0, 8, 1.0)
            seg_done(seg0 + SEGQ)
            seg_wait_pe(seg0 + SEGV)

            def v_wave(tgs, banks):
                prev = wave_evac.get(banks[0] // 4)
                if prev is not None:
                    em.wait("tensor", "ms_scalar", prev)
                for j, tg in enumerate(tgs):
                    for kk in range(8):
                        lh = A("xhT")[:, kk * T + tg * 128:kk * T + (tg + 1) * 128]
                        em.mm(pf(banks[2 * j]), lh,
                              seg_ap(seg0 + SEGV, kk * 1024, kk * 1024 + 512),
                              start=(kk == 0), stop=False)
                        em.mm(pf(banks[2 * j + 1]), lh,
                              seg_ap(seg0 + SEGV, kk * 1024 + 512, (kk + 1) * 1024),
                              start=(kk == 0), stop=False)
                    em.mm(pf(banks[2 * j]), ones_row(128), A("rowpack")[0:1, 0:512],
                          start=False, stop=True)
                    em.mm(pf(banks[2 * j + 1]), ones_row(128), A("rowpack")[0:1, 512:1024],
                          start=False, stop=True)
                em.mark("tensor")
                mmk = em.cnt("ms_tensor")
                em.wait("scalar", "ms_tensor", mmk)
                for j, tg in enumerate(tgs):
                    em.act(A("vT")[:, tg * 1024:tg * 1024 + 512],
                           pf(banks[2 * j]), AF.Identity)
                    em.act(A("vT")[:, tg * 1024 + 512:(tg + 1) * 1024],
                           pf(banks[2 * j + 1]), AF.Identity)
                em.mark("scalar")
                wave_evac[banks[0] // 4] = em.cnt("ms_scalar")

            v_wave([0, 1], [0, 1, 2, 3])
            v_wave([2, 3], [4, 5, 6, 7])
            seg_done(seg0 + SEGK)
            seg_wait_pe(seg0 + SEGRQRO)
            prev = wave_evac.get(0)
            if prev is not None:
                em.wait("tensor", "ms_scalar", prev)
            for och in range(2):
                for kk in range(8):
                    em.mm(pf(och), seg_ap(seg0 + SEGRQRO, (och * 8 + kk) * 128,
                                          (och * 8 + kk + 1) * 128),
                          A("xhT")[:, kk * T:(kk + 1) * T], start=(kk == 0), stop=(kk == 7))
            em.mark("tensor")
            em.join("scalar", "tensor")
            for och in range(2):
                em.act(A("rqT")[:, och * T:(och + 1) * T], pf(och), AF.Identity,
                       bias=A("bcols")[:, 40 + och:41 + och], scale=0.125)
            em.mark("scalar")
            em.join("tensor", "scalar")
            seg_done(seg0 + SEGV)

            ck()
            ev_marks = {}

            def l_scores(g, m):
                par = m % 2
                key = ("exp", g * 8 + m - 2)
                if key in ev_marks:
                    em.wait("tensor", "ms_scalar", ev_marks[key])
                for i, h in enumerate((m, m + 8)):
                    och = h // 2
                    rlo = (h % 2) * 64
                    qsl = A("qT")[rlo:rlo + 64, och * T + g * 128:och * T + (g + 1) * 128]
                    dst = pf(par, i * 256, (i + 1) * 256)
                    if g == 0:
                        em.mm(pf(par, i * 256, i * 256 + 128), qsl,
                              A("kcarry")[rlo:rlo + 64, och * 128:(och + 1) * 128],
                              start=True, stop=True)
                        em.mm(pf(par, i * 256 + 128, i * 256 + 256), qsl,
                              A("kT")[rlo:rlo + 64, och * T:och * T + 128],
                              start=True, stop=True)
                    else:
                        em.mm(dst, qsl,
                              A("kT")[rlo:rlo + 64, och * T + (g - 1) * 128:och * T + (g + 1) * 128],
                              start=True, stop=True)
                em.mark("tensor")
                ev_marks[("sc", g * 8 + m)] = em.cnt("ms_tensor")

            def l_chain1(g, m):
                par = m % 2
                k = g * 8 + m
                first_g = (t == 0 and g == 0)
                pslot = (k % 4) * 512
                em.wait("vector", "ms_tensor", ev_marks[("sc", k)])
                em.red("vector", SC("nmax", 2 * par, 2),
                       ps_l[par].ap().rearrange("p (bk c) -> p bk c", c=256),
                       "max", neg=True)
                em.mark("vector")
                em.join("scalar", "vector")
                for i in range(2):
                    em.act(A("probsB")[:, pslot + i * 256:pslot + (i + 1) * 256],
                           pf(par, i * 256, (i + 1) * 256), AF.Exp,
                           bias=SC("nmax", i + 2 * par))
                em.mark("scalar")
                ev_marks[("exp", k)] = em.cnt("ms_scalar")
                em.join("gpsimd", "scalar")
                mofs = 0 if first_g else 256
                for i in range(2):
                    pr = A("probsB")[:, pslot + i * 256:pslot + (i + 1) * 256]
                    em.tt("gpsimd", pr, pr, A("binAB")[:, mofs:mofs + 256], ALU.mult)
                em.mark("gpsimd")
                ev_marks[("mask", k)] = em.cnt("ms_gpsimd")

            def l_chain2(g, m):
                par = m % 2
                k = g * 8 + m
                first_g = (t == 0 and g == 0)
                pslot = (k % 4) * 512
                em.wait("vector", "ms_gpsimd", ev_marks[("mask", k)])
                em.red("vector", SC("rsum", 2 * par, 2),
                       A("probsB").rearrange("p (bk c) -> p bk c", c=256)[:, 2 * (k % 4):2 * (k % 4) + 2],
                       "sum")
                if first_g:
                    em.ts("vector", SC("rsum", 2 * par, 2), SC("rsum", 2 * par, 2), 1e-30,
                          None, ALU.max)
                em.recip("vector", SC("rr", 2 * par, 2), SC("rsum", 2 * par, 2))
                for i in range(2):
                    pr = A("probsB")[:, pslot + i * 256:pslot + (i + 1) * 256]
                    em.ts("vector", pr, pr, SC("rr", i + 2 * par))
                em.mark("vector")
                ev_marks[("norm", k)] = em.cnt("ms_vector")

            def l_trans(g, m):
                par = m % 2
                k = g * 8 + m
                pslot = (k % 4) * 512
                em.wait("tensor", "ms_vector", ev_marks[("norm", k)])
                key = ("ptev", k - 1)
                if key in ev_marks:
                    em.wait("tensor", ev_marks[key][0], ev_marks[key][1])
                for i in range(2):
                    pr0 = A("probsB")[:, pslot + i * 256:pslot + i * 256 + 128]
                    pr1 = A("probsB")[:, pslot + i * 256 + 128:pslot + (i + 1) * 256]
                    em.trans(pb(2, par * 512 + i * 256, par * 512 + i * 256 + 128), pr0,
                             A("ident"))
                    em.trans(pb(2, par * 512 + i * 256 + 128, par * 512 + (i + 1) * 256), pr1,
                             A("ident"))
                em.mark("tensor")
                eng = "scalar" if k % 2 == 0 else "vector"
                em.join(eng, "tensor")
                if eng == "scalar":
                    em.act(A("probsTB")[:, pslot:pslot + 512],
                           pb(2, par * 512, par * 512 + 512), AF.Identity)
                else:
                    em.copy("vector", A("probsTB")[:, pslot:pslot + 512],
                            pb(2, par * 512, par * 512 + 512))
                em.mark(eng)
                ev_marks[("ptev", k)] = ("ms_" + eng, em.cnt("ms_" + eng))

            def l_pv(g, m):
                par = m % 2
                k = g * 8 + m
                pslot = (k % 4) * 512
                em.wait("tensor", ev_marks[("ptev", k)][0], ev_marks[("ptev", k)][1])
                key = ("lev", k - 1)
                if key in ev_marks:
                    em.wait("tensor", ev_marks[key][0], ev_marks[key][1])
                for i, h in enumerate((m, m + 8)):
                    vcol = h * 64
                    if g == 0:
                        vprev = A("vcarry")[:, vcol:vcol + 64]
                    else:
                        vprev = A("vT")[:, (g - 1) * 1024 + vcol:(g - 1) * 1024 + vcol + 64]
                    vcur = A("vT")[:, g * 1024 + vcol:g * 1024 + vcol + 64]
                    dst = pf(3, par * 256 + i * 128, par * 256 + (i + 1) * 128, 0, 64)
                    em.mm(dst, vprev, A("probsTB")[:, pslot + i * 256:pslot + i * 256 + 128],
                          start=True, stop=False)
                    em.mm(dst, vcur, A("probsTB")[:, pslot + i * 256 + 128:pslot + (i + 1) * 256],
                          start=False, stop=True)
                em.mark("tensor")
                pvm = em.cnt("ms_tensor")
                em.wait("scalar", "ms_tensor", pvm)
                for i, h in enumerate((m, m + 8)):
                    och = h // 2
                    rlo = (h % 2) * 64
                    em.act(A("localT")[rlo:rlo + 64, och * T + g * 128:och * T + (g + 1) * 128],
                           pf(3, par * 256 + i * 128, par * 256 + (i + 1) * 128, 0, 64),
                           AF.Identity)
                em.mark("scalar")
                ev_marks[("lev", k)] = ("ms_scalar", em.cnt("ms_scalar"))

            def lat_scores(tg):
                key = ("lexp", tg - 2)
                if key in ev_marks:
                    em.wait("tensor", "ms_scalar", ev_marks[key])
                for lhi in range(LH):
                    och = lhi // 2
                    rlo = (lhi % 2) * 64
                    em.mm(pf(4 + lhi % 2, (lhi // 2) * 64, (lhi // 2) * 64 + NLAT),
                          A("rqT")[rlo:rlo + 64, och * T + tg * 128:och * T + (tg + 1) * 128],
                          A("rkres")[rlo:rlo + 64, (b * 2 + och) * 64:(b * 2 + och) * 64 + NLAT],
                          start=True, stop=True)
                em.mark("tensor")
                ev_marks[("lsc", tg)] = em.cnt("ms_tensor")

            def lat_chain(tg):
                sti = t * QG + tg
                em.wait("vector", "ms_tensor", ev_marks[("lsc", tg)])
                for lhi in range(LH):
                    em.red("vector", SC("nmax", 4 + lhi),
                           pf(4 + lhi % 2, (lhi // 2) * 64, (lhi // 2) * 64 + NLAT),
                           "max", neg=True)
                em.mark("vector")
                em.join("scalar", "vector")
                for lhi in range(LH):
                    em.act(A("latpr")[:, lhi * 64:lhi * 64 + NLAT],
                           pf(4 + lhi % 2, (lhi // 2) * 64, (lhi // 2) * 64 + NLAT),
                           AF.Exp, bias=SC("nmax", 4 + lhi))
                em.mark("scalar")
                ev_marks[("lexp", tg)] = em.cnt("ms_scalar")
                em.join("gpsimd", "scalar")
                for lhi in range(LH):
                    pr = A("latpr")[:, lhi * 64:lhi * 64 + NLAT]
                    em.tt("gpsimd", pr, pr, A("latbin")[:, sti * NLAT:(sti + 1) * NLAT], ALU.mult)
                em.mark("gpsimd")
                em.join("vector", "gpsimd")
                for lhi in range(LH):
                    em.red("vector", SC("rsum", 4 + lhi), A("latpr")[:, lhi * 64:lhi * 64 + NLAT],
                           "sum")
                if tg == 0 and t == 0:
                    em.ts("vector", SC("rsum", 4, 4), SC("rsum", 4, 4), 1e-30, None, ALU.max)
                em.recip("vector", SC("rr", 4, 4), SC("rsum", 4, 4))
                for lhi in range(LH):
                    pr = A("latpr")[:, lhi * 64:lhi * 64 + NLAT]
                    em.ts("vector", pr, pr, SC("rr", 4 + lhi))
                em.mark("vector")
                ev_marks[("lnorm", tg)] = em.cnt("ms_vector")

            def lat_pv(tg):
                em.wait("tensor", "ms_vector", ev_marks[("lnorm", tg)])
                key = ("llev", tg - 1)
                if key in ev_marks:
                    em.wait("tensor", "ms_scalar", ev_marks[key])
                for lhi in range(LH):
                    em.trans(pb(6, lhi * 128, lhi * 128 + 128, 0, NLAT),
                             A("latpr")[:, lhi * 64:lhi * 64 + NLAT], A("ident"))
                em.mark("tensor")
                em.join("scalar", "tensor")
                for lhi in range(LH):
                    em.act(A("lattnT")[0:NLAT, lhi * 128:lhi * 128 + 128],
                           pb(6, lhi * 128, lhi * 128 + 128, 0, NLAT), AF.Identity)
                em.mark("scalar")
                em.join("tensor", "scalar")
                for lhi in range(LH):
                    rvoff = (b * 2 + lhi // 2) * 128 + (lhi % 2) * 64
                    em.mm(pf(7, lhi * 128, (lhi + 1) * 128, 0, 64),
                          A("rvres")[0:NLAT, rvoff:rvoff + 64],
                          A("lattnT")[0:NLAT, lhi * 128:(lhi + 1) * 128], start=True, stop=True)
                em.mark("tensor")
                lpv = em.cnt("ms_tensor")
                em.wait("scalar", "ms_tensor", lpv)
                for lhi in range(LH):
                    och = lhi // 2
                    rlo = (lhi % 2) * 64
                    em.act(A("latoT")[rlo:rlo + 64, och * T + tg * 128:och * T + (tg + 1) * 128],
                           pf(7, lhi * 128, (lhi + 1) * 128, 0, 64), AF.Identity)
                em.mark("scalar")
                ev_marks[("llev", tg)] = em.cnt("ms_scalar")

            NB = QG * 8

            def batch(k):
                return (k // 8, k % 8)

            for i in range(NB + 4):
                if i < NB:
                    l_scores(*batch(i))
                if 0 <= i - 1 < NB:
                    l_chain1(*batch(i - 1))
                if 0 <= i - 2 < NB:
                    l_chain2(*batch(i - 2))
                if 0 <= i - 3 < NB:
                    l_trans(*batch(i - 3))
                if 0 <= i - 4 < NB:
                    l_pv(*batch(i - 4))
                if i % 8 == 4 and i // 8 < QG:
                    lat_scores(i // 8)
                if i % 8 == 6 and i // 8 < QG:
                    lat_chain(i // 8)
                if i % 8 == 0 and 0 < i // 8 <= QG:
                    lat_pv(i // 8 - 1)

            em.join("gpsimd", "tensor")
            em.copy("gpsimd",
                    A("kcarry").rearrange("p (k c) -> p k c", k=8),
                    A("kT").rearrange("p (k t) -> p k t", k=8)[:, :, T - 128:T])
            em.copy("gpsimd", A("vcarry"), A("vT")[:, (QG - 1) * 1024:QG * 1024])
            em.mark("gpsimd")
            ck()
            em.join("tensor", "scalar")
            em.join("tensor", "vector")
            em.join("tensor", "gpsimd")
            seg_wait_pe(seg0 + SEGRQRO)
            for grp in range(2):
                for oc in range(grp * 4, grp * 4 + 4):
                    for L in range(2):
                        em.mm(pf(oc % 4),
                              seg_ap(seg0 + SEGRQRO, 2048 + (oc * 2 + L) * 128,
                                     2048 + (oc * 2 + L + 1) * 128),
                              A("latoT")[:, L * T:(L + 1) * T], start=(L == 0), stop=(L == 1))
                em.mark("tensor")
                em.join("scalar", "tensor")
                for oc in range(grp * 4, grp * 4 + 4):
                    em.act(A("remT")[:, oc * T:(oc + 1) * T], pf(oc % 4), AF.Identity,
                           bias=A("bcols")[:, 22 + oc:23 + oc])
                em.mark("scalar")
                em.join("tensor", "scalar")
            seg_done(seg0 + SEGRQRO)
            for cidx in range(8):
                em.mm(pf(4, 0, 512, 0, 16), A("glgr")[:, cidx * 16:(cidx + 1) * 16],
                      A("localT")[:, cidx * T:(cidx + 1) * T], start=(cidx == 0), stop=False)
            for cidx in range(8):
                em.mm(pf(4, 0, 512, 0, 16), A("glgr")[:, 128 + cidx * 16:128 + (cidx + 1) * 16],
                      A("remT")[:, cidx * T:(cidx + 1) * T], start=False, stop=(cidx == 7))
            em.mark("tensor")
            em.join("scalar", "tensor")
            em.act(A("gatesf")[0:16, 0:512], pf(4, 0, 512, 0, 16), AF.Exp, scale=-1.0)
            em.mark("scalar")
            em.join("vector", "scalar")
            em.ts("vector", A("gatesf")[0:16, 0:512], A("gatesf")[0:16, 0:512], 1.0, None, ALU.add)
            em.recip("vector", A("gatesf")[0:16, 0:512], A("gatesf")[0:16, 0:512])
            em.copy("vector", A("gatesb")[0:16, 0:512], A("gatesf")[0:16, 0:512])
            em.mark("vector")
            gate_mark = em.cnt("ms_vector")
            em.wait("tensor", "ms_vector", gate_mark)
            for cidx in range(8):
                gb = 5 + cidx % 2
                if cidx >= 2:
                    em.wait("tensor", "ms_vector", ev_marks.get(("mix", cidx - 2), 0))
                em.mm(pf(gb), A("selc")[0:16, cidx * 128:(cidx + 1) * 128],
                      A("gatesb")[0:16, 0:512], start=True, stop=True)
                em.mark("tensor")
                em.join("vector", "tensor")
                lc = A("localT")[:, cidx * T:(cidx + 1) * T]
                rc = A("remT")[:, cidx * T:(cidx + 1) * T]
                em.tt("vector", A("dtmp"), lc, rc, ALU.subtract)
                em.tt("vector", A("dtmp"), A("dtmp"), pf(gb), ALU.mult)
                em.tt("vector", A("mixT")[:, cidx * T:(cidx + 1) * T], A("dtmp"), rc, ALU.add)
                em.mark("vector")
                ev_marks[("mix", cidx)] = em.cnt("ms_vector")

            ck()
            seg_wait_pe(seg0 + SEGOW)
            em.join("tensor", "vector")
            em.wait("scalar", "s_out0", em.cnt("s_out0"))
            em.wait("scalar", "s_out1", em.cnt("s_out1"))
            mm_marks = {}
            for tg in range(QG):
                for cidx in range(8):
                    lh = A("mixT")[:, cidx * T + tg * 128:cidx * T + (tg + 1) * 128]
                    em.mm(pf(2 * tg), lh, seg_ap(seg0 + SEGOW, cidx * 1024, cidx * 1024 + 512),
                          start=(cidx == 0), stop=False)
                    em.mm(pf(2 * tg + 1), lh,
                          seg_ap(seg0 + SEGOW, cidx * 1024 + 512, (cidx + 1) * 1024),
                          start=(cidx == 0), stop=False)
                em.mm(pf(2 * tg), ones_row(128), A("rowpack")[0:1, 1024:1536],
                      start=False, stop=True)
                em.mm(pf(2 * tg + 1), ones_row(128), A("rowpack")[0:1, 1536:2048],
                      start=False, stop=True)
                em.mark("tensor")
                mm_marks[tg] = em.cnt("ms_tensor")
            seg_done(seg0 + SEGOW)
            for tg in range(QG):
                y1 = A("y1n")[:, tg * 1024:(tg + 1) * 1024]
                em.wait("vector", "ms_tensor", mm_marks[tg])
                em.tt("vector", y1[:, 0:512], pf(2 * tg), A("x")[:, tg * 1024:tg * 1024 + 512],
                      ALU.add)
                em.tt("vector", y1[:, 512:1024], pf(2 * tg + 1),
                      A("x")[:, tg * 1024 + 512:(tg + 1) * 1024], ALU.add)
                em.red("vector", SC("negmean", tg), y1, "sum", neg=True)
                em.ts("vector", SC("negmean", tg), SC("negmean", tg), 1.0 / 1024.0)
                em.mark("vector")
                mm_marks[("r", tg)] = em.cnt("ms_vector")
            for tg in range(QG):
                y1 = A("y1n")[:, tg * 1024:(tg + 1) * 1024]
                em.wait("scalar", "ms_vector", mm_marks[("r", tg)])
                em.act(A("outst"), y1, AF.Square, bias=SC("negmean", tg),
                       accum=SC("varp", tg))
                em.act(SC("rstd", tg), SC("varp", tg), AF.Ln, bias=A("stats")[:, 60:61], scale=1.0 / 1024.0)
                em.act(SC("rstd", tg), SC("rstd", tg), AF.Exp, scale=-0.5)
                em.mark("scalar")
                mm_marks[("s", tg)] = em.cnt("ms_scalar")
            for tg in range(QG):
                y1 = A("y1n")[:, tg * 1024:(tg + 1) * 1024]
                em.wait("vector", "ms_scalar", mm_marks[("s", tg)])
                em.ts("vector", y1, y1, SC("negmean", tg), SC("rstd", tg), ALU.add, ALU.mult)
                em.mark("vector")
            nb, ntile = (b, t + 1) if t + 1 < NT else (b + 1, 0)
            if nb < B:
                em.wait("sync", "ms_gpsimd", em.cnt("ms_gpsimd"))
                em.dma("sync", A("x").rearrange("p (g d) -> p g d", d=1024),
                       xin[nb][ntile * T:(ntile + 1) * T, :].rearrange("(g p) d -> p g d", p=128),
                       "s_x")
                x_loaded[0] = True

            ck()
            em.wait("vector", "s_out0", em.cnt("s_out0"))
            em.wait("vector", "s_out1", em.cnt("s_out1"))
            aq_marks = {}
            for tg in range(QG):
                y1 = A("y1n")[:, tg * 1024:(tg + 1) * 1024]
                em.red("vector", SC("rmax", tg), y1, "max", absval=True)
                em.ts("vector", SC("rmax", tg), SC("rmax", tg), 1e-5, None, ALU.max)
                em.recip("vector", SC("sx", tg), SC("rmax", tg))
                em.ts("vector", SC("sx", tg), SC("sx", tg), 127.0)
                em.ts("vector", SC("invsx", tg), SC("rmax", tg), 1.0 / 127.0)
                em.ts("vector", A("outst"), y1, SC("sx", tg), None, ALU.mult)
                em.ts("vector", A("aq1i"), A("outst"), A("mcols")[:, 7:8], None, ALU.add)
                em.ts("vector", A("xqTM")[:, tg * 1024:(tg + 1) * 1024], A("aq1i"),
                      -256.0, SC("invsx", tg), ALU.add, ALU.mult)
                em.mark("vector")
                aq_marks[tg] = em.cnt("ms_vector")
            for tg in range(QG):
                bk = 6 + tg % 2
                em.wait("tensor", "ms_vector", aq_marks[tg])
                if tg >= 2:
                    em.wait("tensor", "ms_scalar", aq_marks[("e", tg - 2)])
                for kk in range(8):
                    em.trans(pb(bk, kk * 128, (kk + 1) * 128),
                             A("xqTM")[:, tg * 1024 + kk * 128:tg * 1024 + (kk + 1) * 128],
                             A("ident"))
                em.mark("tensor")
                em.join("scalar", "tensor")
                dst = A("xqT").rearrange("p (k t) -> p k t", k=8)[:, :, tg * 128:(tg + 1) * 128]
                em.act(dst, pb(bk).rearrange("p (k c) -> p k c", k=8), AF.Identity)
                em.mark("scalar")
                aq_marks[("e", tg)] = em.cnt("ms_scalar")
            em.join("tensor", "scalar")
            em.table("silu_and_others")
            if tile_idx == 0:
                for s in ("s_twg0", "s_twg1"):
                    em.wait("gpsimd", s, em.cnt(s))
            for e in ("tensor", "vector", "scalar", "gpsimd"):
                for f in ("tensor", "vector", "scalar", "gpsimd"):
                    if e != f:
                        em.join(e, f)

            bank_free = {}
            sg_free = {}
            for og in range(64):
                segi = seg0 + GU0 + og // 8
                if og % 8 == 0:
                    seg_wait_pe(segi)
                bank = og % 8
                if og >= 8:
                    em.wait("tensor", "ms_vector" if (og - 8) % 2 == 1 else "ms_scalar",
                            bank_free[og - 8])
                for kk in range(8):
                    em.mm(pf(bank),
                          seg_ap(segi, ((og % 8) * 8 + kk) * 128, ((og % 8) * 8 + kk + 1) * 128),
                          A("xqT")[:, kk * T:(kk + 1) * T], start=(kk == 0), stop=(kk == 7))
                em.mark("tensor")
                if og % 8 == 7:
                    seg_done(segi)
                cpair = og // 2
                if og % 2 == 0:
                    em.join("scalar", "tensor")
                    if cpair >= 2:
                        em.wait("scalar", "ms_vector", sg_free[cpair - 2])
                    em.act(A("sg")[:, (cpair % 2) * 512:(cpair % 2 + 1) * 512], pf(bank),
                           AF.Silu, scale=A("mcols")[:, 0:1])
                    em.mark("scalar")
                    bank_free[og] = em.cnt("ms_scalar")
                else:
                    em.join("vector", "scalar")
                    em.join("vector", "tensor")
                    em.tt("vector", A("inner")[:, cpair * T:(cpair + 1) * T],
                          A("sg")[:, (cpair % 2) * 512:(cpair % 2 + 1) * 512], pf(bank), ALU.mult)
                    em.mark("vector")
                    bank_free[og] = em.cnt("ms_vector")
                    sg_free[cpair] = em.cnt("ms_vector")

            ck()
            inner3 = A("inner").rearrange("p (c t) -> p t c", c=32)
            em.red("vector", A("mpos"), inner3, "max", absval=True)
            em.mark("vector")
            em.join("tensor", "vector")
            em.join("tensor", "scalar")
            for tg in range(QG):
                em.trans(pf(0, tg * 128, (tg + 1) * 128),
                         A("mpos")[:, tg * 128:(tg + 1) * 128], A("identf32"))
            em.mark("tensor")
            em.join("vector", "tensor")
            for tg in range(QG):
                em.red("vector", SC("rmax", tg), pf(0, tg * 128, (tg + 1) * 128), "max")
            for tg in range(QG):
                em.ts("vector", SC("rmax", tg), SC("rmax", tg), A("mcols")[:, 2:3], None, ALU.max)
                em.tt("vector", SC("scale2", tg), SC("rmax", tg), A("mcols")[:, 3:4], ALU.mult)
                em.recip("vector", SC("sx2", tg), SC("rmax", tg))
                em.ts("vector", SC("sx2", tg), SC("sx2", tg), 127.0)
            for tg in range(QG):
                em.copy("vector", SC("tmp", tg), SC("sx2", tg))
            em.mark("vector")
            em.join("tensor", "vector")
            for tg in range(QG):
                em.trans(pf(1, tg * 128, (tg + 1) * 128, 0, 1), SC("tmp", tg), A("identf32"))
            em.mark("tensor")
            em.join("vector", "tensor")
            em.copy("vector", A("sxrow")[0:1, 0:512], pf(1, 0, 512, 0, 1))
            em.mark("vector")
            em.join("tensor", "vector")
            em.mm(pf(2), A("rowpack")[0:1, 2048:2176], A("sxrow")[0:1, 0:512],
                  start=True, stop=True)
            em.mark("tensor")
            em.join("vector", "tensor")
            em.copy("vector", A("sxbc"), pf(2))
            em.mark("vector")

            ck()
            em.join("gpsimd", "vector")
            q2_marks = {}
            for kk in range(32):
                segi = seg0 + DN0 + kk // 8
                if kk % 8 == 0:
                    seg_wait_pe(segi)
                sl = kk % 2
                em.join("gpsimd", "vector")
                if kk >= 2:
                    em.wait("gpsimd", "ms_tensor", q2_marks[kk - 2])
                em.tt("gpsimd", A("sg")[:, sl * 512:(sl + 1) * 512],
                      A("inner")[:, kk * T:(kk + 1) * T], A("sxbc"), ALU.mult)
                em.mark("gpsimd")
                em.join("vector", "gpsimd")
                if kk >= 2:
                    em.wait("vector", "ms_tensor", q2_marks[kk - 2])
                em.ts("vector", A("q2i"), A("sg")[:, sl * 512:(sl + 1) * 512],
                      A("mcols")[:, 7:8], None, ALU.add)
                em.ts("vector", A("q2ch")[:, sl * 512:(sl + 1) * 512], A("q2i"), -256.0, None,
                      ALU.add)
                em.mark("vector")
                em.join("tensor", "vector")
                for tg in range(QG):
                    lh = A("q2ch")[:, sl * 512 + tg * 128:sl * 512 + (tg + 1) * 128]
                    em.mm(pf(2 * tg), lh, seg_ap(segi, (kk % 8) * 1024, (kk % 8) * 1024 + 512),
                          start=(kk == 0), stop=(kk == 31))
                    em.mm(pf(2 * tg + 1), lh,
                          seg_ap(segi, (kk % 8) * 1024 + 512, (kk % 8 + 1) * 1024),
                          start=(kk == 0), stop=(kk == 31))
                em.mark("tensor")
                q2_marks[kk] = em.cnt("ms_tensor")
                if kk % 8 == 7:
                    seg_done(segi)

            ck()
            em.join("vector", "tensor")
            l2m = {}
            for tg in range(QG):
                y2 = A("y2")[:, tg * 1024:(tg + 1) * 1024]
                em.stt("vector", y2[:, 0:512], pf(2 * tg), SC("scale2", tg),
                       A("y1n")[:, tg * 1024:tg * 1024 + 512], ALU.mult, ALU.add)
                em.stt("vector", y2[:, 512:1024], pf(2 * tg + 1), SC("scale2", tg),
                       A("y1n")[:, tg * 1024 + 512:(tg + 1) * 1024], ALU.mult, ALU.add)
                em.red("vector", SC("negmean", tg), y2, "sum", neg=True)
                em.ts("vector", SC("negmean", tg), SC("negmean", tg), 1.0 / 1024.0)
                em.mark("vector")
                l2m[tg] = em.cnt("ms_vector")
            em.wait("scalar", "s_out1", em.cnt("s_out1"))
            for tg in range(QG):
                y2 = A("y2")[:, tg * 1024:(tg + 1) * 1024]
                em.wait("scalar", "ms_vector", l2m[tg])
                em.act(A("sqscr"), y2, AF.Square, bias=SC("negmean", tg), accum=SC("varp", tg))
                em.mark("scalar")
                l2m[("s", tg)] = em.cnt("ms_scalar")
            for tg in range(QG):
                em.wait("vector", "ms_scalar", l2m[("s", tg)])
                em.ts("vector", SC("varp", tg), SC("varp", tg), 1.0 / 1024.0, 1e-5,
                      ALU.mult, ALU.add)
            em.mark("vector")
            if dump == "z":
                for tg in range(QG):
                    em.join("scalar", "vector")
                    em.dma("scalar", out_rows(b, t, tg),
                           A("y2")[:, tg * 1024:(tg + 1) * 1024], "s_out0")
                em.mark("scalar")
                em.join("vector", "scalar")
            em.join("scalar", "vector")
            prev_stats[0] = (b, t)

    except StopEmission:
        pass
    if stop is not None:
        em.wait("sync", "s_out0", em.cnt("s_out0"))
        em.wait("sync", "s_out1", em.cnt("s_out1"))
        with contextlib.ExitStack() as bes:
            sems = {}
            for name in em.counts:
                sems[name] = bes.enter_context(nc.semaphore(name))
            block = bes.enter_context(nc.Block())
            em.replay(block, sems)
        return nc
    em.table("natural_log_exp_and_others")
    pb_, pt_ = prev_stats[0]
    em.join("vector", "gpsimd")
    for tg in range(QG):
        em.act(SC("rstd", tg), SC("varp", tg), AF.Ln)
        em.act(SC("rstd", tg), SC("rstd", tg), AF.Exp, scale=-0.5)
    em.mark("scalar")
    em.join("vector", "scalar")
    outdma = {}
    for tg in range(QG):
        ob = A("outst") if tg % 2 == 0 else A("outst2")
        sem = "s_out%d" % (tg % 2)
        if tg >= 2:
            em.wait("vector", sem, outdma[tg - 2])
        em.ts("vector", ob, A("y2")[:, tg * 1024:(tg + 1) * 1024],
              SC("negmean", tg), SC("rstd", tg), ALU.add, ALU.mult)
        em.mark("vector")
        em.join("scalar", "vector")
        if dump is None:
            em.dma("scalar", out_rows(pb_, pt_, tg), ob, sem)
        outdma[tg] = em.cnt(sem)

    em.wait("sync", "s_out0", em.cnt("s_out0"))
    em.wait("sync", "s_out1", em.cnt("s_out1"))

    with contextlib.ExitStack() as bes:
        sems = {}
        for name in em.counts:
            sems[name] = bes.enter_context(nc.semaphore(name))
        block = bes.enter_context(nc.Block())
        em.replay(block, sems)

    return nc



def kernel(**inputs):
    cfg = make_cfg()
    n_cores = 8
    nc = build_program(cfg)
    in_maps = [prep_core_inputs(ci, n_cores, cfg, inputs) for ci in range(n_cores)]
    res = run_bass_kernel_spmd(nc, in_maps, core_ids=list(range(n_cores)))
    out = np.concatenate([np.asarray(r["yout"]) for r in res.results], axis=0)
    return out.astype(np.float32)


# version 12
# speedup vs baseline: 1.0163x; 1.0163x over previous
import contextlib
import numpy as np
import ml_dtypes
import concourse.bass as bass
import concourse.mybir as mybir
from concourse.bass_utils import run_bass_kernel_spmd

F32 = mybir.dt.float32
BF16 = mybir.dt.bfloat16
I32 = mybir.dt.int32
AF = mybir.ActivationFunctionType
ALU = mybir.AluOpType
AX = mybir.AxisListType
BF = ml_dtypes.bfloat16

D = 1024
H = 16
HD = 64
LH = 4
LHD = 64
LD = 256
CH = 32
INNER = 4096
WIN = 128

SEG_EL = 8192
RING_SLOTS = 2


def make_cfg(B=2, S=2048):
    T = 512 if S >= 512 else S
    return dict(B=B, S=S, T=T, NT=S // T, QG=T // 128, NLAT=S // CH, STI=S // 128)



def build_consts(cfg):
    T, STI, NLAT = cfg["T"], cfg["STI"], cfg["NLAT"]
    c = {}
    c["ident"] = np.eye(128, dtype=BF)
    c["identf32"] = np.eye(128, dtype=np.float32)
    qi = np.arange(128)[:, None]
    kj = np.arange(256)[None, :]
    binB = ((kj >= qi) & (kj <= qi + 127)).astype(BF)
    binA = ((kj >= np.maximum(qi, 128)) & (kj <= qi + 127)).astype(BF)
    c["binB"] = binB
    c["binA"] = binA
    lat = np.zeros((128, STI * NLAT), dtype=BF)
    for sti in range(STI):
        p = np.arange(128)[:, None]
        l = np.arange(NLAT)[None, :]
        lat[:, sti * NLAT:(sti + 1) * NLAT] = (l < (sti * 4 + p // 32)).astype(BF)
    c["latbin"] = lat
    P4 = np.zeros((128, 4), dtype=np.float32)
    P4[np.arange(128), np.arange(128) // 32] = 1.0 / 32.0
    c["P4"] = P4.astype(BF)
    sel = np.zeros((16, 8 * 128), dtype=BF)
    for ci in range(8):
        for p in range(128):
            sel[2 * ci + p // 64, ci * 128 + p] = 1.0
    c["sel"] = sel
    return c



def _pack_ws(w, och_count, k_count):
    d_in, d_out = w.shape
    assert d_in == k_count * 128 and d_out == och_count * 128
    t = w.reshape(k_count, 128, och_count, 128)
    t = np.transpose(t, (1, 2, 0, 3))
    return np.ascontiguousarray(t.reshape(128, och_count * k_count * 128))


def _pack_rhs(w, k_count):
    d_in, d_out = w.shape
    assert d_in == k_count * 128
    t = w.reshape(k_count, 128, d_out)
    t = np.transpose(t, (1, 0, 2))
    return np.ascontiguousarray(t.reshape(128, k_count * d_out))


def prep_core_inputs(ci, n_cores, cfg, inp):
    B, S = cfg["B"], cfg["S"]
    x = np.ascontiguousarray(inp["L0_attn_x"][ci * B:(ci + 1) * B])
    qkv_w = np.asarray(inp["L0_attn_qkv_w"])
    m = {}
    m["xin"] = x
    m["tq"] = _pack_ws(qkv_w[:, 0:1024], 8, 8)
    m["tk"] = _pack_ws(qkv_w[:, 1024:2048], 8, 8)
    m["tv"] = _pack_rhs(qkv_w[:, 2048:3072], 8)
    m["trq"] = _pack_ws(np.asarray(inp["L0_attn_rq_w"]), 2, 8)
    m["trout"] = _pack_ws(np.asarray(inp["L0_attn_rout_w"]), 8, 2)
    m["tow"] = _pack_rhs(np.asarray(inp["L0_attn_out_w"]), 8)
    m["trkv"] = _pack_ws(np.asarray(inp["L0_attn_rkv_w"]), 4, 8)
    wg = np.ascontiguousarray(np.asarray(inp["L0_ffn_w_gate"]).T)
    wu = np.ascontiguousarray(np.asarray(inp["L0_ffn_w_up"]).T)
    wd = np.ascontiguousarray(np.asarray(inp["L0_ffn_w_down"]).T)
    gw = wg.reshape(8, 128, 32, 128)
    uw = wu.reshape(8, 128, 32, 128)
    gu = np.empty((128, 64 * 8 * 128), dtype=np.float32)
    for og in range(64):
        src = gw if og % 2 == 0 else uw
        oc = og // 2
        blk = np.transpose(src[:, :, oc, :], (1, 0, 2)).reshape(128, 8 * 128)
        gu[:, og * 1024:(og + 1) * 1024] = blk
    m["tgu"] = np.ascontiguousarray(gu)
    m["tdn"] = _pack_rhs(wd, 32)
    qkv_b = np.asarray(inp["L0_attn_qkv_b"])
    m["bcol_qk"] = np.ascontiguousarray(qkv_b[0:2048].reshape(16, 128).T)
    m["bcol_rq"] = np.ascontiguousarray(np.asarray(inp["L0_attn_rq_b"]).reshape(2, 128).T)
    m["bcol_rkv"] = np.ascontiguousarray(np.asarray(inp["L0_attn_rkv_b"]).reshape(4, 128).T)
    m["bcol_rout"] = np.ascontiguousarray(np.asarray(inp["L0_attn_rout_b"]).reshape(8, 128).T)
    m["brow_v"] = np.ascontiguousarray(qkv_b[2048:3072].reshape(1, 1024))
    m["brow_out"] = np.ascontiguousarray(np.asarray(inp["L0_attn_out_b"]).reshape(1, 1024))
    m["gl"] = np.ascontiguousarray(np.asarray(inp["L0_attn_gl_w"]).reshape(64, 1))
    m["gr"] = np.ascontiguousarray(np.asarray(inp["L0_attn_gr_w"]).reshape(64, 1))
    return m



ENGS = ("sync", "tensor", "vector", "scalar", "gpsimd")


class Em:
    def __init__(self):
        self.q = {e: [] for e in ENGS}
        self.written = {e: set() for e in ENGS}
        self.readden = {e: set() for e in ENGS}
        self.counts = {}
        self.ranges = {}

    def _overlap(self, s1, s2):
        for a in s1:
            for b in s2:
                if a == b:
                    return True
                ra, rb = self.ranges.get(a), self.ranges.get(b)
                if ra and rb and ra[0] < rb[1] and rb[0] < ra[1]:
                    return True
        return False

    def _names(self, aps):
        out = set()
        for a in aps:
            if a is None or isinstance(a, (int, float)):
                continue
            out.add(a.tensor.name)
        return out

    def op(self, eng, fn, reads=(), writes=()):
        if getattr(self, "frozen", False):
            return
        if eng in ("vector", "scalar", "gpsimd"):
            rd = self._names(reads)
            wr = self._names(writes)
            if (self._overlap(wr, self.written[eng] | self.readden[eng])
                    or self._overlap(rd, self.written[eng])):
                self.q[eng].append(("drain",))
                self.written[eng] = set()
                self.readden[eng] = set()
            self.written[eng] |= wr
            self.readden[eng] |= rd
        self.q[eng].append(("op", fn))

    def inc(self, eng, sem, n=1):
        if getattr(self, "frozen", False):
            return self.counts.get(sem, 0)
        self.counts[sem] = self.counts.get(sem, 0) + n
        q = self.q[eng]
        i = len(q) - 1
        while i >= 0 and q[i][0] == "wait":
            i -= 1
        if i >= 0 and q[i][0] == "inc":
            q.append(("op", lambda e: e.nop(nofuse=True)))
        q.append(("inc", sem, n))
        return self.counts[sem]

    def wait(self, eng, sem, n=None):
        if getattr(self, "frozen", False):
            return
        if n is None:
            n = self.counts.get(sem, 0)
        if n > 0:
            self.q[eng].append(("wait", sem, n))

    def mark(self, eng):
        q = self.q[eng]
        i = len(q) - 1
        while i >= 0 and q[i][0] == "wait":
            i -= 1
        if i >= 0 and q[i][0] == "inc" and q[i][1] == "ms_" + eng:
            return self.counts["ms_" + eng]
        return self.inc(eng, "ms_" + eng, 1)

    def join(self, eng, src):
        self.wait(eng, "ms_" + src)

    def cnt(self, sem):
        return self.counts.get(sem, 0)

    def tt(self, eng, out, a, b, aop):
        self.op(eng, lambda e: e.tensor_tensor(out, a, b, aop), reads=[a, b], writes=[out])

    def ts(self, eng, out, a, s1, s2=None, op0=ALU.mult, op1=ALU.bypass):
        rd = [a] + [x for x in (s1, s2) if hasattr(x, "tensor")]
        self.op(eng, lambda e: e.tensor_scalar(out, a, s1, s2, op0, op1), reads=rd, writes=[out])

    def stt(self, eng, out, in0, scalar, in1, op0, op1):
        rd = [in0, in1] + ([scalar] if hasattr(scalar, "tensor") else [])
        self.op(eng, lambda e: e.scalar_tensor_tensor(out, in0, scalar, in1, op0, op1),
                reads=rd, writes=[out])

    def copy(self, eng, out, a):
        self.op(eng, lambda e: e.tensor_copy(out, a), reads=[a], writes=[out])

    def memset(self, eng, ap, v):
        self.op(eng, lambda e: e.memset(ap, v), reads=[], writes=[ap])

    def red(self, eng, out, a, rop, absval=False, neg=False):
        if rop == "max":
            self.op(eng, lambda e: e.reduce_max(out, a, AX.X, apply_absolute_value=absval,
                                                negate=neg), reads=[a], writes=[out])
        else:
            self.op(eng, lambda e: e.reduce_sum(out, a, AX.X, apply_absolute_value=absval,
                                                negate=neg), reads=[a], writes=[out])

    def recip(self, eng, out, a):
        self.op(eng, lambda e: e.reciprocal(out, a), reads=[a], writes=[out])

    def act(self, out, a, func, bias=0.0, scale=1.0, accum=None):
        rd = [a] + [x for x in (bias, scale) if hasattr(x, "tensor")]
        wr = [out] + ([accum] if accum is not None else [])
        self.op("scalar", lambda e: e.activation(out, a, func, bias=bias, scale=scale,
                                                 accum_out=accum), reads=rd, writes=wr)

    def mm(self, out, lhsT, rhs, start, stop):
        self.op("tensor", lambda e: e.matmul(out, lhsT, rhs, start=start, stop=stop))

    def trans(self, out, a, ident):
        self.op("tensor", lambda e: e.transpose(out, a, ident))

    def dma(self, eng, out, a, sem):
        self.op(eng, lambda e: e.dma_start(out=out, in_=a), reads=[a], writes=[out])
        return self.inc(eng, sem, 16)

    def table(self, set_name):
        if getattr(self, "frozen", False):
            return
        def fn(e):
            from concourse.hw_specs import get_activation_tables
            for idx, (nm, funcs) in enumerate(get_activation_tables(e.bass.m.arch).items()):
                if nm == set_name:
                    return e.add_instruction(mybir.InstLoadActFuncSet(
                        name=e.bass.get_next_instruction_name(), act_func_set_id=idx,
                        ins=[], outs=[]))
            raise ValueError(set_name)
        self.q["scalar"].append(("op", fn))

    def replay(self, block, sems):
        def section(engname):
            def body(engobj):
                last = None
                for item in self.q[engname]:
                    k = item[0]
                    if k == "op":
                        last = item[1](engobj)
                    elif k == "inc":
                        assert last is not None, f"inc with no prior op on {engname}"
                        last.then_inc(sems[item[1]], item[2])
                    elif k == "wait":
                        engobj.wait_ge(sems[item[1]], item[2])
                    elif k == "drain":
                        engobj.drain()
            return body
        block.sync(section("sync"))
        block.tensor(section("tensor"))
        block.vector(section("vector"))
        block.scalar(section("scalar"))
        block.gpsimd(section("gpsimd"))



class StopEmission(Exception):
    pass


def build_program(cfg, stop=None, dump=None):
    B, S, T, NT, QG, NLAT, STI = (cfg["B"], cfg["S"], cfg["T"], cfg["NT"], cfg["QG"],
                                  cfg["NLAT"], cfg["STI"])
    NSEQT = S // 128
    consts = build_consts(cfg)

    nc = bass.Bass("TRN2", target_bir_lowering=False)

    xin = nc.dram_tensor("xin", [B, S, D], F32, kind="ExternalInput").ap()
    yout = nc.dram_tensor("yout", [B, S, D], F32, kind="ExternalOutput").ap()
    hv = {}
    for nm, shape in [("tq", [128, 8192]), ("tk", [128, 8192]), ("tv", [128, 8192]),
                      ("trq", [128, 2048]), ("trout", [128, 2048]), ("tow", [128, 8192]),
                      ("trkv", [128, 4096]), ("tgu", [128, 65536]), ("tdn", [128, 32768]),
                      ("bcol_qk", [128, 16]), ("bcol_rq", [128, 2]), ("bcol_rkv", [128, 4]),
                      ("bcol_rout", [128, 8]), ("brow_v", [1, 1024]), ("brow_out", [1, 1024]),
                      ("gl", [64, 1]), ("gr", [64, 1])]:
        hv[nm] = nc.dram_tensor(nm, shape, F32, kind="ExternalInput").ap()
    cin = {k: nc.inline_tensor(v, "c_" + k).ap() for k, v in consts.items()}
    NSEG_TILE = 5 + 8 + 4
    tape = nc.dram_tensor("tape", [128, NSEG_TILE * SEG_EL], BF16, kind="Internal").ap()

    es = contextlib.ExitStack()
    sb = {}

    def alloc(name, shape, dtype):
        sb[name] = nc.alloc_sbuf_tensor(name, shape, dtype)
        return sb[name]

    alloc("x", [128, 4096], F32)
    alloc("y1n", [128, 4096], F32)
    alloc("y2", [128, 4096], F32)
    alloc("xqT", [128, 8 * T], BF16)
    alloc("kcarry", [128, 1024], BF16)
    alloc("vcarry", [128, 1024], BF16)
    alloc("ring", [128, RING_SLOTS * SEG_EL], BF16)
    alloc("rowpack", [128, 2176], F32)
    alloc("ident", [128, 128], BF16)
    alloc("identf32", [128, 128], F32)
    alloc("binAB", [128, 512], BF16)
    alloc("latbin", [128, STI * NLAT], BF16)
    alloc("P4", [128, 4], BF16)
    alloc("selc", [16, 1024], BF16)
    alloc("glgr", [128, 256], BF16)
    alloc("gl2", [128, 2], F32)
    alloc("rkres", [128, B * 2 * LHD], BF16)
    alloc("rvres", [64, B * LH * LHD], BF16)
    alloc("bcols", [128, 64], F32)
    alloc("stats", [128, 64], F32)
    alloc("mcols", [128, 8], F32)
    alloc("sc1", [1, 16], F32)
    alloc("outst", [128, 1024], F32)
    alloc("onescol", [128, 1], F32)

    AR = {}
    em = Em()
    ar_base = [None]

    def ar(name, shape, dtype, off):
        t = nc.alloc_sbuf_tensor_at(name, shape, dtype, offset=ar_base[0] + off)
        sb[name] = t
        esz = {F32: 4, BF16: 2, I32: 4}[dtype]
        em.ranges[t.name] = (off, off + shape[1] * esz)
        return t


    T2 = 2 * T
    T8 = 8 * T
    ARENA_BYTES = 0x17A00
    ar_base[0] = (nc.sbuf_base + 63) // 64 * 64
    assert nc.sbuf_top - ar_base[0] >= ARENA_BYTES, (
        f"SBUF over budget: base {nc.sbuf_base} top {nc.sbuf_top}")

    ar("xbTM", [128, 4096], BF16, 0x0000)
    ar("xhT", [128, T8], BF16, 0x2000)
    ar("qT", [128, T8], BF16, 0x4000)
    ar("kT", [128, T8], BF16, 0x6000)
    ar("vT", [128, 4096], BF16, 0x8000)
    ar("rqT", [128, T2], BF16, 0xA000)
    ar("probsB", [128, 2048], BF16, 0xA800)
    ar("probsTB", [128, 2048], BF16, 0xB800)
    ar("localT", [128, T8], BF16, 0xC800)
    ar("latoT", [128, T2], BF16, 0xE800)
    ar("latpr", [128, 256], BF16, 0xF000)
    ar("lattnT", [64, 512], BF16, 0xF200)
    ar("remT", [128, T8], BF16, 0xF600)
    ar("mixT", [128, T8], BF16, 0x11600)
    ar("dtmp", [128, 512], F32, 0x13600)
    ar("gatesf", [128, 512], F32, 0x13E00)
    ar("gatesb", [128, 512], BF16, 0x14600)
    ar("aq1i", [128, 1024], I32, 0x10000)
    ar("xqTM", [128, 4096], BF16, 0x15A00)
    ar("inner", [128, 32 * T], F32, 0x0000)
    ar("sg", [128, 1024], F32, 0x10000)
    ar("q2i", [128, 512], I32, 0x11000)
    ar("q2ch", [128, 1024], BF16, 0x11800)
    ar("mpos", [128, 512], F32, 0x12000)
    ar("mneg", [128, 512], F32, 0x12800)
    ar("sxbc", [128, 512], F32, 0x13000)
    ar("sxrow", [1, 512], F32, 0x13800)
    ar("sqscr", [128, 1024], F32, 0x14000)
    ar("outst2", [128, 1024], F32, 0x16A00)
    ar("wslab", [128, 8192], F32, 0x0000)
    ar("oslab", [128, 8192], BF16, 0x8000)
    ar("islab", [128, 4096], I32, 0xC000)
    ar("xbP", [128, 4096], BF16, 0x10000)
    ar("pooledP", [128, 512], BF16, 0x12000)
    ar("acc", [128, 8], F32, 0x12400)
    ar("abscr", [128, 1024], F32, 0x12500)
    ar("rvtmp", [128, 128], BF16, 0x13600)
    ar("w1slab", [128, 4096], F32, 0x13800)

    ps_l = [nc.alloc_psum_tensor("ps%d" % i, [128, 512], F32) for i in range(8)]
    psb_l = [tt.bitcast(BF16) for tt in ps_l]

    def pf(i, lo=0, hi=512, p0=0, p1=128):
        return ps_l[i].ap()[p0:p1, lo:hi]

    def pb(i, lo=0, hi=1024, p0=0, p1=128):
        return psb_l[i].ap()[p0:p1, lo:hi]

    A = lambda n: sb[n].ap()
    stage_no = [0]

    def ck():
        stage_no[0] += 1
        if stop is not None and stage_no[0] >= stop:
            em.frozen = True

    ones_row = lambda w: A("rowpack")[0:1, 2048:2048 + w]

    ring_state = dict(next_load=0, loaded_off={}, done_at={}, load_sem={})

    def seg_load(idx):
        slot = idx % RING_SLOTS
        prev = idx - RING_SLOTS
        if prev >= 0:
            em.wait("sync", "ms_tensor", ring_state["done_at"][prev])
        for sem, cntv in ring_state["tape_w"].get(idx % NSEG_TILE, []):
            em.wait("sync", sem, cntv)
        w = 4096 if (idx % NSEG_TILE) == 3 else SEG_EL
        dst = A("ring")[:, slot * SEG_EL:slot * SEG_EL + w]
        src = tape[:, (idx % NSEG_TILE) * SEG_EL:(idx % NSEG_TILE) * SEG_EL + w]
        n = em.dma("sync", dst, src, "s_tape%d" % slot)
        ring_state["load_sem"][idx] = ("s_tape%d" % slot, n)
        ring_state["loaded_off"][idx] = slot * SEG_EL

    def seg_ap(idx, lo, hi):
        off = ring_state["loaded_off"][idx]
        return A("ring")[:, off + lo:off + hi]

    def seg_wait_pe(idx):
        sem, n = ring_state["load_sem"][idx]
        em.wait("tensor", sem, n)

    def seg_done(idx):
        em.mark("tensor")
        ring_state["done_at"][idx] = em.cnt("ms_tensor")
        nl = ring_state["next_load"]
        if nl == idx + RING_SLOTS:
            pass
        want = idx + RING_SLOTS
        if want < ring_state["total"]:
            seg_load(want)
            ring_state["next_load"] = want + 1

    em.dma("sync", A("ident"), cin["ident"], "s_c")
    em.dma("sync", A("identf32"), cin["identf32"], "s_c")
    em.dma("sync", A("binAB")[:, 0:256], cin["binA"], "s_c")
    em.dma("sync", A("binAB")[:, 256:512], cin["binB"], "s_c")
    em.dma("sync", A("latbin"), cin["latbin"], "s_c")
    em.dma("sync", A("P4"), cin["P4"], "s_c")
    em.dma("sync", A("selc")[0:16, 0:1024], cin["sel"], "s_c")
    em.dma("sync", A("bcols")[:, 0:16], hv["bcol_qk"], "s_c")
    em.dma("sync", A("bcols")[:, 16:18], hv["bcol_rq"], "s_c")
    em.dma("sync", A("bcols")[:, 18:22], hv["bcol_rkv"], "s_c")
    em.dma("sync", A("bcols")[:, 22:30], hv["bcol_rout"], "s_c")
    em.dma("sync", A("rowpack")[0:1, 0:1024], hv["brow_v"], "s_c")
    em.dma("sync", A("rowpack")[0:1, 1024:2048], hv["brow_out"], "s_c")
    em.dma("sync", A("gl2")[0:64, 0:1], hv["gl"], "s_c")
    em.dma("sync", A("gl2")[64:128, 0:1], hv["gl"], "s_c")
    em.dma("sync", A("gl2")[0:64, 1:2], hv["gr"], "s_c")
    em.dma("sync", A("gl2")[64:128, 1:2], hv["gr"], "s_c")
    n_c = em.cnt("s_c")

    ck()
    em.table("natural_log_exp_and_others")

    em.wait("vector", "s_c", n_c)
    em.memset("vector", A("rowpack")[0:1, 2048:2176], 1.0)
    em.memset("vector", A("onescol"), 1.0)
    em.memset("vector", A("stats")[:, 60:61], 1e-5)
    em.memset("vector", A("stats")[:, 62:63], 0.2)
    em.ts("vector", A("aq1i")[:, 0:1], A("stats")[:, 62:63], 1.0, 256.5, ALU.mult, ALU.add)
    em.ts("vector", A("stats")[:, 63:64], A("aq1i")[:, 0:1], -256.0, None, ALU.add)
    em.ts("vector", A("mcols")[:, 7:8], A("stats")[:, 63:64], -0.5, 256.5, ALU.mult, ALU.add)
    em.ts("vector", A("bcols")[:, 32:40], A("bcols")[:, 0:8], 0.125)
    em.ts("vector", A("bcols")[:, 40:42], A("bcols")[:, 16:18], 0.125)
    em.memset("vector", A("glgr"), 0.0)
    for cidx in range(8):
        em.copy("vector", A("glgr")[0:64, cidx * 16 + 2 * cidx:cidx * 16 + 2 * cidx + 1],
                A("gl2")[0:64, 0:1])
        em.copy("vector", A("glgr")[64:128, cidx * 16 + 2 * cidx + 1:cidx * 16 + 2 * cidx + 2],
                A("gl2")[64:128, 0:1])
        em.copy("vector", A("glgr")[0:64, 128 + cidx * 16 + 2 * cidx:128 + cidx * 16 + 2 * cidx + 1],
                A("gl2")[0:64, 1:2])
        em.copy("vector",
                A("glgr")[64:128, 128 + cidx * 16 + 2 * cidx + 1:128 + cidx * 16 + 2 * cidx + 2],
                A("gl2")[64:128, 1:2])
    em.mark("vector")

    ck()
    ring_state["tape_w"] = {}
    SEGQ, SEGK, SEGV, SEGRQRO, SEGOW = 0, 1, 2, 3, 4
    tape_steps = []
    for seg, nm, dsto, tot in [(SEGQ, "tq", 0, 8192), (SEGK, "tk", 0, 8192),
                               (SEGV, "tv", 0, 8192), (SEGRQRO, "trq", 0, 2048),
                               (SEGRQRO, "trout", 2048, 2048), (SEGOW, "tow", 0, 8192)]:
        for half in range(0, tot, 4096):
            tape_steps.append((seg, nm, dsto, half, min(4096, tot - half)))
    cast_m = {}
    tw_m = {}

    def tape_step(si):
        seg, nm, dsto, half, w = tape_steps[si]
        sl = si % 2
        wap = A("wslab")[:, sl * 4096:sl * 4096 + w]
        oap = A("oslab")[:, sl * 4096:sl * 4096 + w]
        if si >= 2:
            em.wait("sync", "ms_vector", cast_m[si - 2])
        nl = em.dma("sync", wap, hv[nm][:, half:half + w], "s_wp%d" % sl)
        em.wait("vector", "s_wp%d" % sl, nl)
        if si >= 2:
            em.wait("vector", "s_tw%d" % sl, tw_m[si - 2])
        em.copy("vector", oap, wap)
        em.mark("vector")
        cast_m[si] = em.cnt("ms_vector")
        em.wait("sync", "ms_vector", cast_m[si])
        em.dma("sync", tape[:, seg * SEG_EL + dsto + half:seg * SEG_EL + dsto + half + w],
               oap, "s_tw%d" % sl)
        tw_m[si] = em.cnt("s_tw%d" % sl)
        ring_state["tape_w"].setdefault(seg, []).append(("s_tw%d" % sl, tw_m[si]))

    em.memset("vector", A("acc")[:, 0:8], 0.0)
    em.memset("vector", A("rvtmp"), 0.0)
    em.mark("vector")
    p1_steps = []
    for nm, nsl in (("tgu", 32), ("tdn", 16)):
        for sidx in range(nsl):
            p1_steps.append((nm, sidx))
    abs_m = {}
    addm = {}

    def p1_step(si):
        nm, sidx = p1_steps[si]
        sl = si % 2
        wap = A("w1slab")[:, sl * 2048:(sl + 1) * 2048]
        if si >= 2:
            em.wait("gpsimd", "ms_scalar", abs_m[si - 2])
        nl = em.dma("gpsimd", wap, hv[nm][:, sidx * 2048:(sidx + 1) * 2048], "s_gp%d" % sl)
        em.wait("scalar", "s_gp%d" % sl, nl)
        if si >= 1:
            em.wait("scalar", "ms_vector", addm.get(si - 1, 0))
        for j in range(2):
            em.act(A("abscr"), wap[:, j * 1024:(j + 1) * 1024], AF.Abs,
                   accum=A("stats")[:, 56 + 2 * sl + j:57 + 2 * sl + j])
        em.mark("scalar")
        abs_m[si] = em.cnt("ms_scalar")
        em.wait("vector", "ms_scalar", abs_m[si])
        for j in range(2):
            if nm == "tgu":
                og = sidx * 2 + j
                colidx = 0 if og % 2 == 0 else 1
            else:
                colidx = 6
            em.tt("vector", A("acc")[:, colidx:colidx + 1], A("acc")[:, colidx:colidx + 1],
                  A("stats")[:, 56 + 2 * sl + j:57 + 2 * sl + j], ALU.add)
        em.mark("vector")
        addm[si] = em.cnt("ms_vector")

    for si in range(len(p1_steps)):
        if si < len(tape_steps):
            tape_step(si)
        p1_step(si)
    em.mark("vector")

    em.join("tensor", "vector")
    for i, colidx in enumerate((0, 1, 6)):
        em.mm(pf(5, i, i + 1, 0, 1), A("acc")[:, colidx:colidx + 1],
              A("onescol"), start=True, stop=True)
    em.mark("tensor")
    em.join("vector", "tensor")
    NW = 4096 * 1024
    em.ts("vector", A("sc1")[0:1, 0:3], pf(5, 0, 3, 0, 1), 1.0 / NW, 1e-5, ALU.mult, ALU.max)
    em.recip("vector", A("sc1")[0:1, 4:7], A("sc1")[0:1, 0:3])
    em.ts("vector", A("sc1")[0:1, 8:9], A("sc1")[0:1, 5:6], 1e-5)
    em.tt("vector", A("sc1")[0:1, 9:10], A("sc1")[0:1, 2:3], A("sc1")[0:1, 1:2], ALU.mult)
    em.ts("vector", A("sc1")[0:1, 9:10], A("sc1")[0:1, 9:10], 1.0 / 127.0)
    em.mark("vector")
    em.join("tensor", "vector")
    bc_list = [(0, 0), (1, 1), (2, 8), (3, 9), (4, 4), (5, 5), (6, 6)]
    for mcol, scol in bc_list:
        em.mm(pf(6, mcol, mcol + 1), A("rowpack")[0:1, 2048:2176],
              A("sc1")[0:1, scol:scol + 1], start=True, stop=True)
    em.mark("tensor")
    em.join("vector", "tensor")
    em.copy("vector", A("mcols")[:, 0:7], pf(6, 0, 7))
    em.mark("vector")

    GU0 = 5
    DN0 = 13
    q2_m = {}
    wq_m = {}

    def p2_step(si):
        nm, sidx = p1_steps[si]
        sl = si % 2
        wap = A("wslab")[:, sl * 2048:(sl + 1) * 2048]
        oap = A("oslab")[:, sl * 2048:(sl + 1) * 2048]
        if si >= 2:
            em.wait("gpsimd", "ms_vector", q2_m[si - 2])
        else:
            em.wait("gpsimd", "ms_vector", em.cnt("ms_vector"))
            em.wait("gpsimd", "s_gp%d" % sl, em.cnt("s_gp%d" % sl))
        nl = em.dma("gpsimd", wap, hv[nm][:, sidx * 2048:(sidx + 1) * 2048], "s_g2p%d" % sl)
        em.wait("vector", "s_g2p%d" % sl, nl)
        if si >= 2:
            em.wait("vector", "s_twg%d" % sl, wq_m[si - 2])
        else:
            em.wait("vector", "s_tw0", em.cnt("s_tw0"))
            em.wait("vector", "s_tw1", em.cnt("s_tw1"))
        for j in range(2):
            if nm == "tgu":
                og = sidx * 2 + j
                sw = A("mcols")[:, 4:5] if og % 2 == 0 else A("mcols")[:, 5:6]
            else:
                sw = A("mcols")[:, 6:7]
            blk = wap[:, j * 1024:(j + 1) * 1024]
            fb = sb["islab"].bitcast(F32).ap()[:, (sl * 2 + j) * 1024:(sl * 2 + j + 1) * 1024]
            em.ts("vector", blk, blk, sw, None, ALU.mult)
            em.ts("vector", fb, blk, 0.5, None, ALU.is_gt)
            em.ts("vector", blk, blk, -0.5, None, ALU.is_lt)
            em.tt("vector", oap[:, j * 1024:(j + 1) * 1024], fb, blk, ALU.subtract)
        em.mark("vector")
        q2_m[si] = em.cnt("ms_vector")
        em.wait("gpsimd", "ms_vector", q2_m[si])
        if nm == "tgu":
            seg = GU0 + sidx // 4
            off = (sidx % 4) * 2048
        else:
            seg = DN0 + sidx // 4
            off = (sidx % 4) * 2048
        em.dma("gpsimd", tape[:, seg * SEG_EL + off:seg * SEG_EL + off + 2048], oap,
               "s_twg%d" % sl)
        wq_m[si] = em.cnt("s_twg%d" % sl)
        ring_state["tape_w"].setdefault(seg, []).append(("s_twg%d" % sl, wq_m[si]))

    rkvb = sb["w1slab"].bitcast(BF16).ap()
    em.wait("gpsimd", "ms_scalar", em.cnt("ms_scalar"))
    nl = em.dma("gpsimd", rkvb[:, 0:4096], hv["trkv"], "s_gp0")
    rkv_ld = nl
    p2i = [0]

    def pump2(n=2):
        for _ in range(n):
            if p2i[0] < len(p1_steps):
                p2_step(p2i[0])
                p2i[0] += 1

    for b in range(B):
        ck()
        pump2()
        ngr = (S // 128 + 3) // 4
        for grp in range(ngr):
            t0 = grp * 4
            ntl = min(4, S // 128 - t0)
            em.wait("sync", "ms_gpsimd", em.cnt("ms_gpsimd"))
            nl = em.dma("sync", A("x").rearrange("p (g d) -> p g d", d=1024)[:, 0:ntl],
                        xin[b][t0 * 128:(t0 + ntl) * 128, :].rearrange("(g p) d -> p g d", p=128),
                        "s_xp")
            em.wait("gpsimd", "s_xp", nl)
            em.join("gpsimd", "tensor")
            em.copy("gpsimd", A("xbP")[:, 0:ntl * 1024], A("x")[:, 0:ntl * 1024])
            em.mark("gpsimd")
            em.join("tensor", "gpsimd")
            for tl in range(ntl):
                tg = t0 + tl
                for kk in range(8):
                    em.mm(pf(kk, tg * 4, tg * 4 + 4),
                          A("xbP")[:, tl * 1024 + kk * 128:tl * 1024 + (kk + 1) * 128],
                          A("P4"), start=True, stop=True)
            em.mark("tensor")
            pump2()
        ck()
        em.join("scalar", "tensor")
        for kk in range(8):
            em.act(A("pooledP")[:, kk * NLAT:(kk + 1) * NLAT], pf(kk, 0, NLAT), AF.Identity)
        em.mark("scalar")
        ck()
        em.join("tensor", "scalar")
        em.wait("tensor", "s_gp0", rkv_ld)
        for och in range(4):
            for kk in range(8):
                em.mm(pf(och, 0, NLAT),
                      rkvb[:, (och * 8 + kk) * 128:(och * 8 + kk + 1) * 128],
                      A("pooledP")[:, kk * NLAT:(kk + 1) * NLAT], start=(kk == 0), stop=(kk == 7))
        em.mark("tensor")
        ck()
        pump2()
        em.join("scalar", "tensor")
        for och in range(2):
            em.act(A("rkres")[:, (b * 2 + och) * 64:(b * 2 + och) * 64 + NLAT],
                   pf(och, 0, NLAT), AF.Identity, bias=A("bcols")[:, 18 + och:19 + och])
        for och in range(2, 4):
            em.act(A("rvtmp")[:, 0:NLAT] if och == 2 else A("rvtmp")[:, 64:64 + NLAT],
                   pf(och, 0, NLAT), AF.Identity, bias=A("bcols")[:, 18 + och:19 + och])
        em.mark("scalar")
        ck()
        em.join("tensor", "scalar")
        for half in range(2):
            em.trans(pb(4, half * 256, half * 256 + 128, 0, 64),
                     A("rvtmp")[:, half * 64:(half + 1) * 64], A("ident"))
        em.mark("tensor")
        ck()
        em.join("scalar", "tensor")
        for half in range(2):
            em.act(A("rvres")[0:NLAT, (b * 2 + half) * 128:(b * 2 + half + 1) * 128],
                   pb(4, half * 256, half * 256 + 128, 0, NLAT), AF.Identity)
        em.mark("scalar")
        ck()
        em.join("tensor", "scalar")
        pump2()
    pump2(len(p1_steps))
    ck()
    ring_state["total"] = NSEG_TILE * B * NT
    ring_state["done_at"] = {}
    seg_load(0)
    seg_load(1)
    ring_state["next_load"] = 2

    x_loaded = [False]
    x_casted = [False]
    cast_mh = [None]
    prev_stats = [None]

    def out_rows(b, t, tg):
        return yout[b][t * T + tg * 128:t * T + (tg + 1) * 128, :]

    stc = {"nmax": 0, "rsum": 8, "rr": 16, "negmean": 24, "varp": 28, "rstd": 32,
           "rmax": 36, "sx": 40, "invsx": 44, "scale2": 48, "sx2": 52, "tmp": 56}

    def SC(name, i, w=1):
        return A("stats")[:, stc[name] + i:stc[name] + i + w]

    try:
      for b in range(B):
        for t in range(NT):
            tile_idx = b * NT + t
            seg0 = tile_idx * NSEG_TILE

            em.table("natural_log_exp_and_others")

            if not x_loaded[0]:
                em.wait("sync", "ms_gpsimd", em.cnt("ms_gpsimd"))
                nl = em.dma("sync", A("x").rearrange("p (g d) -> p g d", d=1024),
                            xin[b][t * T:(t + 1) * T, :].rearrange("(g p) d -> p g d", p=128),
                            "s_x")
            x_loaded[0] = False
            if not x_casted[0]:
                em.wait("vector", "s_x", em.cnt("s_x"))
                if tile_idx == 0:
                    for s in ("s_twg0", "s_twg1", "s_gp0", "s_gp1", "s_g2p0", "s_g2p1",
                              "s_wp0", "s_wp1", "s_xp"):
                        em.wait("vector", s, em.cnt(s))
                cast_mh[0] = {}
                for c in range((QG + 3) // 4):
                    lo, hi = 4 * c * 1024, min(4 * c + 4, QG) * 1024
                    em.copy("vector", A("xbTM")[:, lo:hi], A("x")[:, lo:hi])
                    em.mark("vector")
                    cast_mh[0][c] = em.cnt("ms_vector")
            x_casted[0] = False
            cast_m = cast_mh[0]

            if t == 0:
                em.join("gpsimd", "tensor")
                em.memset("gpsimd", A("kcarry"), 0.0)
                em.memset("gpsimd", A("vcarry"), 0.0)
                em.mark("gpsimd")
                em.join("tensor", "gpsimd")

            ck()
            em.join("tensor", "vector")
            a2m = {}
            for tg in range(QG):
                bk = 6 + tg % 2
                em.wait("tensor", "ms_vector", cast_m[tg // 4])
                if tg >= 2:
                    em.wait("tensor", "ms_scalar", a2m[tg - 2])
                for kk in range(8):
                    em.trans(pb(bk, kk * 128, (kk + 1) * 128),
                             A("xbTM")[:, tg * 1024 + kk * 128:tg * 1024 + (kk + 1) * 128],
                             A("ident"))
                em.mark("tensor")
                em.join("scalar", "tensor")
                dst = A("xhT").rearrange("p (k t) -> p k t", k=8)[:, :, tg * 128:(tg + 1) * 128]
                em.act(dst, pb(bk).rearrange("p (k c) -> p k c", k=8), AF.Identity)
                em.mark("scalar")
                a2m[tg] = em.cnt("ms_scalar")
            em.join("tensor", "scalar")

            ck()
            if prev_stats[0] is not None:
                pb_, pt_ = prev_stats[0]
                em.join("vector", "gpsimd")
                for tg in range(QG):
                    em.act(SC("rstd", tg), SC("varp", tg), AF.Ln)
                    em.act(SC("rstd", tg), SC("rstd", tg), AF.Exp, scale=-0.5)
                em.mark("scalar")
                em.join("vector", "scalar")
                outdma = {}
                for tg in range(QG):
                    ob = A("outst") if tg % 2 == 0 else A("outst2")
                    sem = "s_out%d" % (tg % 2)
                    if tg >= 2:
                        em.wait("vector", sem, outdma[tg - 2])
                    em.ts("vector", ob, A("y2")[:, tg * 1024:(tg + 1) * 1024],
                          SC("negmean", tg), SC("rstd", tg), ALU.add, ALU.mult)
                    em.mark("vector")
                    em.join("scalar", "vector")
                    if dump is None:
                        em.dma("scalar", out_rows(pb_, pt_, tg), ob, sem)
                    outdma[tg] = em.cnt(sem)

            seg_wait_pe(seg0 + SEGQ)
            wave_evac = {}

            def proj_wave(name, segi, ochs, banks, dst, dstoff, bias0, scale):
                prev = wave_evac.get(banks[0] // 4)
                if prev is not None:
                    em.wait("tensor", "ms_scalar", prev)
                for j, och in enumerate(ochs):
                    for kk in range(8):
                        em.mm(pf(banks[j]),
                              seg_ap(segi, (och * 8 + kk) * 128, (och * 8 + kk + 1) * 128),
                              A("xhT")[:, kk * T:(kk + 1) * T], start=(kk == 0), stop=(kk == 7))
                em.mark("tensor")
                mmk = em.cnt("ms_tensor")
                em.wait("scalar", "ms_tensor", mmk)
                for j, och in enumerate(ochs):
                    em.act(A(dst)[:, (dstoff + och) * T:(dstoff + och + 1) * T], pf(banks[j]),
                           AF.Identity, bias=A("bcols")[:, bias0 + och:bias0 + och + 1],
                           scale=scale)
                em.mark("scalar")
                wave_evac[banks[0] // 4] = em.cnt("ms_scalar")

            proj_wave("qT", seg0 + SEGQ, range(0, 4), [0, 1, 2, 3], "qT", 0, 32, 0.125)
            proj_wave("qT", seg0 + SEGQ, range(4, 8), [4, 5, 6, 7], "qT", 0, 32, 0.125)
            seg_wait_pe(seg0 + SEGK)
            proj_wave("kT", seg0 + SEGK, range(0, 4), [0, 1, 2, 3], "kT", 0, 8, 1.0)
            proj_wave("kT", seg0 + SEGK, range(4, 8), [4, 5, 6, 7], "kT", 0, 8, 1.0)
            seg_done(seg0 + SEGQ)
            seg_wait_pe(seg0 + SEGV)

            def v_wave(tgs, banks):
                prev = wave_evac.get(banks[0] // 4)
                if prev is not None:
                    em.wait("tensor", "ms_scalar", prev)
                for j, tg in enumerate(tgs):
                    for kk in range(8):
                        lh = A("xhT")[:, kk * T + tg * 128:kk * T + (tg + 1) * 128]
                        em.mm(pf(banks[2 * j]), lh,
                              seg_ap(seg0 + SEGV, kk * 1024, kk * 1024 + 512),
                              start=(kk == 0), stop=False)
                        em.mm(pf(banks[2 * j + 1]), lh,
                              seg_ap(seg0 + SEGV, kk * 1024 + 512, (kk + 1) * 1024),
                              start=(kk == 0), stop=False)
                    em.mm(pf(banks[2 * j]), ones_row(128), A("rowpack")[0:1, 0:512],
                          start=False, stop=True)
                    em.mm(pf(banks[2 * j + 1]), ones_row(128), A("rowpack")[0:1, 512:1024],
                          start=False, stop=True)
                em.mark("tensor")
                mmk = em.cnt("ms_tensor")
                em.wait("scalar", "ms_tensor", mmk)
                for j, tg in enumerate(tgs):
                    em.act(A("vT")[:, tg * 1024:tg * 1024 + 512],
                           pf(banks[2 * j]), AF.Identity)
                    em.act(A("vT")[:, tg * 1024 + 512:(tg + 1) * 1024],
                           pf(banks[2 * j + 1]), AF.Identity)
                em.mark("scalar")
                wave_evac[banks[0] // 4] = em.cnt("ms_scalar")

            v_wave([0, 1], [0, 1, 2, 3])
            v_wave([2, 3], [4, 5, 6, 7])
            seg_done(seg0 + SEGK)
            seg_wait_pe(seg0 + SEGRQRO)
            prev = wave_evac.get(0)
            if prev is not None:
                em.wait("tensor", "ms_scalar", prev)
            for och in range(2):
                for kk in range(8):
                    em.mm(pf(och), seg_ap(seg0 + SEGRQRO, (och * 8 + kk) * 128,
                                          (och * 8 + kk + 1) * 128),
                          A("xhT")[:, kk * T:(kk + 1) * T], start=(kk == 0), stop=(kk == 7))
            em.mark("tensor")
            em.join("scalar", "tensor")
            for och in range(2):
                em.act(A("rqT")[:, och * T:(och + 1) * T], pf(och), AF.Identity,
                       bias=A("bcols")[:, 40 + och:41 + och], scale=0.125)
            em.mark("scalar")
            em.join("tensor", "scalar")
            seg_done(seg0 + SEGV)

            ck()
            ev_marks = {}

            def l_scores(g, m):
                par = m % 2
                key = ("exp", g * 8 + m - 2)
                if key in ev_marks:
                    em.wait("tensor", "ms_scalar", ev_marks[key])
                for i, h in enumerate((m, m + 8)):
                    och = h // 2
                    rlo = (h % 2) * 64
                    qsl = A("qT")[rlo:rlo + 64, och * T + g * 128:och * T + (g + 1) * 128]
                    dst = pf(par, i * 256, (i + 1) * 256)
                    if g == 0:
                        em.mm(pf(par, i * 256, i * 256 + 128), qsl,
                              A("kcarry")[rlo:rlo + 64, och * 128:(och + 1) * 128],
                              start=True, stop=True)
                        em.mm(pf(par, i * 256 + 128, i * 256 + 256), qsl,
                              A("kT")[rlo:rlo + 64, och * T:och * T + 128],
                              start=True, stop=True)
                    else:
                        em.mm(dst, qsl,
                              A("kT")[rlo:rlo + 64, och * T + (g - 1) * 128:och * T + (g + 1) * 128],
                              start=True, stop=True)
                em.mark("tensor")
                ev_marks[("sc", g * 8 + m)] = em.cnt("ms_tensor")

            def l_chain1(g, m):
                par = m % 2
                k = g * 8 + m
                first_g = (t == 0 and g == 0)
                pslot = (k % 4) * 512
                em.wait("vector", "ms_tensor", ev_marks[("sc", k)])
                em.red("vector", SC("nmax", 2 * par, 2),
                       ps_l[par].ap().rearrange("p (bk c) -> p bk c", c=256),
                       "max", neg=True)
                em.mark("vector")
                em.join("scalar", "vector")
                for i in range(2):
                    em.act(A("probsB")[:, pslot + i * 256:pslot + (i + 1) * 256],
                           pf(par, i * 256, (i + 1) * 256), AF.Exp,
                           bias=SC("nmax", i + 2 * par))
                em.mark("scalar")
                ev_marks[("exp", k)] = em.cnt("ms_scalar")
                em.join("gpsimd", "scalar")
                mofs = 0 if first_g else 256
                for i in range(2):
                    pr = A("probsB")[:, pslot + i * 256:pslot + (i + 1) * 256]
                    em.tt("gpsimd", pr, pr, A("binAB")[:, mofs:mofs + 256], ALU.mult)
                em.mark("gpsimd")
                ev_marks[("mask", k)] = em.cnt("ms_gpsimd")

            def l_chain2(g, m):
                par = m % 2
                k = g * 8 + m
                first_g = (t == 0 and g == 0)
                pslot = (k % 4) * 512
                em.wait("vector", "ms_gpsimd", ev_marks[("mask", k)])
                em.red("vector", SC("rsum", 2 * par, 2),
                       A("probsB").rearrange("p (bk c) -> p bk c", c=256)[:, 2 * (k % 4):2 * (k % 4) + 2],
                       "sum")
                if first_g:
                    em.ts("vector", SC("rsum", 2 * par, 2), SC("rsum", 2 * par, 2), 1e-30,
                          None, ALU.max)
                em.recip("vector", SC("rr", 2 * par, 2), SC("rsum", 2 * par, 2))
                for i in range(2):
                    pr = A("probsB")[:, pslot + i * 256:pslot + (i + 1) * 256]
                    em.ts("vector", pr, pr, SC("rr", i + 2 * par))
                em.mark("vector")
                ev_marks[("norm", k)] = em.cnt("ms_vector")

            def l_trans(g, m):
                par = m % 2
                k = g * 8 + m
                pslot = (k % 4) * 512
                em.wait("tensor", "ms_vector", ev_marks[("norm", k)])
                key = ("ptev", k - 1)
                if key in ev_marks:
                    em.wait("tensor", ev_marks[key][0], ev_marks[key][1])
                for i in range(2):
                    pr0 = A("probsB")[:, pslot + i * 256:pslot + i * 256 + 128]
                    pr1 = A("probsB")[:, pslot + i * 256 + 128:pslot + (i + 1) * 256]
                    em.trans(pb(2, par * 512 + i * 256, par * 512 + i * 256 + 128), pr0,
                             A("ident"))
                    em.trans(pb(2, par * 512 + i * 256 + 128, par * 512 + (i + 1) * 256), pr1,
                             A("ident"))
                em.mark("tensor")
                eng = "scalar" if k % 2 == 0 else "vector"
                em.join(eng, "tensor")
                if eng == "scalar":
                    em.act(A("probsTB")[:, pslot:pslot + 512],
                           pb(2, par * 512, par * 512 + 512), AF.Identity)
                else:
                    em.copy("vector", A("probsTB")[:, pslot:pslot + 512],
                            pb(2, par * 512, par * 512 + 512))
                em.mark(eng)
                ev_marks[("ptev", k)] = ("ms_" + eng, em.cnt("ms_" + eng))

            def l_pv(g, m):
                par = m % 2
                k = g * 8 + m
                pslot = (k % 4) * 512
                em.wait("tensor", ev_marks[("ptev", k)][0], ev_marks[("ptev", k)][1])
                key = ("lev", k - 1)
                if key in ev_marks:
                    em.wait("tensor", ev_marks[key][0], ev_marks[key][1])
                for i, h in enumerate((m, m + 8)):
                    vcol = h * 64
                    if g == 0:
                        vprev = A("vcarry")[:, vcol:vcol + 64]
                    else:
                        vprev = A("vT")[:, (g - 1) * 1024 + vcol:(g - 1) * 1024 + vcol + 64]
                    vcur = A("vT")[:, g * 1024 + vcol:g * 1024 + vcol + 64]
                    dst = pf(3, par * 256 + i * 128, par * 256 + (i + 1) * 128, 0, 64)
                    em.mm(dst, vprev, A("probsTB")[:, pslot + i * 256:pslot + i * 256 + 128],
                          start=True, stop=False)
                    em.mm(dst, vcur, A("probsTB")[:, pslot + i * 256 + 128:pslot + (i + 1) * 256],
                          start=False, stop=True)
                em.mark("tensor")
                pvm = em.cnt("ms_tensor")
                em.wait("scalar", "ms_tensor", pvm)
                for i, h in enumerate((m, m + 8)):
                    och = h // 2
                    rlo = (h % 2) * 64
                    em.act(A("localT")[rlo:rlo + 64, och * T + g * 128:och * T + (g + 1) * 128],
                           pf(3, par * 256 + i * 128, par * 256 + (i + 1) * 128, 0, 64),
                           AF.Identity)
                em.mark("scalar")
                ev_marks[("lev", k)] = ("ms_scalar", em.cnt("ms_scalar"))

            def lat_scores(tg):
                key = ("lexp", tg - 2)
                if key in ev_marks:
                    em.wait("tensor", "ms_scalar", ev_marks[key])
                for lhi in range(LH):
                    och = lhi // 2
                    rlo = (lhi % 2) * 64
                    em.mm(pf(4 + lhi % 2, (lhi // 2) * 64, (lhi // 2) * 64 + NLAT),
                          A("rqT")[rlo:rlo + 64, och * T + tg * 128:och * T + (tg + 1) * 128],
                          A("rkres")[rlo:rlo + 64, (b * 2 + och) * 64:(b * 2 + och) * 64 + NLAT],
                          start=True, stop=True)
                em.mark("tensor")
                ev_marks[("lsc", tg)] = em.cnt("ms_tensor")

            def lat_chain(tg):
                sti = t * QG + tg
                em.wait("vector", "ms_tensor", ev_marks[("lsc", tg)])
                for lhi in range(LH):
                    em.red("vector", SC("nmax", 4 + lhi),
                           pf(4 + lhi % 2, (lhi // 2) * 64, (lhi // 2) * 64 + NLAT),
                           "max", neg=True)
                em.mark("vector")
                em.join("scalar", "vector")
                for lhi in range(LH):
                    em.act(A("latpr")[:, lhi * 64:lhi * 64 + NLAT],
                           pf(4 + lhi % 2, (lhi // 2) * 64, (lhi // 2) * 64 + NLAT),
                           AF.Exp, bias=SC("nmax", 4 + lhi))
                em.mark("scalar")
                ev_marks[("lexp", tg)] = em.cnt("ms_scalar")
                em.join("gpsimd", "scalar")
                for lhi in range(LH):
                    pr = A("latpr")[:, lhi * 64:lhi * 64 + NLAT]
                    em.tt("gpsimd", pr, pr, A("latbin")[:, sti * NLAT:(sti + 1) * NLAT], ALU.mult)
                em.mark("gpsimd")
                em.join("vector", "gpsimd")
                for lhi in range(LH):
                    em.red("vector", SC("rsum", 4 + lhi), A("latpr")[:, lhi * 64:lhi * 64 + NLAT],
                           "sum")
                if tg == 0 and t == 0:
                    em.ts("vector", SC("rsum", 4, 4), SC("rsum", 4, 4), 1e-30, None, ALU.max)
                em.recip("vector", SC("rr", 4, 4), SC("rsum", 4, 4))
                for lhi in range(LH):
                    pr = A("latpr")[:, lhi * 64:lhi * 64 + NLAT]
                    em.ts("vector", pr, pr, SC("rr", 4 + lhi))
                em.mark("vector")
                ev_marks[("lnorm", tg)] = em.cnt("ms_vector")

            def lat_pv(tg):
                em.wait("tensor", "ms_vector", ev_marks[("lnorm", tg)])
                key = ("llev", tg - 1)
                if key in ev_marks:
                    em.wait("tensor", "ms_scalar", ev_marks[key])
                for lhi in range(LH):
                    em.trans(pb(6, lhi * 128, lhi * 128 + 128, 0, NLAT),
                             A("latpr")[:, lhi * 64:lhi * 64 + NLAT], A("ident"))
                em.mark("tensor")
                em.join("scalar", "tensor")
                for lhi in range(LH):
                    em.act(A("lattnT")[0:NLAT, lhi * 128:lhi * 128 + 128],
                           pb(6, lhi * 128, lhi * 128 + 128, 0, NLAT), AF.Identity)
                em.mark("scalar")
                em.join("tensor", "scalar")
                for lhi in range(LH):
                    rvoff = (b * 2 + lhi // 2) * 128 + (lhi % 2) * 64
                    em.mm(pf(7, lhi * 128, (lhi + 1) * 128, 0, 64),
                          A("rvres")[0:NLAT, rvoff:rvoff + 64],
                          A("lattnT")[0:NLAT, lhi * 128:(lhi + 1) * 128], start=True, stop=True)
                em.mark("tensor")
                lpv = em.cnt("ms_tensor")
                em.wait("scalar", "ms_tensor", lpv)
                for lhi in range(LH):
                    och = lhi // 2
                    rlo = (lhi % 2) * 64
                    em.act(A("latoT")[rlo:rlo + 64, och * T + tg * 128:och * T + (tg + 1) * 128],
                           pf(7, lhi * 128, (lhi + 1) * 128, 0, 64), AF.Identity)
                em.mark("scalar")
                ev_marks[("llev", tg)] = em.cnt("ms_scalar")

            NB = QG * 8

            def batch(k):
                return (k // 8, k % 8)

            for i in range(NB + 4):
                if i < NB:
                    l_scores(*batch(i))
                if 0 <= i - 1 < NB:
                    l_chain1(*batch(i - 1))
                if 0 <= i - 2 < NB:
                    l_chain2(*batch(i - 2))
                if 0 <= i - 3 < NB:
                    l_trans(*batch(i - 3))
                if 0 <= i - 4 < NB:
                    l_pv(*batch(i - 4))
                if i % 8 == 4 and i // 8 < QG:
                    lat_scores(i // 8)
                if i % 8 == 6 and i // 8 < QG:
                    lat_chain(i // 8)
                if i % 8 == 0 and 0 < i // 8 <= QG:
                    lat_pv(i // 8 - 1)

            em.join("gpsimd", "tensor")
            em.copy("gpsimd",
                    A("kcarry").rearrange("p (k c) -> p k c", k=8),
                    A("kT").rearrange("p (k t) -> p k t", k=8)[:, :, T - 128:T])
            em.copy("gpsimd", A("vcarry"), A("vT")[:, (QG - 1) * 1024:QG * 1024])
            em.mark("gpsimd")
            ck()
            em.join("tensor", "scalar")
            em.join("tensor", "vector")
            em.join("tensor", "gpsimd")
            seg_wait_pe(seg0 + SEGRQRO)
            for grp in range(2):
                for oc in range(grp * 4, grp * 4 + 4):
                    for L in range(2):
                        em.mm(pf(oc % 4),
                              seg_ap(seg0 + SEGRQRO, 2048 + (oc * 2 + L) * 128,
                                     2048 + (oc * 2 + L + 1) * 128),
                              A("latoT")[:, L * T:(L + 1) * T], start=(L == 0), stop=(L == 1))
                em.mark("tensor")
                em.join("scalar", "tensor")
                for oc in range(grp * 4, grp * 4 + 4):
                    em.act(A("remT")[:, oc * T:(oc + 1) * T], pf(oc % 4), AF.Identity,
                           bias=A("bcols")[:, 22 + oc:23 + oc])
                em.mark("scalar")
                em.join("tensor", "scalar")
            seg_done(seg0 + SEGRQRO)
            for cidx in range(8):
                em.mm(pf(4, 0, 512, 0, 16), A("glgr")[:, cidx * 16:(cidx + 1) * 16],
                      A("localT")[:, cidx * T:(cidx + 1) * T], start=(cidx == 0), stop=False)
            for cidx in range(8):
                em.mm(pf(4, 0, 512, 0, 16), A("glgr")[:, 128 + cidx * 16:128 + (cidx + 1) * 16],
                      A("remT")[:, cidx * T:(cidx + 1) * T], start=False, stop=(cidx == 7))
            em.mark("tensor")
            em.join("scalar", "tensor")
            em.act(A("gatesf")[0:16, 0:512], pf(4, 0, 512, 0, 16), AF.Exp, scale=-1.0)
            em.mark("scalar")
            em.join("vector", "scalar")
            em.ts("vector", A("gatesf")[0:16, 0:512], A("gatesf")[0:16, 0:512], 1.0, None, ALU.add)
            em.recip("vector", A("gatesf")[0:16, 0:512], A("gatesf")[0:16, 0:512])
            em.copy("vector", A("gatesb")[0:16, 0:512], A("gatesf")[0:16, 0:512])
            em.mark("vector")
            gate_mark = em.cnt("ms_vector")
            em.wait("tensor", "ms_vector", gate_mark)
            for cidx in range(8):
                gb = 5 + cidx % 2
                if cidx >= 2:
                    em.wait("tensor", "ms_vector", ev_marks.get(("mix", cidx - 2), 0))
                em.mm(pf(gb), A("selc")[0:16, cidx * 128:(cidx + 1) * 128],
                      A("gatesb")[0:16, 0:512], start=True, stop=True)
                em.mark("tensor")
                em.join("vector", "tensor")
                lc = A("localT")[:, cidx * T:(cidx + 1) * T]
                rc = A("remT")[:, cidx * T:(cidx + 1) * T]
                em.tt("vector", A("dtmp"), lc, rc, ALU.subtract)
                em.tt("vector", A("dtmp"), A("dtmp"), pf(gb), ALU.mult)
                em.tt("vector", A("mixT")[:, cidx * T:(cidx + 1) * T], A("dtmp"), rc, ALU.add)
                em.mark("vector")
                ev_marks[("mix", cidx)] = em.cnt("ms_vector")

            ck()
            seg_wait_pe(seg0 + SEGOW)
            em.join("tensor", "vector")
            em.wait("scalar", "s_out0", em.cnt("s_out0"))
            em.wait("scalar", "s_out1", em.cnt("s_out1"))
            mm_marks = {}
            for tg in range(QG):
                for cidx in range(8):
                    lh = A("mixT")[:, cidx * T + tg * 128:cidx * T + (tg + 1) * 128]
                    em.mm(pf(2 * tg), lh, seg_ap(seg0 + SEGOW, cidx * 1024, cidx * 1024 + 512),
                          start=(cidx == 0), stop=False)
                    em.mm(pf(2 * tg + 1), lh,
                          seg_ap(seg0 + SEGOW, cidx * 1024 + 512, (cidx + 1) * 1024),
                          start=(cidx == 0), stop=False)
                em.mm(pf(2 * tg), ones_row(128), A("rowpack")[0:1, 1024:1536],
                      start=False, stop=True)
                em.mm(pf(2 * tg + 1), ones_row(128), A("rowpack")[0:1, 1536:2048],
                      start=False, stop=True)
                em.mark("tensor")
                mm_marks[tg] = em.cnt("ms_tensor")
            seg_done(seg0 + SEGOW)
            for tg in range(QG):
                y1 = A("y1n")[:, tg * 1024:(tg + 1) * 1024]
                em.wait("vector", "ms_tensor", mm_marks[tg])
                em.tt("vector", y1[:, 0:512], pf(2 * tg), A("x")[:, tg * 1024:tg * 1024 + 512],
                      ALU.add)
                em.tt("vector", y1[:, 512:1024], pf(2 * tg + 1),
                      A("x")[:, tg * 1024 + 512:(tg + 1) * 1024], ALU.add)
                em.red("vector", SC("negmean", tg), y1, "sum", neg=True)
                em.ts("vector", SC("negmean", tg), SC("negmean", tg), 1.0 / 1024.0)
                em.mark("vector")
                mm_marks[("r", tg)] = em.cnt("ms_vector")
            for tg in range(QG):
                y1 = A("y1n")[:, tg * 1024:(tg + 1) * 1024]
                em.wait("scalar", "ms_vector", mm_marks[("r", tg)])
                em.act(A("outst"), y1, AF.Square, bias=SC("negmean", tg),
                       accum=SC("varp", tg))
                em.act(SC("rstd", tg), SC("varp", tg), AF.Ln, bias=A("stats")[:, 60:61], scale=1.0 / 1024.0)
                em.act(SC("rstd", tg), SC("rstd", tg), AF.Exp, scale=-0.5)
                em.mark("scalar")
                mm_marks[("s", tg)] = em.cnt("ms_scalar")
            for tg in range(QG):
                y1 = A("y1n")[:, tg * 1024:(tg + 1) * 1024]
                em.wait("vector", "ms_scalar", mm_marks[("s", tg)])
                em.ts("vector", y1, y1, SC("negmean", tg), SC("rstd", tg), ALU.add, ALU.mult)
                em.mark("vector")
            nb, ntile = (b, t + 1) if t + 1 < NT else (b + 1, 0)
            if nb < B:
                em.wait("sync", "ms_vector", em.cnt("ms_vector"))
                em.dma("sync", A("x").rearrange("p (g d) -> p g d", d=1024),
                       xin[nb][ntile * T:(ntile + 1) * T, :].rearrange("(g p) d -> p g d", p=128),
                       "s_x")
                x_loaded[0] = True

            ck()
            em.wait("vector", "s_out0", em.cnt("s_out0"))
            em.wait("vector", "s_out1", em.cnt("s_out1"))
            aq_marks = {}
            for tg in range(QG):
                y1 = A("y1n")[:, tg * 1024:(tg + 1) * 1024]
                em.red("vector", SC("rmax", tg), y1, "max", absval=True)
                em.ts("vector", SC("rmax", tg), SC("rmax", tg), 1e-5, None, ALU.max)
                em.recip("vector", SC("sx", tg), SC("rmax", tg))
                em.ts("vector", SC("sx", tg), SC("sx", tg), 127.0)
                em.ts("vector", SC("invsx", tg), SC("rmax", tg), 1.0 / 127.0)
                em.ts("vector", A("outst"), y1, SC("sx", tg), None, ALU.mult)
                em.ts("vector", A("aq1i"), A("outst"), A("mcols")[:, 7:8], None, ALU.add)
                em.ts("vector", A("xqTM")[:, tg * 1024:(tg + 1) * 1024], A("aq1i"),
                      -256.0, SC("invsx", tg), ALU.add, ALU.mult)
                em.mark("vector")
                aq_marks[tg] = em.cnt("ms_vector")
            for tg in range(QG):
                bk = 6 + tg % 2
                em.wait("tensor", "ms_vector", aq_marks[tg])
                if tg >= 2:
                    em.wait("tensor", "ms_scalar", aq_marks[("e", tg - 2)])
                for kk in range(8):
                    em.trans(pb(bk, kk * 128, (kk + 1) * 128),
                             A("xqTM")[:, tg * 1024 + kk * 128:tg * 1024 + (kk + 1) * 128],
                             A("ident"))
                em.mark("tensor")
                em.join("scalar", "tensor")
                dst = A("xqT").rearrange("p (k t) -> p k t", k=8)[:, :, tg * 128:(tg + 1) * 128]
                em.act(dst, pb(bk).rearrange("p (k c) -> p k c", k=8), AF.Identity)
                em.mark("scalar")
                aq_marks[("e", tg)] = em.cnt("ms_scalar")
            em.join("tensor", "scalar")
            em.table("silu_and_others")
            if tile_idx == 0:
                for s in ("s_twg0", "s_twg1"):
                    em.wait("gpsimd", s, em.cnt(s))
            for e in ("tensor", "vector", "scalar", "gpsimd"):
                for f in ("tensor", "vector", "scalar", "gpsimd"):
                    if e != f:
                        em.join(e, f)

            bank_free = {}
            sg_free = {}
            for og in range(64):
                segi = seg0 + GU0 + og // 8
                if og % 8 == 0:
                    seg_wait_pe(segi)
                bank = og % 8
                if og >= 8:
                    em.wait("tensor", "ms_vector" if (og - 8) % 2 == 1 else "ms_scalar",
                            bank_free[og - 8])
                for kk in range(8):
                    em.mm(pf(bank),
                          seg_ap(segi, ((og % 8) * 8 + kk) * 128, ((og % 8) * 8 + kk + 1) * 128),
                          A("xqT")[:, kk * T:(kk + 1) * T], start=(kk == 0), stop=(kk == 7))
                em.mark("tensor")
                if og % 8 == 7:
                    seg_done(segi)
                cpair = og // 2
                if og % 2 == 0:
                    em.join("scalar", "tensor")
                    if cpair >= 2:
                        em.wait("scalar", "ms_vector", sg_free[cpair - 2])
                    em.act(A("sg")[:, (cpair % 2) * 512:(cpair % 2 + 1) * 512], pf(bank),
                           AF.Silu, scale=A("mcols")[:, 0:1])
                    em.mark("scalar")
                    bank_free[og] = em.cnt("ms_scalar")
                else:
                    em.join("vector", "scalar")
                    em.join("vector", "tensor")
                    em.tt("vector", A("inner")[:, cpair * T:(cpair + 1) * T],
                          A("sg")[:, (cpair % 2) * 512:(cpair % 2 + 1) * 512], pf(bank), ALU.mult)
                    em.mark("vector")
                    bank_free[og] = em.cnt("ms_vector")
                    sg_free[cpair] = em.cnt("ms_vector")

            ck()
            inner3 = A("inner").rearrange("p (c t) -> p t c", c=32)
            em.red("vector", A("mpos"), inner3, "max", absval=True)
            em.mark("vector")
            em.join("tensor", "vector")
            em.join("tensor", "scalar")
            for tg in range(QG):
                em.trans(pf(0, tg * 128, (tg + 1) * 128),
                         A("mpos")[:, tg * 128:(tg + 1) * 128], A("identf32"))
            em.mark("tensor")
            em.join("vector", "tensor")
            for tg in range(QG):
                em.red("vector", SC("rmax", tg), pf(0, tg * 128, (tg + 1) * 128), "max")
            for tg in range(QG):
                em.ts("vector", SC("rmax", tg), SC("rmax", tg), A("mcols")[:, 2:3], None, ALU.max)
                em.tt("vector", SC("scale2", tg), SC("rmax", tg), A("mcols")[:, 3:4], ALU.mult)
                em.recip("vector", SC("sx2", tg), SC("rmax", tg))
                em.ts("vector", SC("sx2", tg), SC("sx2", tg), 127.0)
            for tg in range(QG):
                em.copy("vector", SC("tmp", tg), SC("sx2", tg))
            em.mark("vector")
            em.join("tensor", "vector")
            for tg in range(QG):
                em.trans(pf(1, tg * 128, (tg + 1) * 128, 0, 1), SC("tmp", tg), A("identf32"))
            em.mark("tensor")
            em.join("vector", "tensor")
            em.copy("vector", A("sxrow")[0:1, 0:512], pf(1, 0, 512, 0, 1))
            em.mark("vector")
            em.join("tensor", "vector")
            em.mm(pf(2), A("rowpack")[0:1, 2048:2176], A("sxrow")[0:1, 0:512],
                  start=True, stop=True)
            em.mark("tensor")
            em.join("vector", "tensor")
            em.copy("vector", A("sxbc"), pf(2))
            em.mark("vector")

            ck()
            em.join("gpsimd", "vector")
            q2_marks = {}
            for kk in range(32):
                segi = seg0 + DN0 + kk // 8
                if kk % 8 == 0:
                    seg_wait_pe(segi)
                sl = kk % 2
                em.join("gpsimd", "vector")
                if kk >= 2:
                    em.wait("gpsimd", "ms_tensor", q2_marks[kk - 2])
                em.tt("gpsimd", A("sg")[:, sl * 512:(sl + 1) * 512],
                      A("inner")[:, kk * T:(kk + 1) * T], A("sxbc"), ALU.mult)
                em.mark("gpsimd")
                em.join("vector", "gpsimd")
                if kk >= 2:
                    em.wait("vector", "ms_tensor", q2_marks[kk - 2])
                em.ts("vector", A("q2i"), A("sg")[:, sl * 512:(sl + 1) * 512],
                      A("mcols")[:, 7:8], None, ALU.add)
                em.ts("vector", A("q2ch")[:, sl * 512:(sl + 1) * 512], A("q2i"), -256.0, None,
                      ALU.add)
                em.mark("vector")
                em.join("tensor", "vector")
                for tg in range(QG):
                    lh = A("q2ch")[:, sl * 512 + tg * 128:sl * 512 + (tg + 1) * 128]
                    em.mm(pf(2 * tg), lh, seg_ap(segi, (kk % 8) * 1024, (kk % 8) * 1024 + 512),
                          start=(kk == 0), stop=(kk == 31))
                    em.mm(pf(2 * tg + 1), lh,
                          seg_ap(segi, (kk % 8) * 1024 + 512, (kk % 8 + 1) * 1024),
                          start=(kk == 0), stop=(kk == 31))
                em.mark("tensor")
                q2_marks[kk] = em.cnt("ms_tensor")
                if kk % 8 == 7:
                    seg_done(segi)

            ck()
            em.join("vector", "tensor")
            l2m = {}
            for tg in range(QG):
                y2 = A("y2")[:, tg * 1024:(tg + 1) * 1024]
                em.stt("vector", y2[:, 0:512], pf(2 * tg), SC("scale2", tg),
                       A("y1n")[:, tg * 1024:tg * 1024 + 512], ALU.mult, ALU.add)
                em.stt("vector", y2[:, 512:1024], pf(2 * tg + 1), SC("scale2", tg),
                       A("y1n")[:, tg * 1024 + 512:(tg + 1) * 1024], ALU.mult, ALU.add)
                em.red("vector", SC("negmean", tg), y2, "sum", neg=True)
                em.ts("vector", SC("negmean", tg), SC("negmean", tg), 1.0 / 1024.0)
                em.mark("vector")
                l2m[tg] = em.cnt("ms_vector")
            if x_loaded[0]:
                em.join("vector", "gpsimd")
                em.wait("vector", "s_x", em.cnt("s_x"))
                cast_mh[0] = {}
                for c in range((QG + 3) // 4):
                    lo, hi = 4 * c * 1024, min(4 * c + 4, QG) * 1024
                    em.copy("vector", A("xbTM")[:, lo:hi], A("x")[:, lo:hi])
                    em.mark("vector")
                    cast_mh[0][c] = em.cnt("ms_vector")
                x_casted[0] = True
            em.wait("scalar", "s_out1", em.cnt("s_out1"))
            for tg in range(QG):
                y2 = A("y2")[:, tg * 1024:(tg + 1) * 1024]
                em.wait("scalar", "ms_vector", l2m[tg])
                em.act(A("sqscr"), y2, AF.Square, bias=SC("negmean", tg), accum=SC("varp", tg))
                em.mark("scalar")
                l2m[("s", tg)] = em.cnt("ms_scalar")
            for tg in range(QG):
                em.wait("vector", "ms_scalar", l2m[("s", tg)])
                em.ts("vector", SC("varp", tg), SC("varp", tg), 1.0 / 1024.0, 1e-5,
                      ALU.mult, ALU.add)
            em.mark("vector")
            if dump == "z":
                for tg in range(QG):
                    em.join("scalar", "vector")
                    em.dma("scalar", out_rows(b, t, tg),
                           A("y2")[:, tg * 1024:(tg + 1) * 1024], "s_out0")
                em.mark("scalar")
                em.join("vector", "scalar")
            em.join("scalar", "vector")
            prev_stats[0] = (b, t)

    except StopEmission:
        pass
    if stop is not None:
        em.wait("sync", "s_out0", em.cnt("s_out0"))
        em.wait("sync", "s_out1", em.cnt("s_out1"))
        with contextlib.ExitStack() as bes:
            sems = {}
            for name in em.counts:
                sems[name] = bes.enter_context(nc.semaphore(name))
            block = bes.enter_context(nc.Block())
            em.replay(block, sems)
        return nc
    em.table("natural_log_exp_and_others")
    pb_, pt_ = prev_stats[0]
    em.join("vector", "gpsimd")
    for tg in range(QG):
        em.act(SC("rstd", tg), SC("varp", tg), AF.Ln)
        em.act(SC("rstd", tg), SC("rstd", tg), AF.Exp, scale=-0.5)
    em.mark("scalar")
    em.join("vector", "scalar")
    outdma = {}
    for tg in range(QG):
        ob = A("outst") if tg % 2 == 0 else A("outst2")
        sem = "s_out%d" % (tg % 2)
        if tg >= 2:
            em.wait("vector", sem, outdma[tg - 2])
        em.ts("vector", ob, A("y2")[:, tg * 1024:(tg + 1) * 1024],
              SC("negmean", tg), SC("rstd", tg), ALU.add, ALU.mult)
        em.mark("vector")
        em.join("scalar", "vector")
        if dump is None:
            em.dma("scalar", out_rows(pb_, pt_, tg), ob, sem)
        outdma[tg] = em.cnt(sem)

    em.wait("sync", "s_out0", em.cnt("s_out0"))
    em.wait("sync", "s_out1", em.cnt("s_out1"))

    with contextlib.ExitStack() as bes:
        sems = {}
        for name in em.counts:
            sems[name] = bes.enter_context(nc.semaphore(name))
        block = bes.enter_context(nc.Block())
        em.replay(block, sems)

    return nc



def kernel(**inputs):
    cfg = make_cfg()
    n_cores = 8
    nc = build_program(cfg)
    in_maps = [prep_core_inputs(ci, n_cores, cfg, inputs) for ci in range(n_cores)]
    res = run_bass_kernel_spmd(nc, in_maps, core_ids=list(range(n_cores)))
    out = np.concatenate([np.asarray(r["yout"]) for r in res.results], axis=0)
    return out.astype(np.float32)


# version 13
# speedup vs baseline: 1.0175x; 1.0012x over previous
import contextlib
import numpy as np
import ml_dtypes
import concourse.bass as bass
import concourse.mybir as mybir
from concourse.bass_utils import run_bass_kernel_spmd

F32 = mybir.dt.float32
BF16 = mybir.dt.bfloat16
I32 = mybir.dt.int32
AF = mybir.ActivationFunctionType
ALU = mybir.AluOpType
AX = mybir.AxisListType
BF = ml_dtypes.bfloat16

D = 1024
H = 16
HD = 64
LH = 4
LHD = 64
LD = 256
CH = 32
INNER = 4096
WIN = 128

SEG_EL = 8192
RING_SLOTS = 2


def make_cfg(B=2, S=2048):
    T = 512 if S >= 512 else S
    return dict(B=B, S=S, T=T, NT=S // T, QG=T // 128, NLAT=S // CH, STI=S // 128)



def build_consts(cfg):
    T, STI, NLAT = cfg["T"], cfg["STI"], cfg["NLAT"]
    c = {}
    c["ident"] = np.eye(128, dtype=BF)
    c["identf32"] = np.eye(128, dtype=np.float32)
    qi = np.arange(128)[:, None]
    kj = np.arange(256)[None, :]
    binB = ((kj >= qi) & (kj <= qi + 127)).astype(BF)
    binA = ((kj >= np.maximum(qi, 128)) & (kj <= qi + 127)).astype(BF)
    c["binB"] = binB
    c["binA"] = binA
    lat = np.zeros((128, STI * NLAT), dtype=BF)
    for sti in range(STI):
        p = np.arange(128)[:, None]
        l = np.arange(NLAT)[None, :]
        lat[:, sti * NLAT:(sti + 1) * NLAT] = (l < (sti * 4 + p // 32)).astype(BF)
    c["latbin"] = lat
    P4 = np.zeros((128, 4), dtype=np.float32)
    P4[np.arange(128), np.arange(128) // 32] = 1.0 / 32.0
    c["P4"] = P4.astype(BF)
    sel = np.zeros((16, 8 * 128), dtype=BF)
    for ci in range(8):
        for p in range(128):
            sel[2 * ci + p // 64, ci * 128 + p] = 1.0
    c["sel"] = sel
    return c



def _pack_ws(w, och_count, k_count):
    d_in, d_out = w.shape
    assert d_in == k_count * 128 and d_out == och_count * 128
    t = w.reshape(k_count, 128, och_count, 128)
    t = np.transpose(t, (1, 2, 0, 3))
    return np.ascontiguousarray(t.reshape(128, och_count * k_count * 128))


def _pack_rhs(w, k_count):
    d_in, d_out = w.shape
    assert d_in == k_count * 128
    t = w.reshape(k_count, 128, d_out)
    t = np.transpose(t, (1, 0, 2))
    return np.ascontiguousarray(t.reshape(128, k_count * d_out))


def prep_core_inputs(ci, n_cores, cfg, inp):
    B, S = cfg["B"], cfg["S"]
    x = np.ascontiguousarray(inp["L0_attn_x"][ci * B:(ci + 1) * B])
    qkv_w = np.asarray(inp["L0_attn_qkv_w"])
    m = {}
    m["xin"] = x
    m["tq"] = _pack_ws(qkv_w[:, 0:1024], 8, 8)
    m["tk"] = _pack_ws(qkv_w[:, 1024:2048], 8, 8)
    m["tv"] = _pack_rhs(qkv_w[:, 2048:3072], 8)
    m["trq"] = _pack_ws(np.asarray(inp["L0_attn_rq_w"]), 2, 8)
    m["trout"] = _pack_ws(np.asarray(inp["L0_attn_rout_w"]), 8, 2)
    m["tow"] = _pack_rhs(np.asarray(inp["L0_attn_out_w"]), 8)
    m["trkv"] = _pack_ws(np.asarray(inp["L0_attn_rkv_w"]), 4, 8)
    wg = np.ascontiguousarray(np.asarray(inp["L0_ffn_w_gate"]).T)
    wu = np.ascontiguousarray(np.asarray(inp["L0_ffn_w_up"]).T)
    wd = np.ascontiguousarray(np.asarray(inp["L0_ffn_w_down"]).T)
    gw = wg.reshape(8, 128, 32, 128)
    uw = wu.reshape(8, 128, 32, 128)
    gu = np.empty((128, 64 * 8 * 128), dtype=np.float32)
    for og in range(64):
        src = gw if og % 2 == 0 else uw
        oc = og // 2
        blk = np.transpose(src[:, :, oc, :], (1, 0, 2)).reshape(128, 8 * 128)
        gu[:, og * 1024:(og + 1) * 1024] = blk
    m["tgu"] = np.ascontiguousarray(gu)
    m["tdn"] = _pack_rhs(wd, 32)
    qkv_b = np.asarray(inp["L0_attn_qkv_b"])
    m["bcol_qk"] = np.ascontiguousarray(qkv_b[0:2048].reshape(16, 128).T)
    m["bcol_rq"] = np.ascontiguousarray(np.asarray(inp["L0_attn_rq_b"]).reshape(2, 128).T)
    m["bcol_rkv"] = np.ascontiguousarray(np.asarray(inp["L0_attn_rkv_b"]).reshape(4, 128).T)
    m["bcol_rout"] = np.ascontiguousarray(np.asarray(inp["L0_attn_rout_b"]).reshape(8, 128).T)
    m["brow_v"] = np.ascontiguousarray(qkv_b[2048:3072].reshape(1, 1024))
    m["brow_out"] = np.ascontiguousarray(np.asarray(inp["L0_attn_out_b"]).reshape(1, 1024))
    m["gl"] = np.ascontiguousarray(np.asarray(inp["L0_attn_gl_w"]).reshape(64, 1))
    m["gr"] = np.ascontiguousarray(np.asarray(inp["L0_attn_gr_w"]).reshape(64, 1))
    return m



ENGS = ("sync", "tensor", "vector", "scalar", "gpsimd")


class Em:
    def __init__(self):
        self.q = {e: [] for e in ENGS}
        self.written = {e: set() for e in ENGS}
        self.readden = {e: set() for e in ENGS}
        self.counts = {}
        self.ranges = {}

    def _overlap(self, s1, s2):
        for a in s1:
            for b in s2:
                if a == b:
                    return True
                ra, rb = self.ranges.get(a), self.ranges.get(b)
                if ra and rb and ra[0] < rb[1] and rb[0] < ra[1]:
                    return True
        return False

    def _names(self, aps):
        out = set()
        for a in aps:
            if a is None or isinstance(a, (int, float)):
                continue
            out.add(a.tensor.name)
        return out

    def op(self, eng, fn, reads=(), writes=()):
        if getattr(self, "frozen", False):
            return
        if eng in ("vector", "scalar", "gpsimd"):
            rd = self._names(reads)
            wr = self._names(writes)
            if (self._overlap(wr, self.written[eng] | self.readden[eng])
                    or self._overlap(rd, self.written[eng])):
                self.q[eng].append(("drain",))
                self.written[eng] = set()
                self.readden[eng] = set()
            self.written[eng] |= wr
            self.readden[eng] |= rd
        self.q[eng].append(("op", fn))

    def inc(self, eng, sem, n=1):
        if getattr(self, "frozen", False):
            return self.counts.get(sem, 0)
        self.counts[sem] = self.counts.get(sem, 0) + n
        q = self.q[eng]
        i = len(q) - 1
        while i >= 0 and q[i][0] == "wait":
            i -= 1
        if i >= 0 and q[i][0] == "inc":
            q.append(("op", lambda e: e.nop(nofuse=True)))
        q.append(("inc", sem, n))
        return self.counts[sem]

    def wait(self, eng, sem, n=None):
        if getattr(self, "frozen", False):
            return
        if n is None:
            n = self.counts.get(sem, 0)
        if n > 0:
            self.q[eng].append(("wait", sem, n))

    def mark(self, eng):
        q = self.q[eng]
        i = len(q) - 1
        while i >= 0 and q[i][0] == "wait":
            i -= 1
        if i >= 0 and q[i][0] == "inc" and q[i][1] == "ms_" + eng:
            return self.counts["ms_" + eng]
        return self.inc(eng, "ms_" + eng, 1)

    def join(self, eng, src):
        self.wait(eng, "ms_" + src)

    def cnt(self, sem):
        return self.counts.get(sem, 0)

    def tt(self, eng, out, a, b, aop):
        self.op(eng, lambda e: e.tensor_tensor(out, a, b, aop), reads=[a, b], writes=[out])

    def ts(self, eng, out, a, s1, s2=None, op0=ALU.mult, op1=ALU.bypass):
        rd = [a] + [x for x in (s1, s2) if hasattr(x, "tensor")]
        self.op(eng, lambda e: e.tensor_scalar(out, a, s1, s2, op0, op1), reads=rd, writes=[out])

    def stt(self, eng, out, in0, scalar, in1, op0, op1):
        rd = [in0, in1] + ([scalar] if hasattr(scalar, "tensor") else [])
        self.op(eng, lambda e: e.scalar_tensor_tensor(out, in0, scalar, in1, op0, op1),
                reads=rd, writes=[out])

    def copy(self, eng, out, a):
        self.op(eng, lambda e: e.tensor_copy(out, a), reads=[a], writes=[out])

    def memset(self, eng, ap, v):
        self.op(eng, lambda e: e.memset(ap, v), reads=[], writes=[ap])

    def red(self, eng, out, a, rop, absval=False, neg=False):
        if rop == "max":
            self.op(eng, lambda e: e.reduce_max(out, a, AX.X, apply_absolute_value=absval,
                                                negate=neg), reads=[a], writes=[out])
        else:
            self.op(eng, lambda e: e.reduce_sum(out, a, AX.X, apply_absolute_value=absval,
                                                negate=neg), reads=[a], writes=[out])

    def recip(self, eng, out, a):
        self.op(eng, lambda e: e.reciprocal(out, a), reads=[a], writes=[out])

    def act(self, out, a, func, bias=0.0, scale=1.0, accum=None):
        rd = [a] + [x for x in (bias, scale) if hasattr(x, "tensor")]
        wr = [out] + ([accum] if accum is not None else [])
        self.op("scalar", lambda e: e.activation(out, a, func, bias=bias, scale=scale,
                                                 accum_out=accum), reads=rd, writes=wr)

    def mm(self, out, lhsT, rhs, start, stop):
        self.op("tensor", lambda e: e.matmul(out, lhsT, rhs, start=start, stop=stop))

    def trans(self, out, a, ident):
        self.op("tensor", lambda e: e.transpose(out, a, ident))

    def dma(self, eng, out, a, sem):
        self.op(eng, lambda e: e.dma_start(out=out, in_=a), reads=[a], writes=[out])
        return self.inc(eng, sem, 16)

    def table(self, set_name):
        if getattr(self, "frozen", False):
            return
        def fn(e):
            from concourse.hw_specs import get_activation_tables
            for idx, (nm, funcs) in enumerate(get_activation_tables(e.bass.m.arch).items()):
                if nm == set_name:
                    return e.add_instruction(mybir.InstLoadActFuncSet(
                        name=e.bass.get_next_instruction_name(), act_func_set_id=idx,
                        ins=[], outs=[]))
            raise ValueError(set_name)
        self.q["scalar"].append(("op", fn))

    def replay(self, block, sems):
        def section(engname):
            def body(engobj):
                last = None
                for item in self.q[engname]:
                    k = item[0]
                    if k == "op":
                        last = item[1](engobj)
                    elif k == "inc":
                        assert last is not None, f"inc with no prior op on {engname}"
                        last.then_inc(sems[item[1]], item[2])
                    elif k == "wait":
                        engobj.wait_ge(sems[item[1]], item[2])
                    elif k == "drain":
                        engobj.drain()
            return body
        block.sync(section("sync"))
        block.tensor(section("tensor"))
        block.vector(section("vector"))
        block.scalar(section("scalar"))
        block.gpsimd(section("gpsimd"))



class StopEmission(Exception):
    pass


def build_program(cfg, stop=None, dump=None):
    B, S, T, NT, QG, NLAT, STI = (cfg["B"], cfg["S"], cfg["T"], cfg["NT"], cfg["QG"],
                                  cfg["NLAT"], cfg["STI"])
    NSEQT = S // 128
    consts = build_consts(cfg)

    nc = bass.Bass("TRN2", target_bir_lowering=False)

    xin = nc.dram_tensor("xin", [B, S, D], F32, kind="ExternalInput").ap()
    yout = nc.dram_tensor("yout", [B, S, D], F32, kind="ExternalOutput").ap()
    hv = {}
    for nm, shape in [("tq", [128, 8192]), ("tk", [128, 8192]), ("tv", [128, 8192]),
                      ("trq", [128, 2048]), ("trout", [128, 2048]), ("tow", [128, 8192]),
                      ("trkv", [128, 4096]), ("tgu", [128, 65536]), ("tdn", [128, 32768]),
                      ("bcol_qk", [128, 16]), ("bcol_rq", [128, 2]), ("bcol_rkv", [128, 4]),
                      ("bcol_rout", [128, 8]), ("brow_v", [1, 1024]), ("brow_out", [1, 1024]),
                      ("gl", [64, 1]), ("gr", [64, 1])]:
        hv[nm] = nc.dram_tensor(nm, shape, F32, kind="ExternalInput").ap()
    cin = {k: nc.inline_tensor(v, "c_" + k).ap() for k, v in consts.items()}
    NSEG_TILE = 5 + 8 + 4
    tape = nc.dram_tensor("tape", [128, NSEG_TILE * SEG_EL], BF16, kind="Internal").ap()

    es = contextlib.ExitStack()
    sb = {}

    def alloc(name, shape, dtype):
        sb[name] = nc.alloc_sbuf_tensor(name, shape, dtype)
        return sb[name]

    alloc("x", [128, 4096], F32)
    alloc("y1n", [128, 4096], F32)
    alloc("y2", [128, 4096], F32)
    alloc("xqT", [128, 8 * T], BF16)
    alloc("kcarry", [128, 1024], BF16)
    alloc("vcarry", [128, 1024], BF16)
    alloc("ring", [128, RING_SLOTS * SEG_EL], BF16)
    alloc("rowpack", [128, 2176], F32)
    alloc("ident", [128, 128], BF16)
    alloc("identf32", [128, 128], F32)
    alloc("binAB", [128, 512], BF16)
    alloc("latbin", [128, STI * NLAT], BF16)
    alloc("P4", [128, 4], BF16)
    alloc("selc", [16, 1024], BF16)
    alloc("glgr", [128, 256], BF16)
    alloc("gl2", [128, 2], F32)
    alloc("rkres", [128, B * 2 * LHD], BF16)
    alloc("rvres", [64, B * LH * LHD], BF16)
    alloc("bcols", [128, 64], F32)
    alloc("stats", [128, 64], F32)
    alloc("mcols", [128, 8], F32)
    alloc("sc1", [1, 16], F32)
    alloc("outst", [128, 1024], F32)
    alloc("onescol", [128, 1], F32)

    AR = {}
    em = Em()
    ar_base = [None]

    def ar(name, shape, dtype, off):
        t = nc.alloc_sbuf_tensor_at(name, shape, dtype, offset=ar_base[0] + off)
        sb[name] = t
        esz = {F32: 4, BF16: 2, I32: 4}[dtype]
        em.ranges[t.name] = (off, off + shape[1] * esz)
        return t


    T2 = 2 * T
    T8 = 8 * T
    ARENA_BYTES = 0x17A00
    ar_base[0] = (nc.sbuf_base + 63) // 64 * 64
    assert nc.sbuf_top - ar_base[0] >= ARENA_BYTES, (
        f"SBUF over budget: base {nc.sbuf_base} top {nc.sbuf_top}")

    ar("xbTM", [128, 4096], BF16, 0x0000)
    ar("xhT", [128, T8], BF16, 0x2000)
    ar("qT", [128, T8], BF16, 0x4000)
    ar("kT", [128, T8], BF16, 0x6000)
    ar("vT", [128, 4096], BF16, 0x8000)
    ar("rqT", [128, T2], BF16, 0xA000)
    ar("probsB", [128, 2048], BF16, 0xA800)
    ar("probsTB", [128, 2048], BF16, 0xB800)
    ar("localT", [128, T8], BF16, 0xC800)
    ar("latoT", [128, T2], BF16, 0xE800)
    ar("latpr", [128, 256], BF16, 0xF000)
    ar("lattnT", [64, 512], BF16, 0xF200)
    ar("remT", [128, T8], BF16, 0xF600)
    ar("mixT", [128, T8], BF16, 0x11600)
    ar("dtmp", [128, 512], F32, 0x13600)
    ar("gatesf", [128, 512], F32, 0x13E00)
    ar("gatesb", [128, 512], BF16, 0x14600)
    ar("aq1i", [128, 1024], I32, 0x10000)
    ar("xqTM", [128, 4096], BF16, 0x15A00)
    ar("inner", [128, 32 * T], F32, 0x0000)
    ar("sg", [128, 1024], F32, 0x10000)
    ar("q2i", [128, 512], I32, 0x11000)
    ar("q2ch", [128, 1024], BF16, 0x11800)
    ar("mpos", [128, 512], F32, 0x12000)
    ar("mneg", [128, 512], F32, 0x12800)
    ar("sxbc", [128, 512], F32, 0x13000)
    ar("sxrow", [1, 512], F32, 0x13800)
    ar("sqscr", [128, 1024], F32, 0x14000)
    ar("outst2", [128, 1024], F32, 0x16A00)
    ar("wslab", [128, 8192], F32, 0x0000)
    ar("oslab", [128, 8192], BF16, 0x8000)
    ar("islab", [128, 4096], I32, 0xC000)
    ar("xbP", [128, 4096], BF16, 0x10000)
    ar("pooledP", [128, 512], BF16, 0x12000)
    ar("acc", [128, 8], F32, 0x12400)
    ar("abscr", [128, 1024], F32, 0x12500)
    ar("rvtmp", [128, 128], BF16, 0x13600)
    ar("w1slab", [128, 4096], F32, 0x13800)

    ps_l = [nc.alloc_psum_tensor("ps%d" % i, [128, 512], F32) for i in range(8)]
    psb_l = [tt.bitcast(BF16) for tt in ps_l]

    def pf(i, lo=0, hi=512, p0=0, p1=128):
        return ps_l[i].ap()[p0:p1, lo:hi]

    def pb(i, lo=0, hi=1024, p0=0, p1=128):
        return psb_l[i].ap()[p0:p1, lo:hi]

    A = lambda n: sb[n].ap()
    stage_no = [0]

    def ck():
        stage_no[0] += 1
        if stop is not None and stage_no[0] >= stop:
            em.frozen = True

    ones_row = lambda w: A("rowpack")[0:1, 2048:2048 + w]

    ring_state = dict(next_load=0, loaded_off={}, done_at={}, load_sem={})

    def seg_load(idx):
        slot = idx % RING_SLOTS
        prev = idx - RING_SLOTS
        if prev >= 0:
            em.wait("sync", "ms_tensor", ring_state["done_at"][prev])
        for sem, cntv in ring_state["tape_w"].get(idx % NSEG_TILE, []):
            em.wait("sync", sem, cntv)
        w = 4096 if (idx % NSEG_TILE) == 3 else SEG_EL
        dst = A("ring")[:, slot * SEG_EL:slot * SEG_EL + w]
        src = tape[:, (idx % NSEG_TILE) * SEG_EL:(idx % NSEG_TILE) * SEG_EL + w]
        n = em.dma("sync", dst, src, "s_tape%d" % slot)
        ring_state["load_sem"][idx] = ("s_tape%d" % slot, n)
        ring_state["loaded_off"][idx] = slot * SEG_EL

    def seg_ap(idx, lo, hi):
        off = ring_state["loaded_off"][idx]
        return A("ring")[:, off + lo:off + hi]

    def seg_wait_pe(idx):
        sem, n = ring_state["load_sem"][idx]
        em.wait("tensor", sem, n)

    def seg_done(idx):
        em.mark("tensor")
        ring_state["done_at"][idx] = em.cnt("ms_tensor")
        nl = ring_state["next_load"]
        if nl == idx + RING_SLOTS:
            pass
        want = idx + RING_SLOTS
        if want < ring_state["total"]:
            seg_load(want)
            ring_state["next_load"] = want + 1

    em.dma("sync", A("ident"), cin["ident"], "s_c")
    em.dma("sync", A("identf32"), cin["identf32"], "s_c")
    em.dma("sync", A("binAB")[:, 0:256], cin["binA"], "s_c")
    em.dma("sync", A("binAB")[:, 256:512], cin["binB"], "s_c")
    em.dma("sync", A("latbin"), cin["latbin"], "s_c")
    em.dma("sync", A("P4"), cin["P4"], "s_c")
    em.dma("sync", A("selc")[0:16, 0:1024], cin["sel"], "s_c")
    em.dma("sync", A("bcols")[:, 0:16], hv["bcol_qk"], "s_c")
    em.dma("sync", A("bcols")[:, 16:18], hv["bcol_rq"], "s_c")
    em.dma("sync", A("bcols")[:, 18:22], hv["bcol_rkv"], "s_c")
    em.dma("sync", A("bcols")[:, 22:30], hv["bcol_rout"], "s_c")
    em.dma("sync", A("rowpack")[0:1, 0:1024], hv["brow_v"], "s_c")
    em.dma("sync", A("rowpack")[0:1, 1024:2048], hv["brow_out"], "s_c")
    em.dma("sync", A("gl2")[0:64, 0:1], hv["gl"], "s_c")
    em.dma("sync", A("gl2")[64:128, 0:1], hv["gl"], "s_c")
    em.dma("sync", A("gl2")[0:64, 1:2], hv["gr"], "s_c")
    em.dma("sync", A("gl2")[64:128, 1:2], hv["gr"], "s_c")
    n_c = em.cnt("s_c")

    ck()
    em.table("natural_log_exp_and_others")

    em.wait("vector", "s_c", n_c)
    em.memset("vector", A("rowpack")[0:1, 2048:2176], 1.0)
    em.memset("vector", A("onescol"), 1.0)
    em.memset("vector", A("stats")[:, 60:61], 1e-5)
    em.memset("vector", A("stats")[:, 62:63], 0.2)
    em.ts("vector", A("aq1i")[:, 0:1], A("stats")[:, 62:63], 1.0, 256.5, ALU.mult, ALU.add)
    em.ts("vector", A("stats")[:, 63:64], A("aq1i")[:, 0:1], -256.0, None, ALU.add)
    em.ts("vector", A("mcols")[:, 7:8], A("stats")[:, 63:64], -0.5, 256.5, ALU.mult, ALU.add)
    em.ts("vector", A("bcols")[:, 32:40], A("bcols")[:, 0:8], 0.125)
    em.ts("vector", A("bcols")[:, 40:42], A("bcols")[:, 16:18], 0.125)
    em.memset("vector", A("glgr"), 0.0)
    for cidx in range(8):
        em.copy("vector", A("glgr")[0:64, cidx * 16 + 2 * cidx:cidx * 16 + 2 * cidx + 1],
                A("gl2")[0:64, 0:1])
        em.copy("vector", A("glgr")[64:128, cidx * 16 + 2 * cidx + 1:cidx * 16 + 2 * cidx + 2],
                A("gl2")[64:128, 0:1])
        em.copy("vector", A("glgr")[0:64, 128 + cidx * 16 + 2 * cidx:128 + cidx * 16 + 2 * cidx + 1],
                A("gl2")[0:64, 1:2])
        em.copy("vector",
                A("glgr")[64:128, 128 + cidx * 16 + 2 * cidx + 1:128 + cidx * 16 + 2 * cidx + 2],
                A("gl2")[64:128, 1:2])
    em.mark("vector")

    ck()
    ring_state["tape_w"] = {}
    SEGQ, SEGK, SEGV, SEGRQRO, SEGOW = 0, 1, 2, 3, 4
    tape_steps = []
    for seg, nm, dsto, tot in [(SEGQ, "tq", 0, 8192), (SEGK, "tk", 0, 8192),
                               (SEGV, "tv", 0, 8192), (SEGRQRO, "trq", 0, 2048),
                               (SEGRQRO, "trout", 2048, 2048), (SEGOW, "tow", 0, 8192)]:
        for half in range(0, tot, 4096):
            tape_steps.append((seg, nm, dsto, half, min(4096, tot - half)))
    cast_m = {}
    tw_m = {}

    def tape_step(si):
        seg, nm, dsto, half, w = tape_steps[si]
        sl = si % 2
        wap = A("wslab")[:, sl * 4096:sl * 4096 + w]
        oap = A("oslab")[:, sl * 4096:sl * 4096 + w]
        if si >= 2:
            em.wait("sync", "ms_vector", cast_m[si - 2])
        nl = em.dma("sync", wap, hv[nm][:, half:half + w], "s_wp%d" % sl)
        em.wait("vector", "s_wp%d" % sl, nl)
        if si >= 2:
            em.wait("vector", "s_tw%d" % sl, tw_m[si - 2])
        em.copy("vector", oap, wap)
        em.mark("vector")
        cast_m[si] = em.cnt("ms_vector")
        em.wait("sync", "ms_vector", cast_m[si])
        em.dma("sync", tape[:, seg * SEG_EL + dsto + half:seg * SEG_EL + dsto + half + w],
               oap, "s_tw%d" % sl)
        tw_m[si] = em.cnt("s_tw%d" % sl)
        ring_state["tape_w"].setdefault(seg, []).append(("s_tw%d" % sl, tw_m[si]))

    em.memset("vector", A("acc")[:, 0:8], 0.0)
    em.memset("vector", A("rvtmp"), 0.0)
    em.mark("vector")
    p1_steps = []
    for nm, nsl in (("tgu", 32), ("tdn", 16)):
        for sidx in range(nsl):
            p1_steps.append((nm, sidx))
    abs_m = {}
    addm = {}

    def p1_step(si):
        nm, sidx = p1_steps[si]
        sl = si % 2
        wap = A("w1slab")[:, sl * 2048:(sl + 1) * 2048]
        if si >= 2:
            em.wait("gpsimd", "ms_scalar", abs_m[si - 2])
        nl = em.dma("gpsimd", wap, hv[nm][:, sidx * 2048:(sidx + 1) * 2048], "s_gp%d" % sl)
        em.wait("scalar", "s_gp%d" % sl, nl)
        if si >= 1:
            em.wait("scalar", "ms_vector", addm.get(si - 1, 0))
        for j in range(2):
            em.act(A("abscr"), wap[:, j * 1024:(j + 1) * 1024], AF.Abs,
                   accum=A("stats")[:, 56 + 2 * sl + j:57 + 2 * sl + j])
        em.mark("scalar")
        abs_m[si] = em.cnt("ms_scalar")
        em.wait("vector", "ms_scalar", abs_m[si])
        for j in range(2):
            if nm == "tgu":
                og = sidx * 2 + j
                colidx = 0 if og % 2 == 0 else 1
            else:
                colidx = 6
            em.tt("vector", A("acc")[:, colidx:colidx + 1], A("acc")[:, colidx:colidx + 1],
                  A("stats")[:, 56 + 2 * sl + j:57 + 2 * sl + j], ALU.add)
        em.mark("vector")
        addm[si] = em.cnt("ms_vector")

    for si in range(len(p1_steps)):
        if si < len(tape_steps):
            tape_step(si)
        p1_step(si)
    em.mark("vector")

    em.join("tensor", "vector")
    for i, colidx in enumerate((0, 1, 6)):
        em.mm(pf(5, i, i + 1, 0, 1), A("acc")[:, colidx:colidx + 1],
              A("onescol"), start=True, stop=True)
    em.mark("tensor")
    em.join("vector", "tensor")
    NW = 4096 * 1024
    em.ts("vector", A("sc1")[0:1, 0:3], pf(5, 0, 3, 0, 1), 1.0 / NW, 1e-5, ALU.mult, ALU.max)
    em.recip("vector", A("sc1")[0:1, 4:7], A("sc1")[0:1, 0:3])
    em.ts("vector", A("sc1")[0:1, 8:9], A("sc1")[0:1, 5:6], 1e-5)
    em.tt("vector", A("sc1")[0:1, 9:10], A("sc1")[0:1, 2:3], A("sc1")[0:1, 1:2], ALU.mult)
    em.ts("vector", A("sc1")[0:1, 9:10], A("sc1")[0:1, 9:10], 1.0 / 127.0)
    em.mark("vector")
    em.join("tensor", "vector")
    bc_list = [(0, 0), (1, 1), (2, 8), (3, 9), (4, 4), (5, 5), (6, 6)]
    for mcol, scol in bc_list:
        em.mm(pf(6, mcol, mcol + 1), A("rowpack")[0:1, 2048:2176],
              A("sc1")[0:1, scol:scol + 1], start=True, stop=True)
    em.mark("tensor")
    em.join("vector", "tensor")
    em.copy("vector", A("mcols")[:, 0:7], pf(6, 0, 7))
    em.mark("vector")

    GU0 = 5
    DN0 = 13
    q2_m = {}
    wq_m = {}

    def p2_step(si):
        nm, sidx = p1_steps[si]
        sl = si % 2
        wap = A("wslab")[:, sl * 2048:(sl + 1) * 2048]
        oap = A("oslab")[:, sl * 2048:(sl + 1) * 2048]
        if si >= 2:
            em.wait("gpsimd", "ms_vector", q2_m[si - 2])
        else:
            em.wait("gpsimd", "ms_vector", em.cnt("ms_vector"))
            em.wait("gpsimd", "s_gp%d" % sl, em.cnt("s_gp%d" % sl))
        nl = em.dma("gpsimd", wap, hv[nm][:, sidx * 2048:(sidx + 1) * 2048], "s_g2p%d" % sl)
        em.wait("vector", "s_g2p%d" % sl, nl)
        if si >= 2:
            em.wait("vector", "s_twg%d" % sl, wq_m[si - 2])
        else:
            em.wait("vector", "s_tw0", em.cnt("s_tw0"))
            em.wait("vector", "s_tw1", em.cnt("s_tw1"))
        for j in range(2):
            if nm == "tgu":
                og = sidx * 2 + j
                sw = A("mcols")[:, 4:5] if og % 2 == 0 else A("mcols")[:, 5:6]
            else:
                sw = A("mcols")[:, 6:7]
            blk = wap[:, j * 1024:(j + 1) * 1024]
            fb = sb["islab"].bitcast(F32).ap()[:, (sl * 2 + j) * 1024:(sl * 2 + j + 1) * 1024]
            em.ts("vector", blk, blk, sw, None, ALU.mult)
            em.ts("vector", fb, blk, 0.5, None, ALU.is_gt)
            em.ts("vector", blk, blk, -0.5, None, ALU.is_lt)
            em.tt("vector", oap[:, j * 1024:(j + 1) * 1024], fb, blk, ALU.subtract)
        em.mark("vector")
        q2_m[si] = em.cnt("ms_vector")
        em.wait("gpsimd", "ms_vector", q2_m[si])
        if nm == "tgu":
            seg = GU0 + sidx // 4
            off = (sidx % 4) * 2048
        else:
            seg = DN0 + sidx // 4
            off = (sidx % 4) * 2048
        em.dma("gpsimd", tape[:, seg * SEG_EL + off:seg * SEG_EL + off + 2048], oap,
               "s_twg%d" % sl)
        wq_m[si] = em.cnt("s_twg%d" % sl)
        ring_state["tape_w"].setdefault(seg, []).append(("s_twg%d" % sl, wq_m[si]))

    rkvb = sb["w1slab"].bitcast(BF16).ap()
    em.wait("gpsimd", "ms_scalar", em.cnt("ms_scalar"))
    nl = em.dma("gpsimd", rkvb[:, 0:4096], hv["trkv"], "s_gp0")
    rkv_ld = nl
    p2i = [0]

    def pump2(n=2):
        for _ in range(n):
            if p2i[0] < len(p1_steps):
                p2_step(p2i[0])
                p2i[0] += 1

    for b in range(B):
        ck()
        pump2()
        ngr = (S // 128 + 3) // 4
        for grp in range(ngr):
            t0 = grp * 4
            ntl = min(4, S // 128 - t0)
            em.wait("gpsimd", "ms_tensor", em.cnt("ms_tensor"))
            nl = em.dma("gpsimd", A("xbP").rearrange("p (g d) -> p g d", d=1024)[:, 0:ntl],
                        xin[b][t0 * 128:(t0 + ntl) * 128, :].rearrange("(g p) d -> p g d", p=128),
                        "s_xp")
            em.wait("tensor", "s_xp", nl)
            for tl in range(ntl):
                tg = t0 + tl
                for kk in range(8):
                    em.mm(pf(kk, tg * 4, tg * 4 + 4),
                          A("xbP")[:, tl * 1024 + kk * 128:tl * 1024 + (kk + 1) * 128],
                          A("P4"), start=True, stop=True)
            em.mark("tensor")
            pump2()
        ck()
        em.join("scalar", "tensor")
        for kk in range(8):
            em.act(A("pooledP")[:, kk * NLAT:(kk + 1) * NLAT], pf(kk, 0, NLAT), AF.Identity)
        em.mark("scalar")
        ck()
        em.join("tensor", "scalar")
        em.wait("tensor", "s_gp0", rkv_ld)
        for och in range(4):
            for kk in range(8):
                em.mm(pf(och, 0, NLAT),
                      rkvb[:, (och * 8 + kk) * 128:(och * 8 + kk + 1) * 128],
                      A("pooledP")[:, kk * NLAT:(kk + 1) * NLAT], start=(kk == 0), stop=(kk == 7))
        em.mark("tensor")
        ck()
        pump2()
        em.join("scalar", "tensor")
        for och in range(2):
            em.act(A("rkres")[:, (b * 2 + och) * 64:(b * 2 + och) * 64 + NLAT],
                   pf(och, 0, NLAT), AF.Identity, bias=A("bcols")[:, 18 + och:19 + och])
        for och in range(2, 4):
            em.act(A("rvtmp")[:, 0:NLAT] if och == 2 else A("rvtmp")[:, 64:64 + NLAT],
                   pf(och, 0, NLAT), AF.Identity, bias=A("bcols")[:, 18 + och:19 + och])
        em.mark("scalar")
        ck()
        em.join("tensor", "scalar")
        for half in range(2):
            em.trans(pb(4, half * 256, half * 256 + 128, 0, 64),
                     A("rvtmp")[:, half * 64:(half + 1) * 64], A("ident"))
        em.mark("tensor")
        ck()
        em.join("scalar", "tensor")
        for half in range(2):
            em.act(A("rvres")[0:NLAT, (b * 2 + half) * 128:(b * 2 + half + 1) * 128],
                   pb(4, half * 256, half * 256 + 128, 0, NLAT), AF.Identity)
        em.mark("scalar")
        ck()
        em.join("tensor", "scalar")
        pump2()
    pump2(len(p1_steps))
    ck()
    ring_state["total"] = NSEG_TILE * B * NT
    ring_state["done_at"] = {}
    seg_load(0)
    seg_load(1)
    ring_state["next_load"] = 2

    x_loaded = [False]
    x_casted = [False]
    cast_mh = [None]
    prev_stats = [None]

    def out_rows(b, t, tg):
        return yout[b][t * T + tg * 128:t * T + (tg + 1) * 128, :]

    stc = {"nmax": 0, "rsum": 8, "rr": 16, "negmean": 24, "varp": 28, "rstd": 32,
           "rmax": 36, "sx": 40, "invsx": 44, "scale2": 48, "sx2": 52, "tmp": 56}

    def SC(name, i, w=1):
        return A("stats")[:, stc[name] + i:stc[name] + i + w]

    try:
      for b in range(B):
        for t in range(NT):
            tile_idx = b * NT + t
            seg0 = tile_idx * NSEG_TILE

            em.table("natural_log_exp_and_others")

            if not x_loaded[0]:
                em.wait("sync", "ms_gpsimd", em.cnt("ms_gpsimd"))
                nl = em.dma("sync", A("x").rearrange("p (g d) -> p g d", d=1024),
                            xin[b][t * T:(t + 1) * T, :].rearrange("(g p) d -> p g d", p=128),
                            "s_x")
            x_loaded[0] = False
            if not x_casted[0]:
                em.wait("vector", "s_x", em.cnt("s_x"))
                if tile_idx == 0:
                    for s in ("s_twg0", "s_twg1", "s_gp0", "s_gp1", "s_g2p0", "s_g2p1",
                              "s_wp0", "s_wp1", "s_xp"):
                        em.wait("vector", s, em.cnt(s))
                cast_mh[0] = {}
                for c in range((QG + 3) // 4):
                    lo, hi = 4 * c * 1024, min(4 * c + 4, QG) * 1024
                    em.copy("vector", A("xbTM")[:, lo:hi], A("x")[:, lo:hi])
                    em.mark("vector")
                    cast_mh[0][c] = em.cnt("ms_vector")
            x_casted[0] = False
            cast_m = cast_mh[0]

            if t == 0:
                em.join("gpsimd", "tensor")
                em.memset("gpsimd", A("kcarry"), 0.0)
                em.memset("gpsimd", A("vcarry"), 0.0)
                em.mark("gpsimd")
                em.join("tensor", "gpsimd")

            ck()
            em.join("tensor", "vector")
            a2m = {}
            for tg in range(QG):
                bk = 6 + tg % 2
                em.wait("tensor", "ms_vector", cast_m[tg // 4])
                if tg >= 2:
                    em.wait("tensor", "ms_scalar", a2m[tg - 2])
                for kk in range(8):
                    em.trans(pb(bk, kk * 128, (kk + 1) * 128),
                             A("xbTM")[:, tg * 1024 + kk * 128:tg * 1024 + (kk + 1) * 128],
                             A("ident"))
                em.mark("tensor")
                em.join("scalar", "tensor")
                dst = A("xhT").rearrange("p (k t) -> p k t", k=8)[:, :, tg * 128:(tg + 1) * 128]
                em.act(dst, pb(bk).rearrange("p (k c) -> p k c", k=8), AF.Identity)
                em.mark("scalar")
                a2m[tg] = em.cnt("ms_scalar")
            em.join("tensor", "scalar")

            ck()
            if prev_stats[0] is not None:
                pb_, pt_ = prev_stats[0]
                em.join("vector", "gpsimd")
                for tg in range(QG):
                    em.act(SC("rstd", tg), SC("varp", tg), AF.Ln)
                    em.act(SC("rstd", tg), SC("rstd", tg), AF.Exp, scale=-0.5)
                em.mark("scalar")
                em.join("vector", "scalar")
                outdma = {}
                for tg in range(QG):
                    ob = A("outst") if tg % 2 == 0 else A("outst2")
                    sem = "s_out%d" % (tg % 2)
                    if tg >= 2:
                        em.wait("vector", sem, outdma[tg - 2])
                    em.ts("vector", ob, A("y2")[:, tg * 1024:(tg + 1) * 1024],
                          SC("negmean", tg), SC("rstd", tg), ALU.add, ALU.mult)
                    em.mark("vector")
                    em.join("scalar", "vector")
                    if dump is None:
                        em.dma("scalar", out_rows(pb_, pt_, tg), ob, sem)
                    outdma[tg] = em.cnt(sem)

            seg_wait_pe(seg0 + SEGQ)
            wave_evac = {}

            def proj_wave(name, segi, ochs, banks, dst, dstoff, bias0, scale):
                prev = wave_evac.get(banks[0] // 4)
                if prev is not None:
                    em.wait("tensor", "ms_scalar", prev)
                for j, och in enumerate(ochs):
                    for kk in range(8):
                        em.mm(pf(banks[j]),
                              seg_ap(segi, (och * 8 + kk) * 128, (och * 8 + kk + 1) * 128),
                              A("xhT")[:, kk * T:(kk + 1) * T], start=(kk == 0), stop=(kk == 7))
                em.mark("tensor")
                mmk = em.cnt("ms_tensor")
                em.wait("scalar", "ms_tensor", mmk)
                for j, och in enumerate(ochs):
                    em.act(A(dst)[:, (dstoff + och) * T:(dstoff + och + 1) * T], pf(banks[j]),
                           AF.Identity, bias=A("bcols")[:, bias0 + och:bias0 + och + 1],
                           scale=scale)
                em.mark("scalar")
                wave_evac[banks[0] // 4] = em.cnt("ms_scalar")

            proj_wave("qT", seg0 + SEGQ, range(0, 4), [0, 1, 2, 3], "qT", 0, 32, 0.125)
            proj_wave("qT", seg0 + SEGQ, range(4, 8), [4, 5, 6, 7], "qT", 0, 32, 0.125)
            seg_wait_pe(seg0 + SEGK)
            proj_wave("kT", seg0 + SEGK, range(0, 4), [0, 1, 2, 3], "kT", 0, 8, 1.0)
            proj_wave("kT", seg0 + SEGK, range(4, 8), [4, 5, 6, 7], "kT", 0, 8, 1.0)
            seg_done(seg0 + SEGQ)
            seg_wait_pe(seg0 + SEGV)

            def v_wave(tgs, banks):
                prev = wave_evac.get(banks[0] // 4)
                if prev is not None:
                    em.wait("tensor", "ms_scalar", prev)
                for j, tg in enumerate(tgs):
                    for kk in range(8):
                        lh = A("xhT")[:, kk * T + tg * 128:kk * T + (tg + 1) * 128]
                        em.mm(pf(banks[2 * j]), lh,
                              seg_ap(seg0 + SEGV, kk * 1024, kk * 1024 + 512),
                              start=(kk == 0), stop=False)
                        em.mm(pf(banks[2 * j + 1]), lh,
                              seg_ap(seg0 + SEGV, kk * 1024 + 512, (kk + 1) * 1024),
                              start=(kk == 0), stop=False)
                    em.mm(pf(banks[2 * j]), ones_row(128), A("rowpack")[0:1, 0:512],
                          start=False, stop=True)
                    em.mm(pf(banks[2 * j + 1]), ones_row(128), A("rowpack")[0:1, 512:1024],
                          start=False, stop=True)
                em.mark("tensor")
                mmk = em.cnt("ms_tensor")
                em.wait("scalar", "ms_tensor", mmk)
                for j, tg in enumerate(tgs):
                    em.act(A("vT")[:, tg * 1024:tg * 1024 + 512],
                           pf(banks[2 * j]), AF.Identity)
                    em.act(A("vT")[:, tg * 1024 + 512:(tg + 1) * 1024],
                           pf(banks[2 * j + 1]), AF.Identity)
                em.mark("scalar")
                wave_evac[banks[0] // 4] = em.cnt("ms_scalar")

            v_wave([0, 1], [0, 1, 2, 3])
            v_wave([2, 3], [4, 5, 6, 7])
            seg_done(seg0 + SEGK)
            seg_wait_pe(seg0 + SEGRQRO)
            prev = wave_evac.get(0)
            if prev is not None:
                em.wait("tensor", "ms_scalar", prev)
            for och in range(2):
                for kk in range(8):
                    em.mm(pf(och), seg_ap(seg0 + SEGRQRO, (och * 8 + kk) * 128,
                                          (och * 8 + kk + 1) * 128),
                          A("xhT")[:, kk * T:(kk + 1) * T], start=(kk == 0), stop=(kk == 7))
            em.mark("tensor")
            em.join("scalar", "tensor")
            for och in range(2):
                em.act(A("rqT")[:, och * T:(och + 1) * T], pf(och), AF.Identity,
                       bias=A("bcols")[:, 40 + och:41 + och], scale=0.125)
            em.mark("scalar")
            em.join("tensor", "scalar")
            seg_done(seg0 + SEGV)

            ck()
            ev_marks = {}

            def l_scores(g, m):
                par = m % 2
                key = ("exp", g * 8 + m - 2)
                if key in ev_marks:
                    em.wait("tensor", "ms_scalar", ev_marks[key])
                for i, h in enumerate((m, m + 8)):
                    och = h // 2
                    rlo = (h % 2) * 64
                    qsl = A("qT")[rlo:rlo + 64, och * T + g * 128:och * T + (g + 1) * 128]
                    dst = pf(par, i * 256, (i + 1) * 256)
                    if g == 0:
                        em.mm(pf(par, i * 256, i * 256 + 128), qsl,
                              A("kcarry")[rlo:rlo + 64, och * 128:(och + 1) * 128],
                              start=True, stop=True)
                        em.mm(pf(par, i * 256 + 128, i * 256 + 256), qsl,
                              A("kT")[rlo:rlo + 64, och * T:och * T + 128],
                              start=True, stop=True)
                    else:
                        em.mm(dst, qsl,
                              A("kT")[rlo:rlo + 64, och * T + (g - 1) * 128:och * T + (g + 1) * 128],
                              start=True, stop=True)
                em.mark("tensor")
                ev_marks[("sc", g * 8 + m)] = em.cnt("ms_tensor")

            def l_chain1(g, m):
                par = m % 2
                k = g * 8 + m
                first_g = (t == 0 and g == 0)
                pslot = (k % 4) * 512
                em.wait("vector", "ms_tensor", ev_marks[("sc", k)])
                em.red("vector", SC("nmax", 2 * par, 2),
                       ps_l[par].ap().rearrange("p (bk c) -> p bk c", c=256),
                       "max", neg=True)
                em.mark("vector")
                em.join("scalar", "vector")
                for i in range(2):
                    em.act(A("probsB")[:, pslot + i * 256:pslot + (i + 1) * 256],
                           pf(par, i * 256, (i + 1) * 256), AF.Exp,
                           bias=SC("nmax", i + 2 * par))
                em.mark("scalar")
                ev_marks[("exp", k)] = em.cnt("ms_scalar")
                em.join("gpsimd", "scalar")
                mofs = 0 if first_g else 256
                for i in range(2):
                    pr = A("probsB")[:, pslot + i * 256:pslot + (i + 1) * 256]
                    em.tt("gpsimd", pr, pr, A("binAB")[:, mofs:mofs + 256], ALU.mult)
                em.mark("gpsimd")
                ev_marks[("mask", k)] = em.cnt("ms_gpsimd")

            def l_chain2(g, m):
                par = m % 2
                k = g * 8 + m
                first_g = (t == 0 and g == 0)
                pslot = (k % 4) * 512
                em.wait("vector", "ms_gpsimd", ev_marks[("mask", k)])
                em.red("vector", SC("rsum", 2 * par, 2),
                       A("probsB").rearrange("p (bk c) -> p bk c", c=256)[:, 2 * (k % 4):2 * (k % 4) + 2],
                       "sum")
                if first_g:
                    em.ts("vector", SC("rsum", 2 * par, 2), SC("rsum", 2 * par, 2), 1e-30,
                          None, ALU.max)
                em.recip("vector", SC("rr", 2 * par, 2), SC("rsum", 2 * par, 2))
                for i in range(2):
                    pr = A("probsB")[:, pslot + i * 256:pslot + (i + 1) * 256]
                    em.ts("vector", pr, pr, SC("rr", i + 2 * par))
                em.mark("vector")
                ev_marks[("norm", k)] = em.cnt("ms_vector")

            def l_trans(g, m):
                par = m % 2
                k = g * 8 + m
                pslot = (k % 4) * 512
                em.wait("tensor", "ms_vector", ev_marks[("norm", k)])
                key = ("ptev", k - 1)
                if key in ev_marks:
                    em.wait("tensor", ev_marks[key][0], ev_marks[key][1])
                for i in range(2):
                    pr0 = A("probsB")[:, pslot + i * 256:pslot + i * 256 + 128]
                    pr1 = A("probsB")[:, pslot + i * 256 + 128:pslot + (i + 1) * 256]
                    em.trans(pb(2, par * 512 + i * 256, par * 512 + i * 256 + 128), pr0,
                             A("ident"))
                    em.trans(pb(2, par * 512 + i * 256 + 128, par * 512 + (i + 1) * 256), pr1,
                             A("ident"))
                em.mark("tensor")
                eng = "scalar" if k % 2 == 0 else "vector"
                em.join(eng, "tensor")
                if eng == "scalar":
                    em.act(A("probsTB")[:, pslot:pslot + 512],
                           pb(2, par * 512, par * 512 + 512), AF.Identity)
                else:
                    em.copy("vector", A("probsTB")[:, pslot:pslot + 512],
                            pb(2, par * 512, par * 512 + 512))
                em.mark(eng)
                ev_marks[("ptev", k)] = ("ms_" + eng, em.cnt("ms_" + eng))

            def l_pv(g, m):
                par = m % 2
                k = g * 8 + m
                pslot = (k % 4) * 512
                em.wait("tensor", ev_marks[("ptev", k)][0], ev_marks[("ptev", k)][1])
                key = ("lev", k - 1)
                if key in ev_marks:
                    em.wait("tensor", ev_marks[key][0], ev_marks[key][1])
                for i, h in enumerate((m, m + 8)):
                    vcol = h * 64
                    if g == 0:
                        vprev = A("vcarry")[:, vcol:vcol + 64]
                    else:
                        vprev = A("vT")[:, (g - 1) * 1024 + vcol:(g - 1) * 1024 + vcol + 64]
                    vcur = A("vT")[:, g * 1024 + vcol:g * 1024 + vcol + 64]
                    dst = pf(3, par * 256 + i * 128, par * 256 + (i + 1) * 128, 0, 64)
                    em.mm(dst, vprev, A("probsTB")[:, pslot + i * 256:pslot + i * 256 + 128],
                          start=True, stop=False)
                    em.mm(dst, vcur, A("probsTB")[:, pslot + i * 256 + 128:pslot + (i + 1) * 256],
                          start=False, stop=True)
                em.mark("tensor")
                pvm = em.cnt("ms_tensor")
                em.wait("scalar", "ms_tensor", pvm)
                for i, h in enumerate((m, m + 8)):
                    och = h // 2
                    rlo = (h % 2) * 64
                    em.act(A("localT")[rlo:rlo + 64, och * T + g * 128:och * T + (g + 1) * 128],
                           pf(3, par * 256 + i * 128, par * 256 + (i + 1) * 128, 0, 64),
                           AF.Identity)
                em.mark("scalar")
                ev_marks[("lev", k)] = ("ms_scalar", em.cnt("ms_scalar"))

            def lat_scores(tg):
                key = ("lexp", tg - 2)
                if key in ev_marks:
                    em.wait("tensor", "ms_scalar", ev_marks[key])
                for lhi in range(LH):
                    och = lhi // 2
                    rlo = (lhi % 2) * 64
                    em.mm(pf(4 + lhi % 2, (lhi // 2) * 64, (lhi // 2) * 64 + NLAT),
                          A("rqT")[rlo:rlo + 64, och * T + tg * 128:och * T + (tg + 1) * 128],
                          A("rkres")[rlo:rlo + 64, (b * 2 + och) * 64:(b * 2 + och) * 64 + NLAT],
                          start=True, stop=True)
                em.mark("tensor")
                ev_marks[("lsc", tg)] = em.cnt("ms_tensor")

            def lat_chain(tg):
                sti = t * QG + tg
                em.wait("vector", "ms_tensor", ev_marks[("lsc", tg)])
                for lhi in range(LH):
                    em.red("vector", SC("nmax", 4 + lhi),
                           pf(4 + lhi % 2, (lhi // 2) * 64, (lhi // 2) * 64 + NLAT),
                           "max", neg=True)
                em.mark("vector")
                em.join("scalar", "vector")
                for lhi in range(LH):
                    em.act(A("latpr")[:, lhi * 64:lhi * 64 + NLAT],
                           pf(4 + lhi % 2, (lhi // 2) * 64, (lhi // 2) * 64 + NLAT),
                           AF.Exp, bias=SC("nmax", 4 + lhi))
                em.mark("scalar")
                ev_marks[("lexp", tg)] = em.cnt("ms_scalar")
                em.join("gpsimd", "scalar")
                for lhi in range(LH):
                    pr = A("latpr")[:, lhi * 64:lhi * 64 + NLAT]
                    em.tt("gpsimd", pr, pr, A("latbin")[:, sti * NLAT:(sti + 1) * NLAT], ALU.mult)
                em.mark("gpsimd")
                em.join("vector", "gpsimd")
                for lhi in range(LH):
                    em.red("vector", SC("rsum", 4 + lhi), A("latpr")[:, lhi * 64:lhi * 64 + NLAT],
                           "sum")
                if tg == 0 and t == 0:
                    em.ts("vector", SC("rsum", 4, 4), SC("rsum", 4, 4), 1e-30, None, ALU.max)
                em.recip("vector", SC("rr", 4, 4), SC("rsum", 4, 4))
                for lhi in range(LH):
                    pr = A("latpr")[:, lhi * 64:lhi * 64 + NLAT]
                    em.ts("vector", pr, pr, SC("rr", 4 + lhi))
                em.mark("vector")
                ev_marks[("lnorm", tg)] = em.cnt("ms_vector")

            def lat_pv(tg):
                em.wait("tensor", "ms_vector", ev_marks[("lnorm", tg)])
                key = ("llev", tg - 1)
                if key in ev_marks:
                    em.wait("tensor", "ms_scalar", ev_marks[key])
                for lhi in range(LH):
                    em.trans(pb(6, lhi * 128, lhi * 128 + 128, 0, NLAT),
                             A("latpr")[:, lhi * 64:lhi * 64 + NLAT], A("ident"))
                em.mark("tensor")
                em.join("scalar", "tensor")
                for lhi in range(LH):
                    em.act(A("lattnT")[0:NLAT, lhi * 128:lhi * 128 + 128],
                           pb(6, lhi * 128, lhi * 128 + 128, 0, NLAT), AF.Identity)
                em.mark("scalar")
                em.join("tensor", "scalar")
                for lhi in range(LH):
                    rvoff = (b * 2 + lhi // 2) * 128 + (lhi % 2) * 64
                    em.mm(pf(7, lhi * 128, (lhi + 1) * 128, 0, 64),
                          A("rvres")[0:NLAT, rvoff:rvoff + 64],
                          A("lattnT")[0:NLAT, lhi * 128:(lhi + 1) * 128], start=True, stop=True)
                em.mark("tensor")
                lpv = em.cnt("ms_tensor")
                em.wait("scalar", "ms_tensor", lpv)
                for lhi in range(LH):
                    och = lhi // 2
                    rlo = (lhi % 2) * 64
                    em.act(A("latoT")[rlo:rlo + 64, och * T + tg * 128:och * T + (tg + 1) * 128],
                           pf(7, lhi * 128, (lhi + 1) * 128, 0, 64), AF.Identity)
                em.mark("scalar")
                ev_marks[("llev", tg)] = em.cnt("ms_scalar")

            NB = QG * 8

            def batch(k):
                return (k // 8, k % 8)

            for i in range(NB + 4):
                if i < NB:
                    l_scores(*batch(i))
                if 0 <= i - 1 < NB:
                    l_chain1(*batch(i - 1))
                if 0 <= i - 2 < NB:
                    l_chain2(*batch(i - 2))
                if 0 <= i - 3 < NB:
                    l_trans(*batch(i - 3))
                if 0 <= i - 4 < NB:
                    l_pv(*batch(i - 4))
                if i % 8 == 4 and i // 8 < QG:
                    lat_scores(i // 8)
                if i % 8 == 6 and i // 8 < QG:
                    lat_chain(i // 8)
                if i % 8 == 0 and 0 < i // 8 <= QG:
                    lat_pv(i // 8 - 1)

            em.join("gpsimd", "tensor")
            em.copy("gpsimd",
                    A("kcarry").rearrange("p (k c) -> p k c", k=8),
                    A("kT").rearrange("p (k t) -> p k t", k=8)[:, :, T - 128:T])
            em.copy("gpsimd", A("vcarry"), A("vT")[:, (QG - 1) * 1024:QG * 1024])
            em.mark("gpsimd")
            ck()
            em.join("tensor", "scalar")
            em.join("tensor", "vector")
            em.join("tensor", "gpsimd")
            seg_wait_pe(seg0 + SEGRQRO)
            for grp in range(2):
                for oc in range(grp * 4, grp * 4 + 4):
                    for L in range(2):
                        em.mm(pf(oc % 4),
                              seg_ap(seg0 + SEGRQRO, 2048 + (oc * 2 + L) * 128,
                                     2048 + (oc * 2 + L + 1) * 128),
                              A("latoT")[:, L * T:(L + 1) * T], start=(L == 0), stop=(L == 1))
                em.mark("tensor")
                em.join("scalar", "tensor")
                for oc in range(grp * 4, grp * 4 + 4):
                    em.act(A("remT")[:, oc * T:(oc + 1) * T], pf(oc % 4), AF.Identity,
                           bias=A("bcols")[:, 22 + oc:23 + oc])
                em.mark("scalar")
                em.join("tensor", "scalar")
            seg_done(seg0 + SEGRQRO)
            for cidx in range(8):
                em.mm(pf(4, 0, 512, 0, 16), A("glgr")[:, cidx * 16:(cidx + 1) * 16],
                      A("localT")[:, cidx * T:(cidx + 1) * T], start=(cidx == 0), stop=False)
            for cidx in range(8):
                em.mm(pf(4, 0, 512, 0, 16), A("glgr")[:, 128 + cidx * 16:128 + (cidx + 1) * 16],
                      A("remT")[:, cidx * T:(cidx + 1) * T], start=False, stop=(cidx == 7))
            em.mark("tensor")
            em.join("scalar", "tensor")
            em.act(A("gatesf")[0:16, 0:512], pf(4, 0, 512, 0, 16), AF.Exp, scale=-1.0)
            em.mark("scalar")
            em.join("vector", "scalar")
            em.ts("vector", A("gatesf")[0:16, 0:512], A("gatesf")[0:16, 0:512], 1.0, None, ALU.add)
            em.recip("vector", A("gatesf")[0:16, 0:512], A("gatesf")[0:16, 0:512])
            em.copy("vector", A("gatesb")[0:16, 0:512], A("gatesf")[0:16, 0:512])
            em.mark("vector")
            gate_mark = em.cnt("ms_vector")
            em.wait("tensor", "ms_vector", gate_mark)
            for cidx in range(8):
                gb = 5 + cidx % 2
                if cidx >= 2:
                    em.wait("tensor", "ms_vector", ev_marks.get(("mix", cidx - 2), 0))
                em.mm(pf(gb), A("selc")[0:16, cidx * 128:(cidx + 1) * 128],
                      A("gatesb")[0:16, 0:512], start=True, stop=True)
                em.mark("tensor")
                em.join("vector", "tensor")
                lc = A("localT")[:, cidx * T:(cidx + 1) * T]
                rc = A("remT")[:, cidx * T:(cidx + 1) * T]
                em.tt("vector", A("dtmp"), lc, rc, ALU.subtract)
                em.tt("vector", A("dtmp"), A("dtmp"), pf(gb), ALU.mult)
                em.tt("vector", A("mixT")[:, cidx * T:(cidx + 1) * T], A("dtmp"), rc, ALU.add)
                em.mark("vector")
                ev_marks[("mix", cidx)] = em.cnt("ms_vector")

            ck()
            seg_wait_pe(seg0 + SEGOW)
            em.join("tensor", "vector")
            em.wait("scalar", "s_out0", em.cnt("s_out0"))
            em.wait("scalar", "s_out1", em.cnt("s_out1"))
            mm_marks = {}
            for tg in range(QG):
                for cidx in range(8):
                    lh = A("mixT")[:, cidx * T + tg * 128:cidx * T + (tg + 1) * 128]
                    em.mm(pf(2 * tg), lh, seg_ap(seg0 + SEGOW, cidx * 1024, cidx * 1024 + 512),
                          start=(cidx == 0), stop=False)
                    em.mm(pf(2 * tg + 1), lh,
                          seg_ap(seg0 + SEGOW, cidx * 1024 + 512, (cidx + 1) * 1024),
                          start=(cidx == 0), stop=False)
                em.mm(pf(2 * tg), ones_row(128), A("rowpack")[0:1, 1024:1536],
                      start=False, stop=True)
                em.mm(pf(2 * tg + 1), ones_row(128), A("rowpack")[0:1, 1536:2048],
                      start=False, stop=True)
                em.mark("tensor")
                mm_marks[tg] = em.cnt("ms_tensor")
            seg_done(seg0 + SEGOW)
            for tg in range(QG):
                y1 = A("y1n")[:, tg * 1024:(tg + 1) * 1024]
                em.wait("vector", "ms_tensor", mm_marks[tg])
                em.tt("vector", y1[:, 0:512], pf(2 * tg), A("x")[:, tg * 1024:tg * 1024 + 512],
                      ALU.add)
                em.tt("vector", y1[:, 512:1024], pf(2 * tg + 1),
                      A("x")[:, tg * 1024 + 512:(tg + 1) * 1024], ALU.add)
                em.red("vector", SC("negmean", tg), y1, "sum", neg=True)
                em.ts("vector", SC("negmean", tg), SC("negmean", tg), 1.0 / 1024.0)
                em.mark("vector")
                mm_marks[("r", tg)] = em.cnt("ms_vector")
            for tg in range(QG):
                y1 = A("y1n")[:, tg * 1024:(tg + 1) * 1024]
                em.wait("scalar", "ms_vector", mm_marks[("r", tg)])
                em.act(A("outst"), y1, AF.Square, bias=SC("negmean", tg),
                       accum=SC("varp", tg))
                em.act(SC("rstd", tg), SC("varp", tg), AF.Ln, bias=A("stats")[:, 60:61], scale=1.0 / 1024.0)
                em.act(SC("rstd", tg), SC("rstd", tg), AF.Exp, scale=-0.5)
                em.mark("scalar")
                mm_marks[("s", tg)] = em.cnt("ms_scalar")
            for tg in range(QG):
                y1 = A("y1n")[:, tg * 1024:(tg + 1) * 1024]
                em.wait("vector", "ms_scalar", mm_marks[("s", tg)])
                em.ts("vector", y1, y1, SC("negmean", tg), SC("rstd", tg), ALU.add, ALU.mult)
                em.mark("vector")
            nb, ntile = (b, t + 1) if t + 1 < NT else (b + 1, 0)
            if nb < B:
                em.wait("sync", "ms_vector", em.cnt("ms_vector"))
                em.dma("sync", A("x").rearrange("p (g d) -> p g d", d=1024),
                       xin[nb][ntile * T:(ntile + 1) * T, :].rearrange("(g p) d -> p g d", p=128),
                       "s_x")
                x_loaded[0] = True

            ck()
            em.wait("vector", "s_out0", em.cnt("s_out0"))
            em.wait("vector", "s_out1", em.cnt("s_out1"))
            aq_marks = {}
            for tg in range(QG):
                y1 = A("y1n")[:, tg * 1024:(tg + 1) * 1024]
                em.red("vector", SC("rmax", tg), y1, "max", absval=True)
                em.ts("vector", SC("rmax", tg), SC("rmax", tg), 1e-5, None, ALU.max)
                em.recip("vector", SC("sx", tg), SC("rmax", tg))
                em.ts("vector", SC("sx", tg), SC("sx", tg), 127.0)
                em.ts("vector", SC("invsx", tg), SC("rmax", tg), 1.0 / 127.0)
                em.ts("vector", A("outst"), y1, SC("sx", tg), None, ALU.mult)
                em.ts("vector", A("aq1i"), A("outst"), A("mcols")[:, 7:8], None, ALU.add)
                em.ts("vector", A("xqTM")[:, tg * 1024:(tg + 1) * 1024], A("aq1i"),
                      -256.0, SC("invsx", tg), ALU.add, ALU.mult)
                em.mark("vector")
                aq_marks[tg] = em.cnt("ms_vector")
            for tg in range(QG):
                bk = 6 + tg % 2
                em.wait("tensor", "ms_vector", aq_marks[tg])
                if tg >= 2:
                    em.wait("tensor", "ms_scalar", aq_marks[("e", tg - 2)])
                for kk in range(8):
                    em.trans(pb(bk, kk * 128, (kk + 1) * 128),
                             A("xqTM")[:, tg * 1024 + kk * 128:tg * 1024 + (kk + 1) * 128],
                             A("ident"))
                em.mark("tensor")
                em.join("scalar", "tensor")
                dst = A("xqT").rearrange("p (k t) -> p k t", k=8)[:, :, tg * 128:(tg + 1) * 128]
                em.act(dst, pb(bk).rearrange("p (k c) -> p k c", k=8), AF.Identity)
                em.mark("scalar")
                aq_marks[("e", tg)] = em.cnt("ms_scalar")
            em.join("tensor", "scalar")
            em.table("silu_and_others")
            if tile_idx == 0:
                for s in ("s_twg0", "s_twg1"):
                    em.wait("gpsimd", s, em.cnt(s))
            for e in ("tensor", "vector", "scalar", "gpsimd"):
                for f in ("tensor", "vector", "scalar", "gpsimd"):
                    if e != f:
                        em.join(e, f)

            bank_free = {}
            sg_free = {}
            for og in range(64):
                segi = seg0 + GU0 + og // 8
                if og % 8 == 0:
                    seg_wait_pe(segi)
                bank = og % 8
                if og >= 8:
                    em.wait("tensor", "ms_vector" if (og - 8) % 2 == 1 else "ms_scalar",
                            bank_free[og - 8])
                for kk in range(8):
                    em.mm(pf(bank),
                          seg_ap(segi, ((og % 8) * 8 + kk) * 128, ((og % 8) * 8 + kk + 1) * 128),
                          A("xqT")[:, kk * T:(kk + 1) * T], start=(kk == 0), stop=(kk == 7))
                em.mark("tensor")
                if og % 8 == 7:
                    seg_done(segi)
                cpair = og // 2
                if og % 2 == 0:
                    em.join("scalar", "tensor")
                    if cpair >= 2:
                        em.wait("scalar", "ms_vector", sg_free[cpair - 2])
                    em.act(A("sg")[:, (cpair % 2) * 512:(cpair % 2 + 1) * 512], pf(bank),
                           AF.Silu, scale=A("mcols")[:, 0:1])
                    em.mark("scalar")
                    bank_free[og] = em.cnt("ms_scalar")
                else:
                    em.join("vector", "scalar")
                    em.join("vector", "tensor")
                    em.tt("vector", A("inner")[:, cpair * T:(cpair + 1) * T],
                          A("sg")[:, (cpair % 2) * 512:(cpair % 2 + 1) * 512], pf(bank), ALU.mult)
                    em.mark("vector")
                    bank_free[og] = em.cnt("ms_vector")
                    sg_free[cpair] = em.cnt("ms_vector")

            ck()
            inner3 = A("inner").rearrange("p (c t) -> p t c", c=32)
            em.red("vector", A("mpos"), inner3, "max", absval=True)
            em.mark("vector")
            em.join("tensor", "vector")
            em.join("tensor", "scalar")
            for tg in range(QG):
                em.trans(pf(0, tg * 128, (tg + 1) * 128),
                         A("mpos")[:, tg * 128:(tg + 1) * 128], A("identf32"))
            em.mark("tensor")
            em.join("vector", "tensor")
            for tg in range(QG):
                em.red("vector", SC("rmax", tg), pf(0, tg * 128, (tg + 1) * 128), "max")
            for tg in range(QG):
                em.ts("vector", SC("rmax", tg), SC("rmax", tg), A("mcols")[:, 2:3], None, ALU.max)
                em.tt("vector", SC("scale2", tg), SC("rmax", tg), A("mcols")[:, 3:4], ALU.mult)
                em.recip("vector", SC("sx2", tg), SC("rmax", tg))
                em.ts("vector", SC("sx2", tg), SC("sx2", tg), 127.0)
            for tg in range(QG):
                em.copy("vector", SC("tmp", tg), SC("sx2", tg))
            em.mark("vector")
            em.join("tensor", "vector")
            for tg in range(QG):
                em.trans(pf(1, tg * 128, (tg + 1) * 128, 0, 1), SC("tmp", tg), A("identf32"))
            em.mark("tensor")
            em.join("vector", "tensor")
            em.copy("vector", A("sxrow")[0:1, 0:512], pf(1, 0, 512, 0, 1))
            em.mark("vector")
            em.join("tensor", "vector")
            em.mm(pf(2), A("rowpack")[0:1, 2048:2176], A("sxrow")[0:1, 0:512],
                  start=True, stop=True)
            em.mark("tensor")
            em.join("vector", "tensor")
            em.copy("vector", A("sxbc"), pf(2))
            em.mark("vector")

            ck()
            em.join("gpsimd", "vector")
            q2_marks = {}
            for kk in range(32):
                segi = seg0 + DN0 + kk // 8
                if kk % 8 == 0:
                    seg_wait_pe(segi)
                sl = kk % 2
                em.join("gpsimd", "vector")
                if kk >= 2:
                    em.wait("gpsimd", "ms_tensor", q2_marks[kk - 2])
                em.tt("gpsimd", A("sg")[:, sl * 512:(sl + 1) * 512],
                      A("inner")[:, kk * T:(kk + 1) * T], A("sxbc"), ALU.mult)
                em.mark("gpsimd")
                em.join("vector", "gpsimd")
                if kk >= 2:
                    em.wait("vector", "ms_tensor", q2_marks[kk - 2])
                em.ts("vector", A("q2i"), A("sg")[:, sl * 512:(sl + 1) * 512],
                      A("mcols")[:, 7:8], None, ALU.add)
                em.ts("vector", A("q2ch")[:, sl * 512:(sl + 1) * 512], A("q2i"), -256.0, None,
                      ALU.add)
                em.mark("vector")
                em.join("tensor", "vector")
                for tg in range(QG):
                    lh = A("q2ch")[:, sl * 512 + tg * 128:sl * 512 + (tg + 1) * 128]
                    em.mm(pf(2 * tg), lh, seg_ap(segi, (kk % 8) * 1024, (kk % 8) * 1024 + 512),
                          start=(kk == 0), stop=(kk == 31))
                    em.mm(pf(2 * tg + 1), lh,
                          seg_ap(segi, (kk % 8) * 1024 + 512, (kk % 8 + 1) * 1024),
                          start=(kk == 0), stop=(kk == 31))
                em.mark("tensor")
                q2_marks[kk] = em.cnt("ms_tensor")
                if kk % 8 == 7:
                    seg_done(segi)

            ck()
            em.join("vector", "tensor")
            l2m = {}
            for tg in range(QG):
                y2 = A("y2")[:, tg * 1024:(tg + 1) * 1024]
                em.stt("vector", y2[:, 0:512], pf(2 * tg), SC("scale2", tg),
                       A("y1n")[:, tg * 1024:tg * 1024 + 512], ALU.mult, ALU.add)
                em.stt("vector", y2[:, 512:1024], pf(2 * tg + 1), SC("scale2", tg),
                       A("y1n")[:, tg * 1024 + 512:(tg + 1) * 1024], ALU.mult, ALU.add)
                em.red("vector", SC("negmean", tg), y2, "sum", neg=True)
                em.ts("vector", SC("negmean", tg), SC("negmean", tg), 1.0 / 1024.0)
                em.mark("vector")
                l2m[tg] = em.cnt("ms_vector")
            if x_loaded[0]:
                em.join("vector", "gpsimd")
                em.wait("vector", "s_x", em.cnt("s_x"))
                cast_mh[0] = {}
                for c in range((QG + 3) // 4):
                    lo, hi = 4 * c * 1024, min(4 * c + 4, QG) * 1024
                    em.copy("vector", A("xbTM")[:, lo:hi], A("x")[:, lo:hi])
                    em.mark("vector")
                    cast_mh[0][c] = em.cnt("ms_vector")
                x_casted[0] = True
            em.wait("scalar", "s_out1", em.cnt("s_out1"))
            for tg in range(QG):
                y2 = A("y2")[:, tg * 1024:(tg + 1) * 1024]
                em.wait("scalar", "ms_vector", l2m[tg])
                em.act(A("sqscr"), y2, AF.Square, bias=SC("negmean", tg), accum=SC("varp", tg))
                em.mark("scalar")
                l2m[("s", tg)] = em.cnt("ms_scalar")
            for tg in range(QG):
                em.wait("vector", "ms_scalar", l2m[("s", tg)])
                em.ts("vector", SC("varp", tg), SC("varp", tg), 1.0 / 1024.0, 1e-5,
                      ALU.mult, ALU.add)
            em.mark("vector")
            if dump == "z":
                for tg in range(QG):
                    em.join("scalar", "vector")
                    em.dma("scalar", out_rows(b, t, tg),
                           A("y2")[:, tg * 1024:(tg + 1) * 1024], "s_out0")
                em.mark("scalar")
                em.join("vector", "scalar")
            em.join("scalar", "vector")
            prev_stats[0] = (b, t)

    except StopEmission:
        pass
    if stop is not None:
        em.wait("sync", "s_out0", em.cnt("s_out0"))
        em.wait("sync", "s_out1", em.cnt("s_out1"))
        with contextlib.ExitStack() as bes:
            sems = {}
            for name in em.counts:
                sems[name] = bes.enter_context(nc.semaphore(name))
            block = bes.enter_context(nc.Block())
            em.replay(block, sems)
        return nc
    em.table("natural_log_exp_and_others")
    pb_, pt_ = prev_stats[0]
    em.join("vector", "gpsimd")
    for tg in range(QG):
        em.act(SC("rstd", tg), SC("varp", tg), AF.Ln)
        em.act(SC("rstd", tg), SC("rstd", tg), AF.Exp, scale=-0.5)
    em.mark("scalar")
    em.join("vector", "scalar")
    outdma = {}
    for tg in range(QG):
        ob = A("outst") if tg % 2 == 0 else A("outst2")
        sem = "s_out%d" % (tg % 2)
        if tg >= 2:
            em.wait("vector", sem, outdma[tg - 2])
        em.ts("vector", ob, A("y2")[:, tg * 1024:(tg + 1) * 1024],
              SC("negmean", tg), SC("rstd", tg), ALU.add, ALU.mult)
        em.mark("vector")
        em.join("scalar", "vector")
        if dump is None:
            em.dma("scalar", out_rows(pb_, pt_, tg), ob, sem)
        outdma[tg] = em.cnt(sem)

    em.wait("sync", "s_out0", em.cnt("s_out0"))
    em.wait("sync", "s_out1", em.cnt("s_out1"))

    with contextlib.ExitStack() as bes:
        sems = {}
        for name in em.counts:
            sems[name] = bes.enter_context(nc.semaphore(name))
        block = bes.enter_context(nc.Block())
        em.replay(block, sems)

    return nc



def kernel(**inputs):
    cfg = make_cfg()
    n_cores = 8
    nc = build_program(cfg)
    in_maps = [prep_core_inputs(ci, n_cores, cfg, inputs) for ci in range(n_cores)]
    res = run_bass_kernel_spmd(nc, in_maps, core_ids=list(range(n_cores)))
    out = np.concatenate([np.asarray(r["yout"]) for r in res.results], axis=0)
    return out.astype(np.float32)


# version 14
# speedup vs baseline: 1.0498x; 1.0318x over previous
import contextlib
import numpy as np
import ml_dtypes
import concourse.bass as bass
import concourse.mybir as mybir
from concourse.bass_utils import run_bass_kernel_spmd

F32 = mybir.dt.float32
BF16 = mybir.dt.bfloat16
I32 = mybir.dt.int32
AF = mybir.ActivationFunctionType
ALU = mybir.AluOpType
AX = mybir.AxisListType
BF = ml_dtypes.bfloat16

D = 1024
H = 16
HD = 64
LH = 4
LHD = 64
LD = 256
CH = 32
INNER = 4096
WIN = 128

SEG_EL = 8192
RING_SLOTS = 2


def make_cfg(B=2, S=2048):
    T = 512 if S >= 512 else S
    return dict(B=B, S=S, T=T, NT=S // T, QG=T // 128, NLAT=S // CH, STI=S // 128)



def build_consts(cfg):
    T, STI, NLAT = cfg["T"], cfg["STI"], cfg["NLAT"]
    c = {}
    c["ident"] = np.eye(128, dtype=BF)
    c["identf32"] = np.eye(128, dtype=np.float32)
    qi = np.arange(128)[:, None]
    kj = np.arange(256)[None, :]
    binB = ((kj >= qi) & (kj <= qi + 127)).astype(BF)
    binA = ((kj >= np.maximum(qi, 128)) & (kj <= qi + 127)).astype(BF)
    c["binB"] = binB
    c["binA"] = binA
    lat = np.zeros((128, STI * NLAT), dtype=BF)
    for sti in range(STI):
        p = np.arange(128)[:, None]
        l = np.arange(NLAT)[None, :]
        lat[:, sti * NLAT:(sti + 1) * NLAT] = (l < (sti * 4 + p // 32)).astype(BF)
    c["latbin"] = lat
    P4 = np.zeros((128, 4), dtype=np.float32)
    P4[np.arange(128), np.arange(128) // 32] = 1.0 / 32.0
    c["P4"] = P4.astype(BF)
    sel = np.zeros((16, 8 * 128), dtype=BF)
    for ci in range(8):
        for p in range(128):
            sel[2 * ci + p // 64, ci * 128 + p] = 1.0
    c["sel"] = sel
    return c



def _pack_ws(w, och_count, k_count):
    d_in, d_out = w.shape
    assert d_in == k_count * 128 and d_out == och_count * 128
    t = w.reshape(k_count, 128, och_count, 128)
    t = np.transpose(t, (1, 2, 0, 3))
    return np.ascontiguousarray(t.reshape(128, och_count * k_count * 128))


def _pack_rhs(w, k_count):
    d_in, d_out = w.shape
    assert d_in == k_count * 128
    t = w.reshape(k_count, 128, d_out)
    t = np.transpose(t, (1, 0, 2))
    return np.ascontiguousarray(t.reshape(128, k_count * d_out))


def prep_core_inputs(ci, n_cores, cfg, inp):
    B, S = cfg["B"], cfg["S"]
    x = np.ascontiguousarray(inp["L0_attn_x"][ci * B:(ci + 1) * B])
    qkv_w = np.asarray(inp["L0_attn_qkv_w"])
    m = {}
    m["xin"] = x
    m["tq"] = _pack_ws(qkv_w[:, 0:1024], 8, 8)
    m["tk"] = _pack_ws(qkv_w[:, 1024:2048], 8, 8)
    m["tv"] = _pack_rhs(qkv_w[:, 2048:3072], 8)
    m["trq"] = _pack_ws(np.asarray(inp["L0_attn_rq_w"]), 2, 8)
    m["trout"] = _pack_ws(np.asarray(inp["L0_attn_rout_w"]), 8, 2)
    m["tow"] = _pack_rhs(np.asarray(inp["L0_attn_out_w"]), 8)
    m["trkv"] = _pack_ws(np.asarray(inp["L0_attn_rkv_w"]), 4, 8)
    wg = np.ascontiguousarray(np.asarray(inp["L0_ffn_w_gate"]).T)
    wu = np.ascontiguousarray(np.asarray(inp["L0_ffn_w_up"]).T)
    wd = np.ascontiguousarray(np.asarray(inp["L0_ffn_w_down"]).T)
    gw = wg.reshape(8, 128, 32, 128)
    uw = wu.reshape(8, 128, 32, 128)
    gu = np.empty((128, 64 * 8 * 128), dtype=np.float32)
    for og in range(64):
        src = gw if og % 2 == 0 else uw
        oc = og // 2
        blk = np.transpose(src[:, :, oc, :], (1, 0, 2)).reshape(128, 8 * 128)
        gu[:, og * 1024:(og + 1) * 1024] = blk
    m["tgu"] = np.ascontiguousarray(gu)
    m["tdn"] = _pack_rhs(wd, 32)
    qkv_b = np.asarray(inp["L0_attn_qkv_b"])
    m["bcol_qk"] = np.ascontiguousarray(qkv_b[0:2048].reshape(16, 128).T)
    m["bcol_rq"] = np.ascontiguousarray(np.asarray(inp["L0_attn_rq_b"]).reshape(2, 128).T)
    m["bcol_rkv"] = np.ascontiguousarray(np.asarray(inp["L0_attn_rkv_b"]).reshape(4, 128).T)
    m["bcol_rout"] = np.ascontiguousarray(np.asarray(inp["L0_attn_rout_b"]).reshape(8, 128).T)
    m["brow_v"] = np.ascontiguousarray(qkv_b[2048:3072].reshape(1, 1024))
    m["brow_out"] = np.ascontiguousarray(np.asarray(inp["L0_attn_out_b"]).reshape(1, 1024))
    m["gl"] = np.ascontiguousarray(np.asarray(inp["L0_attn_gl_w"]).reshape(64, 1))
    m["gr"] = np.ascontiguousarray(np.asarray(inp["L0_attn_gr_w"]).reshape(64, 1))
    return m



ENGS = ("sync", "tensor", "vector", "scalar", "gpsimd")


class Em:
    def __init__(self):
        self.q = {e: [] for e in ENGS}
        self.written = {e: set() for e in ENGS}
        self.readden = {e: set() for e in ENGS}
        self.counts = {}
        self.ranges = {}

    def _overlap(self, s1, s2):
        for a in s1:
            for b in s2:
                if a == b:
                    return True
                ra, rb = self.ranges.get(a), self.ranges.get(b)
                if ra and rb and ra[0] < rb[1] and rb[0] < ra[1]:
                    return True
        return False

    def _names(self, aps):
        out = set()
        for a in aps:
            if a is None or isinstance(a, (int, float)):
                continue
            out.add(a.tensor.name)
        return out

    def op(self, eng, fn, reads=(), writes=()):
        if getattr(self, "frozen", False):
            return
        if eng in ("vector", "scalar", "gpsimd"):
            rd = self._names(reads)
            wr = self._names(writes)
            if (self._overlap(wr, self.written[eng] | self.readden[eng])
                    or self._overlap(rd, self.written[eng])):
                self.q[eng].append(("drain",))
                self.written[eng] = set()
                self.readden[eng] = set()
            self.written[eng] |= wr
            self.readden[eng] |= rd
        self.q[eng].append(("op", fn))

    def inc(self, eng, sem, n=1):
        if getattr(self, "frozen", False):
            return self.counts.get(sem, 0)
        self.counts[sem] = self.counts.get(sem, 0) + n
        q = self.q[eng]
        i = len(q) - 1
        while i >= 0 and q[i][0] == "wait":
            i -= 1
        if i >= 0 and q[i][0] == "inc":
            q.append(("op", lambda e: e.nop(nofuse=True)))
        q.append(("inc", sem, n))
        return self.counts[sem]

    def wait(self, eng, sem, n=None):
        if getattr(self, "frozen", False):
            return
        if n is None:
            n = self.counts.get(sem, 0)
        if n > 0:
            self.q[eng].append(("wait", sem, n))

    def mark(self, eng):
        q = self.q[eng]
        i = len(q) - 1
        while i >= 0 and q[i][0] == "wait":
            i -= 1
        if i >= 0 and q[i][0] == "inc" and q[i][1] == "ms_" + eng:
            return self.counts["ms_" + eng]
        return self.inc(eng, "ms_" + eng, 1)

    def join(self, eng, src):
        self.wait(eng, "ms_" + src)

    def cnt(self, sem):
        return self.counts.get(sem, 0)

    def tt(self, eng, out, a, b, aop):
        self.op(eng, lambda e: e.tensor_tensor(out, a, b, aop), reads=[a, b], writes=[out])

    def ts(self, eng, out, a, s1, s2=None, op0=ALU.mult, op1=ALU.bypass):
        rd = [a] + [x for x in (s1, s2) if hasattr(x, "tensor")]
        self.op(eng, lambda e: e.tensor_scalar(out, a, s1, s2, op0, op1), reads=rd, writes=[out])

    def stt(self, eng, out, in0, scalar, in1, op0, op1):
        rd = [in0, in1] + ([scalar] if hasattr(scalar, "tensor") else [])
        self.op(eng, lambda e: e.scalar_tensor_tensor(out, in0, scalar, in1, op0, op1),
                reads=rd, writes=[out])

    def copy(self, eng, out, a):
        self.op(eng, lambda e: e.tensor_copy(out, a), reads=[a], writes=[out])

    def memset(self, eng, ap, v):
        self.op(eng, lambda e: e.memset(ap, v), reads=[], writes=[ap])

    def red(self, eng, out, a, rop, absval=False, neg=False):
        if rop == "max":
            self.op(eng, lambda e: e.reduce_max(out, a, AX.X, apply_absolute_value=absval,
                                                negate=neg), reads=[a], writes=[out])
        else:
            self.op(eng, lambda e: e.reduce_sum(out, a, AX.X, apply_absolute_value=absval,
                                                negate=neg), reads=[a], writes=[out])

    def recip(self, eng, out, a):
        self.op(eng, lambda e: e.reciprocal(out, a), reads=[a], writes=[out])

    def act(self, out, a, func, bias=0.0, scale=1.0, accum=None):
        rd = [a] + [x for x in (bias, scale) if hasattr(x, "tensor")]
        wr = [out] + ([accum] if accum is not None else [])
        self.op("scalar", lambda e: e.activation(out, a, func, bias=bias, scale=scale,
                                                 accum_out=accum), reads=rd, writes=wr)

    def mm(self, out, lhsT, rhs, start, stop):
        self.op("tensor", lambda e: e.matmul(out, lhsT, rhs, start=start, stop=stop))

    def trans(self, out, a, ident):
        self.op("tensor", lambda e: e.transpose(out, a, ident))

    def dma(self, eng, out, a, sem):
        self.op(eng, lambda e: e.dma_start(out=out, in_=a), reads=[a], writes=[out])
        return self.inc(eng, sem, 16)

    def table(self, set_name):
        if getattr(self, "frozen", False):
            return
        def fn(e):
            from concourse.hw_specs import get_activation_tables
            for idx, (nm, funcs) in enumerate(get_activation_tables(e.bass.m.arch).items()):
                if nm == set_name:
                    return e.add_instruction(mybir.InstLoadActFuncSet(
                        name=e.bass.get_next_instruction_name(), act_func_set_id=idx,
                        ins=[], outs=[]))
            raise ValueError(set_name)
        self.q["scalar"].append(("op", fn))

    def replay(self, block, sems):
        def section(engname):
            def body(engobj):
                last = None
                for item in self.q[engname]:
                    k = item[0]
                    if k == "op":
                        last = item[1](engobj)
                    elif k == "inc":
                        assert last is not None, f"inc with no prior op on {engname}"
                        last.then_inc(sems[item[1]], item[2])
                    elif k == "wait":
                        engobj.wait_ge(sems[item[1]], item[2])
                    elif k == "drain":
                        engobj.drain()
            return body
        block.sync(section("sync"))
        block.tensor(section("tensor"))
        block.vector(section("vector"))
        block.scalar(section("scalar"))
        block.gpsimd(section("gpsimd"))



class StopEmission(Exception):
    pass


def build_program(cfg, stop=None, dump=None):
    B, S, T, NT, QG, NLAT, STI = (cfg["B"], cfg["S"], cfg["T"], cfg["NT"], cfg["QG"],
                                  cfg["NLAT"], cfg["STI"])
    NSEQT = S // 128
    consts = build_consts(cfg)

    nc = bass.Bass("TRN2", target_bir_lowering=False)

    xin = nc.dram_tensor("xin", [B, S, D], F32, kind="ExternalInput").ap()
    yout = nc.dram_tensor("yout", [B, S, D], F32, kind="ExternalOutput").ap()
    hv = {}
    for nm, shape in [("tq", [128, 8192]), ("tk", [128, 8192]), ("tv", [128, 8192]),
                      ("trq", [128, 2048]), ("trout", [128, 2048]), ("tow", [128, 8192]),
                      ("trkv", [128, 4096]), ("tgu", [128, 65536]), ("tdn", [128, 32768]),
                      ("bcol_qk", [128, 16]), ("bcol_rq", [128, 2]), ("bcol_rkv", [128, 4]),
                      ("bcol_rout", [128, 8]), ("brow_v", [1, 1024]), ("brow_out", [1, 1024]),
                      ("gl", [64, 1]), ("gr", [64, 1])]:
        hv[nm] = nc.dram_tensor(nm, shape, F32, kind="ExternalInput").ap()
    cin = {k: nc.inline_tensor(v, "c_" + k).ap() for k, v in consts.items()}
    NSEG_TILE = 5 + 8 + 4
    tape = nc.dram_tensor("tape", [128, NSEG_TILE * SEG_EL], BF16, kind="Internal").ap()

    es = contextlib.ExitStack()
    sb = {}

    def alloc(name, shape, dtype):
        sb[name] = nc.alloc_sbuf_tensor(name, shape, dtype)
        return sb[name]

    alloc("x", [128, 4096], F32)
    alloc("y1n", [128, 4096], F32)
    alloc("y2", [128, 4096], F32)
    alloc("xqT", [128, 8 * T], BF16)
    alloc("kcarry", [128, 1024], BF16)
    alloc("vcarry", [128, 1024], BF16)
    alloc("ring", [128, RING_SLOTS * SEG_EL], BF16)
    alloc("rowpack", [128, 2176], F32)
    alloc("ident", [128, 128], BF16)
    alloc("identf32", [128, 128], F32)
    alloc("binAB", [128, 512], BF16)
    alloc("latbin", [128, STI * NLAT], BF16)
    alloc("P4", [128, 4], BF16)
    alloc("selc", [16, 1024], BF16)
    alloc("glgr", [128, 256], BF16)
    alloc("gl2", [128, 2], F32)
    alloc("rkres", [128, B * 2 * LHD], BF16)
    alloc("rvres", [64, B * LH * LHD], BF16)
    alloc("bcols", [128, 64], F32)
    alloc("stats", [128, 64], F32)
    alloc("mcols", [128, 8], F32)
    alloc("sc1", [1, 16], F32)
    alloc("outst", [128, 1024], F32)
    alloc("onescol", [128, 1], F32)

    AR = {}
    em = Em()
    ar_base = [None]

    def ar(name, shape, dtype, off):
        t = nc.alloc_sbuf_tensor_at(name, shape, dtype, offset=ar_base[0] + off)
        sb[name] = t
        esz = {F32: 4, BF16: 2, I32: 4}[dtype]
        em.ranges[t.name] = (off, off + shape[1] * esz)
        return t


    T2 = 2 * T
    T8 = 8 * T
    ARENA_BYTES = 0x17A00
    ar_base[0] = (nc.sbuf_base + 63) // 64 * 64
    assert nc.sbuf_top - ar_base[0] >= ARENA_BYTES, (
        f"SBUF over budget: base {nc.sbuf_base} top {nc.sbuf_top}")

    ar("xbTM", [128, 4096], BF16, 0x0000)
    ar("xhT", [128, T8], BF16, 0x2000)
    ar("qT", [128, T8], BF16, 0x4000)
    ar("kT", [128, T8], BF16, 0x6000)
    ar("vT", [128, 4096], BF16, 0x8000)
    ar("rqT", [128, T2], BF16, 0xA000)
    ar("probsB", [128, 2048], BF16, 0xA800)
    ar("probsTB", [128, 2048], BF16, 0xB800)
    ar("localT", [128, T8], BF16, 0xC800)
    ar("latoT", [128, T2], BF16, 0xE800)
    ar("latpr", [128, 256], BF16, 0xF000)
    ar("lattnT", [64, 512], BF16, 0xF200)
    ar("remT", [128, T8], BF16, 0xF600)
    ar("mixT", [128, T8], BF16, 0x11600)
    ar("dtmp", [128, 512], F32, 0x13600)
    ar("gatesf", [128, 512], F32, 0x13E00)
    ar("gatesb", [128, 512], BF16, 0x14600)
    ar("aq1i", [128, 1024], I32, 0x10000)
    ar("xqTM", [128, 4096], BF16, 0x15A00)
    ar("inner", [128, 32 * T], F32, 0x0000)
    ar("sg", [128, 1024], F32, 0x10000)
    ar("q2i", [128, 512], I32, 0x11000)
    ar("q2ch", [128, 1024], BF16, 0x11800)
    ar("mpos", [128, 512], F32, 0x12000)
    ar("mneg", [128, 512], F32, 0x12800)
    ar("sxbc", [128, 512], F32, 0x13000)
    ar("sxrow", [1, 512], F32, 0x13800)
    ar("sqscr", [128, 1024], F32, 0x14000)
    ar("aq2p", [128, 512], F32, 0x15000)
    ar("aq2q", [128, 512], F32, 0x11000)
    ar("outst2", [128, 1024], F32, 0x16A00)
    ar("wslab", [128, 8192], F32, 0x0000)
    ar("oslab", [128, 8192], BF16, 0x8000)
    ar("islab", [128, 4096], I32, 0xC000)
    ar("xbP", [128, 4096], BF16, 0x10000)
    ar("pooledP", [128, 512], BF16, 0x12000)
    ar("acc", [128, 8], F32, 0x12400)
    ar("abscr", [128, 1024], F32, 0x12500)
    ar("rvtmp", [128, 128], BF16, 0x13600)
    ar("w1slab", [128, 4096], F32, 0x13800)

    ps_l = [nc.alloc_psum_tensor("ps%d" % i, [128, 512], F32) for i in range(8)]
    psb_l = [tt.bitcast(BF16) for tt in ps_l]

    def pf(i, lo=0, hi=512, p0=0, p1=128):
        return ps_l[i].ap()[p0:p1, lo:hi]

    def pb(i, lo=0, hi=1024, p0=0, p1=128):
        return psb_l[i].ap()[p0:p1, lo:hi]

    A = lambda n: sb[n].ap()
    stage_no = [0]

    def ck():
        stage_no[0] += 1
        if stop is not None and stage_no[0] >= stop:
            em.frozen = True

    ones_row = lambda w: A("rowpack")[0:1, 2048:2048 + w]

    ring_state = dict(next_load=0, loaded_off={}, done_at={}, load_sem={})

    def seg_load(idx):
        slot = idx % RING_SLOTS
        prev = idx - RING_SLOTS
        if prev >= 0:
            em.wait("sync", "ms_tensor", ring_state["done_at"][prev])
        for sem, cntv in ring_state["tape_w"].get(idx % NSEG_TILE, []):
            em.wait("sync", sem, cntv)
        w = 4096 if (idx % NSEG_TILE) == 3 else SEG_EL
        dst = A("ring")[:, slot * SEG_EL:slot * SEG_EL + w]
        src = tape[:, (idx % NSEG_TILE) * SEG_EL:(idx % NSEG_TILE) * SEG_EL + w]
        n = em.dma("sync", dst, src, "s_tape%d" % slot)
        ring_state["load_sem"][idx] = ("s_tape%d" % slot, n)
        ring_state["loaded_off"][idx] = slot * SEG_EL

    def seg_ap(idx, lo, hi):
        off = ring_state["loaded_off"][idx]
        return A("ring")[:, off + lo:off + hi]

    def seg_wait_pe(idx):
        sem, n = ring_state["load_sem"][idx]
        em.wait("tensor", sem, n)

    def seg_done(idx):
        em.mark("tensor")
        ring_state["done_at"][idx] = em.cnt("ms_tensor")
        nl = ring_state["next_load"]
        if nl == idx + RING_SLOTS:
            pass
        want = idx + RING_SLOTS
        if want < ring_state["total"]:
            seg_load(want)
            ring_state["next_load"] = want + 1

    em.dma("sync", A("ident"), cin["ident"], "s_c")
    em.dma("sync", A("identf32"), cin["identf32"], "s_c")
    em.dma("sync", A("binAB")[:, 0:256], cin["binA"], "s_c")
    em.dma("sync", A("binAB")[:, 256:512], cin["binB"], "s_c")
    em.dma("sync", A("latbin"), cin["latbin"], "s_c")
    em.dma("sync", A("P4"), cin["P4"], "s_c")
    em.dma("sync", A("selc")[0:16, 0:1024], cin["sel"], "s_c")
    em.dma("sync", A("bcols")[:, 0:16], hv["bcol_qk"], "s_c")
    em.dma("sync", A("bcols")[:, 16:18], hv["bcol_rq"], "s_c")
    em.dma("sync", A("bcols")[:, 18:22], hv["bcol_rkv"], "s_c")
    em.dma("sync", A("bcols")[:, 22:30], hv["bcol_rout"], "s_c")
    em.dma("sync", A("rowpack")[0:1, 0:1024], hv["brow_v"], "s_c")
    em.dma("sync", A("rowpack")[0:1, 1024:2048], hv["brow_out"], "s_c")
    em.dma("sync", A("gl2")[0:64, 0:1], hv["gl"], "s_c")
    em.dma("sync", A("gl2")[64:128, 0:1], hv["gl"], "s_c")
    em.dma("sync", A("gl2")[0:64, 1:2], hv["gr"], "s_c")
    em.dma("sync", A("gl2")[64:128, 1:2], hv["gr"], "s_c")
    n_c = em.cnt("s_c")

    ck()
    em.table("natural_log_exp_and_others")

    em.wait("vector", "s_c", n_c)
    em.memset("vector", A("rowpack")[0:1, 2048:2176], 1.0)
    em.memset("vector", A("onescol"), 1.0)
    em.memset("vector", A("stats")[:, 60:61], 1e-5)
    em.memset("vector", A("stats")[:, 62:63], 0.2)
    em.ts("vector", A("aq1i")[:, 0:1], A("stats")[:, 62:63], 1.0, 256.5, ALU.mult, ALU.add)
    em.ts("vector", A("stats")[:, 63:64], A("aq1i")[:, 0:1], -256.0, None, ALU.add)
    em.ts("vector", A("mcols")[:, 7:8], A("stats")[:, 63:64], -0.5, 256.5, ALU.mult, ALU.add)
    em.ts("vector", A("bcols")[:, 32:40], A("bcols")[:, 0:8], 0.125)
    em.ts("vector", A("bcols")[:, 40:42], A("bcols")[:, 16:18], 0.125)
    em.memset("vector", A("glgr"), 0.0)
    for cidx in range(8):
        em.copy("vector", A("glgr")[0:64, cidx * 16 + 2 * cidx:cidx * 16 + 2 * cidx + 1],
                A("gl2")[0:64, 0:1])
        em.copy("vector", A("glgr")[64:128, cidx * 16 + 2 * cidx + 1:cidx * 16 + 2 * cidx + 2],
                A("gl2")[64:128, 0:1])
        em.copy("vector", A("glgr")[0:64, 128 + cidx * 16 + 2 * cidx:128 + cidx * 16 + 2 * cidx + 1],
                A("gl2")[0:64, 1:2])
        em.copy("vector",
                A("glgr")[64:128, 128 + cidx * 16 + 2 * cidx + 1:128 + cidx * 16 + 2 * cidx + 2],
                A("gl2")[64:128, 1:2])
    em.mark("vector")

    ck()
    ring_state["tape_w"] = {}
    SEGQ, SEGK, SEGV, SEGRQRO, SEGOW = 0, 1, 2, 3, 4
    tape_steps = []
    for seg, nm, dsto, tot in [(SEGQ, "tq", 0, 8192), (SEGK, "tk", 0, 8192),
                               (SEGV, "tv", 0, 8192), (SEGRQRO, "trq", 0, 2048),
                               (SEGRQRO, "trout", 2048, 2048), (SEGOW, "tow", 0, 8192)]:
        for half in range(0, tot, 4096):
            tape_steps.append((seg, nm, dsto, half, min(4096, tot - half)))
    cast_m = {}
    tw_m = {}

    def tape_step(si):
        seg, nm, dsto, half, w = tape_steps[si]
        sl = si % 2
        wap = A("wslab")[:, sl * 4096:sl * 4096 + w]
        oap = A("oslab")[:, sl * 4096:sl * 4096 + w]
        if si >= 2:
            em.wait("sync", "ms_vector", cast_m[si - 2])
        nl = em.dma("sync", wap, hv[nm][:, half:half + w], "s_wp%d" % sl)
        em.wait("vector", "s_wp%d" % sl, nl)
        if si >= 2:
            em.wait("vector", "s_tw%d" % sl, tw_m[si - 2])
        em.copy("vector", oap, wap)
        em.mark("vector")
        cast_m[si] = em.cnt("ms_vector")
        em.wait("sync", "ms_vector", cast_m[si])
        em.dma("sync", tape[:, seg * SEG_EL + dsto + half:seg * SEG_EL + dsto + half + w],
               oap, "s_tw%d" % sl)
        tw_m[si] = em.cnt("s_tw%d" % sl)
        ring_state["tape_w"].setdefault(seg, []).append(("s_tw%d" % sl, tw_m[si]))

    em.memset("vector", A("acc")[:, 0:8], 0.0)
    em.memset("vector", A("rvtmp"), 0.0)
    em.mark("vector")
    p1_steps = []
    for nm, nsl in (("tgu", 32), ("tdn", 16)):
        for sidx in range(nsl):
            p1_steps.append((nm, sidx))
    abs_m = {}
    addm = {}

    def p1_step(si):
        nm, sidx = p1_steps[si]
        sl = si % 2
        wap = A("w1slab")[:, sl * 2048:(sl + 1) * 2048]
        if si >= 2:
            em.wait("gpsimd", "ms_scalar", abs_m[si - 2])
        nl = em.dma("gpsimd", wap, hv[nm][:, sidx * 2048:(sidx + 1) * 2048], "s_gp%d" % sl)
        em.wait("scalar", "s_gp%d" % sl, nl)
        if si >= 1:
            em.wait("scalar", "ms_vector", addm.get(si - 1, 0))
        for j in range(2):
            em.act(A("abscr"), wap[:, j * 1024:(j + 1) * 1024], AF.Abs,
                   accum=A("stats")[:, 56 + 2 * sl + j:57 + 2 * sl + j])
        em.mark("scalar")
        abs_m[si] = em.cnt("ms_scalar")
        em.wait("vector", "ms_scalar", abs_m[si])
        for j in range(2):
            if nm == "tgu":
                og = sidx * 2 + j
                colidx = 0 if og % 2 == 0 else 1
            else:
                colidx = 6
            em.tt("vector", A("acc")[:, colidx:colidx + 1], A("acc")[:, colidx:colidx + 1],
                  A("stats")[:, 56 + 2 * sl + j:57 + 2 * sl + j], ALU.add)
        em.mark("vector")
        addm[si] = em.cnt("ms_vector")

    for si in range(len(p1_steps)):
        if si < len(tape_steps):
            tape_step(si)
        p1_step(si)
    em.mark("vector")

    em.join("tensor", "vector")
    for i, colidx in enumerate((0, 1, 6)):
        em.mm(pf(5, i, i + 1, 0, 1), A("acc")[:, colidx:colidx + 1],
              A("onescol"), start=True, stop=True)
    em.mark("tensor")
    em.join("vector", "tensor")
    NW = 4096 * 1024
    em.ts("vector", A("sc1")[0:1, 0:3], pf(5, 0, 3, 0, 1), 1.0 / NW, 1e-5, ALU.mult, ALU.max)
    em.recip("vector", A("sc1")[0:1, 4:7], A("sc1")[0:1, 0:3])
    em.ts("vector", A("sc1")[0:1, 8:9], A("sc1")[0:1, 5:6], 1e-5)
    em.tt("vector", A("sc1")[0:1, 9:10], A("sc1")[0:1, 2:3], A("sc1")[0:1, 1:2], ALU.mult)
    em.ts("vector", A("sc1")[0:1, 9:10], A("sc1")[0:1, 9:10], 1.0 / 127.0)
    em.mark("vector")
    em.join("tensor", "vector")
    bc_list = [(0, 0), (1, 1), (2, 8), (3, 9), (4, 4), (5, 5), (6, 6)]
    for mcol, scol in bc_list:
        em.mm(pf(6, mcol, mcol + 1), A("rowpack")[0:1, 2048:2176],
              A("sc1")[0:1, scol:scol + 1], start=True, stop=True)
    em.mark("tensor")
    em.join("vector", "tensor")
    em.copy("vector", A("mcols")[:, 0:7], pf(6, 0, 7))
    em.mark("vector")

    GU0 = 5
    DN0 = 13
    q2_m = {}
    wq_m = {}

    def p2_step(si):
        nm, sidx = p1_steps[si]
        sl = si % 2
        wap = A("wslab")[:, sl * 2048:(sl + 1) * 2048]
        oap = A("oslab")[:, sl * 2048:(sl + 1) * 2048]
        if si >= 2:
            em.wait("gpsimd", "ms_vector", q2_m[si - 2])
        else:
            em.wait("gpsimd", "ms_vector", em.cnt("ms_vector"))
            em.wait("gpsimd", "s_gp%d" % sl, em.cnt("s_gp%d" % sl))
        nl = em.dma("gpsimd", wap, hv[nm][:, sidx * 2048:(sidx + 1) * 2048], "s_g2p%d" % sl)
        em.wait("vector", "s_g2p%d" % sl, nl)
        if si >= 2:
            em.wait("vector", "s_twg%d" % sl, wq_m[si - 2])
        else:
            em.wait("vector", "s_tw0", em.cnt("s_tw0"))
            em.wait("vector", "s_tw1", em.cnt("s_tw1"))
        for j in range(2):
            if nm == "tgu":
                og = sidx * 2 + j
                sw = A("mcols")[:, 4:5] if og % 2 == 0 else A("mcols")[:, 5:6]
            else:
                sw = A("mcols")[:, 6:7]
            blk = wap[:, j * 1024:(j + 1) * 1024]
            fb = sb["islab"].bitcast(F32).ap()[:, (sl * 2 + j) * 1024:(sl * 2 + j + 1) * 1024]
            em.ts("vector", blk, blk, sw, None, ALU.mult)
            em.ts("vector", fb, blk, 0.5, None, ALU.is_gt)
            em.ts("vector", blk, blk, -0.5, None, ALU.is_lt)
            em.tt("vector", oap[:, j * 1024:(j + 1) * 1024], fb, blk, ALU.subtract)
        em.mark("vector")
        q2_m[si] = em.cnt("ms_vector")
        em.wait("gpsimd", "ms_vector", q2_m[si])
        if nm == "tgu":
            seg = GU0 + sidx // 4
            off = (sidx % 4) * 2048
        else:
            seg = DN0 + sidx // 4
            off = (sidx % 4) * 2048
        em.dma("gpsimd", tape[:, seg * SEG_EL + off:seg * SEG_EL + off + 2048], oap,
               "s_twg%d" % sl)
        wq_m[si] = em.cnt("s_twg%d" % sl)
        ring_state["tape_w"].setdefault(seg, []).append(("s_twg%d" % sl, wq_m[si]))

    rkvb = sb["w1slab"].bitcast(BF16).ap()
    em.wait("gpsimd", "ms_scalar", em.cnt("ms_scalar"))
    nl = em.dma("gpsimd", rkvb[:, 0:4096], hv["trkv"], "s_gp0")
    rkv_ld = nl
    p2i = [0]

    def pump2(n=2):
        for _ in range(n):
            if p2i[0] < len(p1_steps):
                p2_step(p2i[0])
                p2i[0] += 1

    for b in range(B):
        ck()
        pump2()
        ngr = (S // 128 + 3) // 4
        for grp in range(ngr):
            t0 = grp * 4
            ntl = min(4, S // 128 - t0)
            em.wait("gpsimd", "ms_tensor", em.cnt("ms_tensor"))
            nl = em.dma("gpsimd", A("xbP").rearrange("p (g d) -> p g d", d=1024)[:, 0:ntl],
                        xin[b][t0 * 128:(t0 + ntl) * 128, :].rearrange("(g p) d -> p g d", p=128),
                        "s_xp")
            em.wait("tensor", "s_xp", nl)
            for tl in range(ntl):
                tg = t0 + tl
                for kk in range(8):
                    em.mm(pf(kk, tg * 4, tg * 4 + 4),
                          A("xbP")[:, tl * 1024 + kk * 128:tl * 1024 + (kk + 1) * 128],
                          A("P4"), start=True, stop=True)
            em.mark("tensor")
            pump2()
        ck()
        em.join("scalar", "tensor")
        for kk in range(8):
            em.act(A("pooledP")[:, kk * NLAT:(kk + 1) * NLAT], pf(kk, 0, NLAT), AF.Identity)
        em.mark("scalar")
        ck()
        em.join("tensor", "scalar")
        em.wait("tensor", "s_gp0", rkv_ld)
        for och in range(4):
            for kk in range(8):
                em.mm(pf(och, 0, NLAT),
                      rkvb[:, (och * 8 + kk) * 128:(och * 8 + kk + 1) * 128],
                      A("pooledP")[:, kk * NLAT:(kk + 1) * NLAT], start=(kk == 0), stop=(kk == 7))
        em.mark("tensor")
        ck()
        pump2()
        em.join("scalar", "tensor")
        for och in range(2):
            em.act(A("rkres")[:, (b * 2 + och) * 64:(b * 2 + och) * 64 + NLAT],
                   pf(och, 0, NLAT), AF.Identity, bias=A("bcols")[:, 18 + och:19 + och])
        for och in range(2, 4):
            em.act(A("rvtmp")[:, 0:NLAT] if och == 2 else A("rvtmp")[:, 64:64 + NLAT],
                   pf(och, 0, NLAT), AF.Identity, bias=A("bcols")[:, 18 + och:19 + och])
        em.mark("scalar")
        ck()
        em.join("tensor", "scalar")
        for half in range(2):
            em.trans(pb(4, half * 256, half * 256 + 128, 0, 64),
                     A("rvtmp")[:, half * 64:(half + 1) * 64], A("ident"))
        em.mark("tensor")
        ck()
        em.join("scalar", "tensor")
        for half in range(2):
            em.act(A("rvres")[0:NLAT, (b * 2 + half) * 128:(b * 2 + half + 1) * 128],
                   pb(4, half * 256, half * 256 + 128, 0, NLAT), AF.Identity)
        em.mark("scalar")
        ck()
        em.join("tensor", "scalar")
        pump2()
    pump2(len(p1_steps))
    ck()
    ring_state["total"] = NSEG_TILE * B * NT
    ring_state["done_at"] = {}
    seg_load(0)
    seg_load(1)
    ring_state["next_load"] = 2

    x_loaded = [False]
    x_casted = [False]
    cast_mh = [None]
    prev_stats = [None]

    def out_rows(b, t, tg):
        return yout[b][t * T + tg * 128:t * T + (tg + 1) * 128, :]

    stc = {"nmax": 0, "rsum": 8, "rr": 16, "negmean": 24, "varp": 28, "rstd": 32,
           "rmax": 36, "sx": 40, "invsx": 44, "scale2": 48, "sx2": 52, "tmp": 56}

    def SC(name, i, w=1):
        return A("stats")[:, stc[name] + i:stc[name] + i + w]

    try:
      for b in range(B):
        for t in range(NT):
            tile_idx = b * NT + t
            seg0 = tile_idx * NSEG_TILE

            em.table("natural_log_exp_and_others")

            if not x_loaded[0]:
                em.wait("sync", "ms_gpsimd", em.cnt("ms_gpsimd"))
                nl = em.dma("sync", A("x").rearrange("p (g d) -> p g d", d=1024),
                            xin[b][t * T:(t + 1) * T, :].rearrange("(g p) d -> p g d", p=128),
                            "s_x")
            x_loaded[0] = False
            if not x_casted[0]:
                em.wait("vector", "s_x", em.cnt("s_x"))
                if tile_idx == 0:
                    for s in ("s_twg0", "s_twg1", "s_gp0", "s_gp1", "s_g2p0", "s_g2p1",
                              "s_wp0", "s_wp1", "s_xp"):
                        em.wait("vector", s, em.cnt(s))
                cast_mh[0] = {}
                for c in range((QG + 3) // 4):
                    lo, hi = 4 * c * 1024, min(4 * c + 4, QG) * 1024
                    em.copy("vector", A("xbTM")[:, lo:hi], A("x")[:, lo:hi])
                    em.mark("vector")
                    cast_mh[0][c] = em.cnt("ms_vector")
            x_casted[0] = False
            cast_m = cast_mh[0]

            if t == 0:
                em.join("gpsimd", "tensor")
                em.memset("gpsimd", A("kcarry"), 0.0)
                em.memset("gpsimd", A("vcarry"), 0.0)
                em.mark("gpsimd")
                em.join("tensor", "gpsimd")

            ck()
            em.join("tensor", "vector")
            a2m = {}
            for tg in range(QG):
                bk = 6 + tg % 2
                em.wait("tensor", "ms_vector", cast_m[tg // 4])
                if tg >= 2:
                    em.wait("tensor", "ms_scalar", a2m[tg - 2])
                for kk in range(8):
                    em.trans(pb(bk, kk * 128, (kk + 1) * 128),
                             A("xbTM")[:, tg * 1024 + kk * 128:tg * 1024 + (kk + 1) * 128],
                             A("ident"))
                em.mark("tensor")
                em.join("scalar", "tensor")
                dst = A("xhT").rearrange("p (k t) -> p k t", k=8)[:, :, tg * 128:(tg + 1) * 128]
                em.act(dst, pb(bk).rearrange("p (k c) -> p k c", k=8), AF.Identity)
                em.mark("scalar")
                a2m[tg] = em.cnt("ms_scalar")
            em.join("tensor", "scalar")

            ck()
            if prev_stats[0] is not None:
                pb_, pt_ = prev_stats[0]
                em.join("vector", "gpsimd")
                for tg in range(QG):
                    em.act(SC("rstd", tg), SC("varp", tg), AF.Ln)
                    em.act(SC("rstd", tg), SC("rstd", tg), AF.Exp, scale=-0.5)
                em.mark("scalar")
                em.join("vector", "scalar")
                outdma = {}
                for tg in range(QG):
                    ob = A("outst") if tg % 2 == 0 else A("outst2")
                    sem = "s_out%d" % (tg % 2)
                    if tg >= 2:
                        em.wait("vector", sem, outdma[tg - 2])
                    em.ts("vector", ob, A("y2")[:, tg * 1024:(tg + 1) * 1024],
                          SC("negmean", tg), SC("rstd", tg), ALU.add, ALU.mult)
                    em.mark("vector")
                    em.join("scalar", "vector")
                    if dump is None:
                        em.dma("scalar", out_rows(pb_, pt_, tg), ob, sem)
                    outdma[tg] = em.cnt(sem)

            seg_wait_pe(seg0 + SEGQ)
            wave_evac = {}

            def proj_wave(name, segi, ochs, banks, dst, dstoff, bias0, scale):
                prev = wave_evac.get(banks[0] // 4)
                if prev is not None:
                    em.wait("tensor", "ms_scalar", prev)
                for j, och in enumerate(ochs):
                    for kk in range(8):
                        em.mm(pf(banks[j]),
                              seg_ap(segi, (och * 8 + kk) * 128, (och * 8 + kk + 1) * 128),
                              A("xhT")[:, kk * T:(kk + 1) * T], start=(kk == 0), stop=(kk == 7))
                em.mark("tensor")
                mmk = em.cnt("ms_tensor")
                em.wait("scalar", "ms_tensor", mmk)
                for j, och in enumerate(ochs):
                    em.act(A(dst)[:, (dstoff + och) * T:(dstoff + och + 1) * T], pf(banks[j]),
                           AF.Identity, bias=A("bcols")[:, bias0 + och:bias0 + och + 1],
                           scale=scale)
                em.mark("scalar")
                wave_evac[banks[0] // 4] = em.cnt("ms_scalar")

            proj_wave("qT", seg0 + SEGQ, range(0, 4), [0, 1, 2, 3], "qT", 0, 32, 0.125)
            proj_wave("qT", seg0 + SEGQ, range(4, 8), [4, 5, 6, 7], "qT", 0, 32, 0.125)
            seg_wait_pe(seg0 + SEGK)
            proj_wave("kT", seg0 + SEGK, range(0, 4), [0, 1, 2, 3], "kT", 0, 8, 1.0)
            proj_wave("kT", seg0 + SEGK, range(4, 8), [4, 5, 6, 7], "kT", 0, 8, 1.0)
            seg_done(seg0 + SEGQ)
            seg_wait_pe(seg0 + SEGV)

            def v_wave(tgs, banks):
                prev = wave_evac.get(banks[0] // 4)
                if prev is not None:
                    em.wait("tensor", "ms_scalar", prev)
                for j, tg in enumerate(tgs):
                    for kk in range(8):
                        lh = A("xhT")[:, kk * T + tg * 128:kk * T + (tg + 1) * 128]
                        em.mm(pf(banks[2 * j]), lh,
                              seg_ap(seg0 + SEGV, kk * 1024, kk * 1024 + 512),
                              start=(kk == 0), stop=False)
                        em.mm(pf(banks[2 * j + 1]), lh,
                              seg_ap(seg0 + SEGV, kk * 1024 + 512, (kk + 1) * 1024),
                              start=(kk == 0), stop=False)
                    em.mm(pf(banks[2 * j]), ones_row(128), A("rowpack")[0:1, 0:512],
                          start=False, stop=True)
                    em.mm(pf(banks[2 * j + 1]), ones_row(128), A("rowpack")[0:1, 512:1024],
                          start=False, stop=True)
                em.mark("tensor")
                mmk = em.cnt("ms_tensor")
                em.wait("scalar", "ms_tensor", mmk)
                for j, tg in enumerate(tgs):
                    em.act(A("vT")[:, tg * 1024:tg * 1024 + 512],
                           pf(banks[2 * j]), AF.Identity)
                    em.act(A("vT")[:, tg * 1024 + 512:(tg + 1) * 1024],
                           pf(banks[2 * j + 1]), AF.Identity)
                em.mark("scalar")
                wave_evac[banks[0] // 4] = em.cnt("ms_scalar")

            v_wave([0, 1], [0, 1, 2, 3])
            v_wave([2, 3], [4, 5, 6, 7])
            seg_done(seg0 + SEGK)
            seg_wait_pe(seg0 + SEGRQRO)
            prev = wave_evac.get(0)
            if prev is not None:
                em.wait("tensor", "ms_scalar", prev)
            for och in range(2):
                for kk in range(8):
                    em.mm(pf(och), seg_ap(seg0 + SEGRQRO, (och * 8 + kk) * 128,
                                          (och * 8 + kk + 1) * 128),
                          A("xhT")[:, kk * T:(kk + 1) * T], start=(kk == 0), stop=(kk == 7))
            em.mark("tensor")
            em.join("scalar", "tensor")
            for och in range(2):
                em.act(A("rqT")[:, och * T:(och + 1) * T], pf(och), AF.Identity,
                       bias=A("bcols")[:, 40 + och:41 + och], scale=0.125)
            em.mark("scalar")
            em.join("tensor", "scalar")
            seg_done(seg0 + SEGV)

            ck()
            ev_marks = {}

            def l_scores(g, m):
                par = m % 2
                key = ("exp", g * 8 + m - 2)
                if key in ev_marks:
                    em.wait("tensor", "ms_scalar", ev_marks[key])
                for i, h in enumerate((m, m + 8)):
                    och = h // 2
                    rlo = (h % 2) * 64
                    qsl = A("qT")[rlo:rlo + 64, och * T + g * 128:och * T + (g + 1) * 128]
                    dst = pf(par, i * 256, (i + 1) * 256)
                    if g == 0:
                        em.mm(pf(par, i * 256, i * 256 + 128), qsl,
                              A("kcarry")[rlo:rlo + 64, och * 128:(och + 1) * 128],
                              start=True, stop=True)
                        em.mm(pf(par, i * 256 + 128, i * 256 + 256), qsl,
                              A("kT")[rlo:rlo + 64, och * T:och * T + 128],
                              start=True, stop=True)
                    else:
                        em.mm(dst, qsl,
                              A("kT")[rlo:rlo + 64, och * T + (g - 1) * 128:och * T + (g + 1) * 128],
                              start=True, stop=True)
                em.mark("tensor")
                ev_marks[("sc", g * 8 + m)] = em.cnt("ms_tensor")

            def l_chain1(g, m):
                par = m % 2
                k = g * 8 + m
                first_g = (t == 0 and g == 0)
                pslot = (k % 4) * 512
                em.wait("vector", "ms_tensor", ev_marks[("sc", k)])
                em.red("vector", SC("nmax", 2 * par, 2),
                       ps_l[par].ap().rearrange("p (bk c) -> p bk c", c=256),
                       "max", neg=True)
                em.mark("vector")
                em.join("scalar", "vector")
                for i in range(2):
                    em.act(A("probsB")[:, pslot + i * 256:pslot + (i + 1) * 256],
                           pf(par, i * 256, (i + 1) * 256), AF.Exp,
                           bias=SC("nmax", i + 2 * par))
                em.mark("scalar")
                ev_marks[("exp", k)] = em.cnt("ms_scalar")
                em.join("gpsimd", "scalar")
                mofs = 0 if first_g else 256
                for i in range(2):
                    pr = A("probsB")[:, pslot + i * 256:pslot + (i + 1) * 256]
                    em.tt("gpsimd", pr, pr, A("binAB")[:, mofs:mofs + 256], ALU.mult)
                em.mark("gpsimd")
                ev_marks[("mask", k)] = em.cnt("ms_gpsimd")

            def l_chain2(g, m):
                par = m % 2
                k = g * 8 + m
                first_g = (t == 0 and g == 0)
                pslot = (k % 4) * 512
                em.wait("vector", "ms_gpsimd", ev_marks[("mask", k)])
                em.red("vector", SC("rsum", 2 * par, 2),
                       A("probsB").rearrange("p (bk c) -> p bk c", c=256)[:, 2 * (k % 4):2 * (k % 4) + 2],
                       "sum")
                if first_g:
                    em.ts("vector", SC("rsum", 2 * par, 2), SC("rsum", 2 * par, 2), 1e-30,
                          None, ALU.max)
                em.recip("vector", SC("rr", 2 * par, 2), SC("rsum", 2 * par, 2))
                for i in range(2):
                    pr = A("probsB")[:, pslot + i * 256:pslot + (i + 1) * 256]
                    em.ts("vector", pr, pr, SC("rr", i + 2 * par))
                em.mark("vector")
                ev_marks[("norm", k)] = em.cnt("ms_vector")

            def l_trans(g, m):
                par = m % 2
                k = g * 8 + m
                pslot = (k % 4) * 512
                em.wait("tensor", "ms_vector", ev_marks[("norm", k)])
                key = ("ptev", k - 1)
                if key in ev_marks:
                    em.wait("tensor", ev_marks[key][0], ev_marks[key][1])
                for i in range(2):
                    pr0 = A("probsB")[:, pslot + i * 256:pslot + i * 256 + 128]
                    pr1 = A("probsB")[:, pslot + i * 256 + 128:pslot + (i + 1) * 256]
                    em.trans(pb(2, par * 512 + i * 256, par * 512 + i * 256 + 128), pr0,
                             A("ident"))
                    em.trans(pb(2, par * 512 + i * 256 + 128, par * 512 + (i + 1) * 256), pr1,
                             A("ident"))
                em.mark("tensor")
                eng = "scalar" if k % 2 == 0 else "vector"
                em.join(eng, "tensor")
                if eng == "scalar":
                    em.act(A("probsTB")[:, pslot:pslot + 512],
                           pb(2, par * 512, par * 512 + 512), AF.Identity)
                else:
                    em.copy("vector", A("probsTB")[:, pslot:pslot + 512],
                            pb(2, par * 512, par * 512 + 512))
                em.mark(eng)
                ev_marks[("ptev", k)] = ("ms_" + eng, em.cnt("ms_" + eng))

            def l_pv(g, m):
                par = m % 2
                k = g * 8 + m
                pslot = (k % 4) * 512
                em.wait("tensor", ev_marks[("ptev", k)][0], ev_marks[("ptev", k)][1])
                key = ("lev", k - 1)
                if key in ev_marks:
                    em.wait("tensor", ev_marks[key][0], ev_marks[key][1])
                for i, h in enumerate((m, m + 8)):
                    vcol = h * 64
                    if g == 0:
                        vprev = A("vcarry")[:, vcol:vcol + 64]
                    else:
                        vprev = A("vT")[:, (g - 1) * 1024 + vcol:(g - 1) * 1024 + vcol + 64]
                    vcur = A("vT")[:, g * 1024 + vcol:g * 1024 + vcol + 64]
                    dst = pf(3, par * 256 + i * 128, par * 256 + (i + 1) * 128, 0, 64)
                    em.mm(dst, vprev, A("probsTB")[:, pslot + i * 256:pslot + i * 256 + 128],
                          start=True, stop=False)
                    em.mm(dst, vcur, A("probsTB")[:, pslot + i * 256 + 128:pslot + (i + 1) * 256],
                          start=False, stop=True)
                em.mark("tensor")
                pvm = em.cnt("ms_tensor")
                em.wait("scalar", "ms_tensor", pvm)
                for i, h in enumerate((m, m + 8)):
                    och = h // 2
                    rlo = (h % 2) * 64
                    em.act(A("localT")[rlo:rlo + 64, och * T + g * 128:och * T + (g + 1) * 128],
                           pf(3, par * 256 + i * 128, par * 256 + (i + 1) * 128, 0, 64),
                           AF.Identity)
                em.mark("scalar")
                ev_marks[("lev", k)] = ("ms_scalar", em.cnt("ms_scalar"))

            def lat_scores(tg):
                key = ("lexp", tg - 2)
                if key in ev_marks:
                    em.wait("tensor", "ms_scalar", ev_marks[key])
                for lhi in range(LH):
                    och = lhi // 2
                    rlo = (lhi % 2) * 64
                    em.mm(pf(4 + lhi % 2, (lhi // 2) * 64, (lhi // 2) * 64 + NLAT),
                          A("rqT")[rlo:rlo + 64, och * T + tg * 128:och * T + (tg + 1) * 128],
                          A("rkres")[rlo:rlo + 64, (b * 2 + och) * 64:(b * 2 + och) * 64 + NLAT],
                          start=True, stop=True)
                em.mark("tensor")
                ev_marks[("lsc", tg)] = em.cnt("ms_tensor")

            def lat_chain(tg):
                sti = t * QG + tg
                em.wait("vector", "ms_tensor", ev_marks[("lsc", tg)])
                for lhi in range(LH):
                    em.red("vector", SC("nmax", 4 + lhi),
                           pf(4 + lhi % 2, (lhi // 2) * 64, (lhi // 2) * 64 + NLAT),
                           "max", neg=True)
                em.mark("vector")
                em.join("scalar", "vector")
                for lhi in range(LH):
                    em.act(A("latpr")[:, lhi * 64:lhi * 64 + NLAT],
                           pf(4 + lhi % 2, (lhi // 2) * 64, (lhi // 2) * 64 + NLAT),
                           AF.Exp, bias=SC("nmax", 4 + lhi))
                em.mark("scalar")
                ev_marks[("lexp", tg)] = em.cnt("ms_scalar")
                em.join("gpsimd", "scalar")
                for lhi in range(LH):
                    pr = A("latpr")[:, lhi * 64:lhi * 64 + NLAT]
                    em.tt("gpsimd", pr, pr, A("latbin")[:, sti * NLAT:(sti + 1) * NLAT], ALU.mult)
                em.mark("gpsimd")
                em.join("vector", "gpsimd")
                for lhi in range(LH):
                    em.red("vector", SC("rsum", 4 + lhi), A("latpr")[:, lhi * 64:lhi * 64 + NLAT],
                           "sum")
                if tg == 0 and t == 0:
                    em.ts("vector", SC("rsum", 4, 4), SC("rsum", 4, 4), 1e-30, None, ALU.max)
                em.recip("vector", SC("rr", 4, 4), SC("rsum", 4, 4))
                for lhi in range(LH):
                    pr = A("latpr")[:, lhi * 64:lhi * 64 + NLAT]
                    em.ts("vector", pr, pr, SC("rr", 4 + lhi))
                em.mark("vector")
                ev_marks[("lnorm", tg)] = em.cnt("ms_vector")

            def lat_pv(tg):
                em.wait("tensor", "ms_vector", ev_marks[("lnorm", tg)])
                key = ("llev", tg - 1)
                if key in ev_marks:
                    em.wait("tensor", "ms_scalar", ev_marks[key])
                for lhi in range(LH):
                    em.trans(pb(6, lhi * 128, lhi * 128 + 128, 0, NLAT),
                             A("latpr")[:, lhi * 64:lhi * 64 + NLAT], A("ident"))
                em.mark("tensor")
                em.join("scalar", "tensor")
                for lhi in range(LH):
                    em.act(A("lattnT")[0:NLAT, lhi * 128:lhi * 128 + 128],
                           pb(6, lhi * 128, lhi * 128 + 128, 0, NLAT), AF.Identity)
                em.mark("scalar")
                em.join("tensor", "scalar")
                for lhi in range(LH):
                    rvoff = (b * 2 + lhi // 2) * 128 + (lhi % 2) * 64
                    em.mm(pf(7, lhi * 128, (lhi + 1) * 128, 0, 64),
                          A("rvres")[0:NLAT, rvoff:rvoff + 64],
                          A("lattnT")[0:NLAT, lhi * 128:(lhi + 1) * 128], start=True, stop=True)
                em.mark("tensor")
                lpv = em.cnt("ms_tensor")
                em.wait("scalar", "ms_tensor", lpv)
                for lhi in range(LH):
                    och = lhi // 2
                    rlo = (lhi % 2) * 64
                    em.act(A("latoT")[rlo:rlo + 64, och * T + tg * 128:och * T + (tg + 1) * 128],
                           pf(7, lhi * 128, (lhi + 1) * 128, 0, 64), AF.Identity)
                em.mark("scalar")
                ev_marks[("llev", tg)] = em.cnt("ms_scalar")

            NB = QG * 8

            def batch(k):
                return (k // 8, k % 8)

            for i in range(NB + 4):
                if i < NB:
                    l_scores(*batch(i))
                if 0 <= i - 1 < NB:
                    l_chain1(*batch(i - 1))
                if 0 <= i - 2 < NB:
                    l_chain2(*batch(i - 2))
                if 0 <= i - 3 < NB:
                    l_trans(*batch(i - 3))
                if 0 <= i - 4 < NB:
                    l_pv(*batch(i - 4))
                if i % 8 == 4 and i // 8 < QG:
                    lat_scores(i // 8)
                if i % 8 == 6 and i // 8 < QG:
                    lat_chain(i // 8)
                if i % 8 == 0 and 0 < i // 8 <= QG:
                    lat_pv(i // 8 - 1)

            em.join("gpsimd", "tensor")
            em.copy("gpsimd",
                    A("kcarry").rearrange("p (k c) -> p k c", k=8),
                    A("kT").rearrange("p (k t) -> p k t", k=8)[:, :, T - 128:T])
            em.copy("gpsimd", A("vcarry"), A("vT")[:, (QG - 1) * 1024:QG * 1024])
            em.mark("gpsimd")
            ck()
            em.join("tensor", "scalar")
            em.join("tensor", "vector")
            em.join("tensor", "gpsimd")
            seg_wait_pe(seg0 + SEGRQRO)
            for grp in range(2):
                for oc in range(grp * 4, grp * 4 + 4):
                    for L in range(2):
                        em.mm(pf(oc % 4),
                              seg_ap(seg0 + SEGRQRO, 2048 + (oc * 2 + L) * 128,
                                     2048 + (oc * 2 + L + 1) * 128),
                              A("latoT")[:, L * T:(L + 1) * T], start=(L == 0), stop=(L == 1))
                em.mark("tensor")
                em.join("scalar", "tensor")
                for oc in range(grp * 4, grp * 4 + 4):
                    em.act(A("remT")[:, oc * T:(oc + 1) * T], pf(oc % 4), AF.Identity,
                           bias=A("bcols")[:, 22 + oc:23 + oc])
                em.mark("scalar")
                em.join("tensor", "scalar")
            seg_done(seg0 + SEGRQRO)
            for cidx in range(8):
                em.mm(pf(4, 0, 512, 0, 16), A("glgr")[:, cidx * 16:(cidx + 1) * 16],
                      A("localT")[:, cidx * T:(cidx + 1) * T], start=(cidx == 0), stop=False)
            for cidx in range(8):
                em.mm(pf(4, 0, 512, 0, 16), A("glgr")[:, 128 + cidx * 16:128 + (cidx + 1) * 16],
                      A("remT")[:, cidx * T:(cidx + 1) * T], start=False, stop=(cidx == 7))
            em.mark("tensor")
            em.join("scalar", "tensor")
            em.act(A("gatesf")[0:16, 0:512], pf(4, 0, 512, 0, 16), AF.Exp, scale=-1.0)
            em.mark("scalar")
            em.join("vector", "scalar")
            em.ts("vector", A("gatesf")[0:16, 0:512], A("gatesf")[0:16, 0:512], 1.0, None, ALU.add)
            em.recip("vector", A("gatesf")[0:16, 0:512], A("gatesf")[0:16, 0:512])
            em.copy("vector", A("gatesb")[0:16, 0:512], A("gatesf")[0:16, 0:512])
            em.mark("vector")
            gate_mark = em.cnt("ms_vector")
            em.wait("tensor", "ms_vector", gate_mark)
            for cidx in range(8):
                gb = 5 + cidx % 2
                if cidx >= 2:
                    em.wait("tensor", "ms_vector", ev_marks.get(("mix", cidx - 2), 0))
                em.mm(pf(gb), A("selc")[0:16, cidx * 128:(cidx + 1) * 128],
                      A("gatesb")[0:16, 0:512], start=True, stop=True)
                em.mark("tensor")
                em.join("vector", "tensor")
                lc = A("localT")[:, cidx * T:(cidx + 1) * T]
                rc = A("remT")[:, cidx * T:(cidx + 1) * T]
                em.tt("vector", A("dtmp"), lc, rc, ALU.subtract)
                em.tt("vector", A("dtmp"), A("dtmp"), pf(gb), ALU.mult)
                em.tt("vector", A("mixT")[:, cidx * T:(cidx + 1) * T], A("dtmp"), rc, ALU.add)
                em.mark("vector")
                ev_marks[("mix", cidx)] = em.cnt("ms_vector")

            ck()
            seg_wait_pe(seg0 + SEGOW)
            em.join("tensor", "vector")
            em.wait("scalar", "s_out0", em.cnt("s_out0"))
            em.wait("scalar", "s_out1", em.cnt("s_out1"))
            mm_marks = {}
            for tg in range(QG):
                for cidx in range(8):
                    lh = A("mixT")[:, cidx * T + tg * 128:cidx * T + (tg + 1) * 128]
                    em.mm(pf(2 * tg), lh, seg_ap(seg0 + SEGOW, cidx * 1024, cidx * 1024 + 512),
                          start=(cidx == 0), stop=False)
                    em.mm(pf(2 * tg + 1), lh,
                          seg_ap(seg0 + SEGOW, cidx * 1024 + 512, (cidx + 1) * 1024),
                          start=(cidx == 0), stop=False)
                em.mm(pf(2 * tg), ones_row(128), A("rowpack")[0:1, 1024:1536],
                      start=False, stop=True)
                em.mm(pf(2 * tg + 1), ones_row(128), A("rowpack")[0:1, 1536:2048],
                      start=False, stop=True)
                em.mark("tensor")
                mm_marks[tg] = em.cnt("ms_tensor")
            seg_done(seg0 + SEGOW)
            for tg in range(QG):
                y1 = A("y1n")[:, tg * 1024:(tg + 1) * 1024]
                em.wait("vector", "ms_tensor", mm_marks[tg])
                em.tt("vector", y1[:, 0:512], pf(2 * tg), A("x")[:, tg * 1024:tg * 1024 + 512],
                      ALU.add)
                em.tt("vector", y1[:, 512:1024], pf(2 * tg + 1),
                      A("x")[:, tg * 1024 + 512:(tg + 1) * 1024], ALU.add)
                em.red("vector", SC("negmean", tg), y1, "sum", neg=True)
                em.ts("vector", SC("negmean", tg), SC("negmean", tg), 1.0 / 1024.0)
                em.mark("vector")
                mm_marks[("r", tg)] = em.cnt("ms_vector")
            for tg in range(QG):
                y1 = A("y1n")[:, tg * 1024:(tg + 1) * 1024]
                em.wait("scalar", "ms_vector", mm_marks[("r", tg)])
                em.act(A("outst"), y1, AF.Square, bias=SC("negmean", tg),
                       accum=SC("varp", tg))
                em.act(SC("rstd", tg), SC("varp", tg), AF.Ln, bias=A("stats")[:, 60:61], scale=1.0 / 1024.0)
                em.act(SC("rstd", tg), SC("rstd", tg), AF.Exp, scale=-0.5)
                em.mark("scalar")
                mm_marks[("s", tg)] = em.cnt("ms_scalar")
            for tg in range(QG):
                y1 = A("y1n")[:, tg * 1024:(tg + 1) * 1024]
                em.wait("vector", "ms_scalar", mm_marks[("s", tg)])
                em.ts("vector", y1, y1, SC("negmean", tg), SC("rstd", tg), ALU.add, ALU.mult)
                em.mark("vector")
            nb, ntile = (b, t + 1) if t + 1 < NT else (b + 1, 0)
            if nb < B:
                em.wait("sync", "ms_vector", em.cnt("ms_vector"))
                em.dma("sync", A("x").rearrange("p (g d) -> p g d", d=1024),
                       xin[nb][ntile * T:(ntile + 1) * T, :].rearrange("(g p) d -> p g d", p=128),
                       "s_x")
                x_loaded[0] = True

            ck()
            em.wait("vector", "s_out0", em.cnt("s_out0"))
            em.wait("vector", "s_out1", em.cnt("s_out1"))
            aq_marks = {}
            for tg in range(QG):
                y1 = A("y1n")[:, tg * 1024:(tg + 1) * 1024]
                em.red("vector", SC("rmax", tg), y1, "max", absval=True)
                em.ts("vector", SC("rmax", tg), SC("rmax", tg), 1e-5, None, ALU.max)
                em.recip("vector", SC("sx", tg), SC("rmax", tg))
                em.ts("vector", SC("sx", tg), SC("sx", tg), 127.0)
                em.ts("vector", SC("invsx", tg), SC("rmax", tg), 1.0 / 127.0)
                em.ts("vector", A("outst"), y1, SC("sx", tg), None, ALU.mult)
                em.ts("vector", A("aq1i"), A("outst"), A("mcols")[:, 7:8], None, ALU.add)
                em.ts("vector", A("xqTM")[:, tg * 1024:(tg + 1) * 1024], A("aq1i"),
                      -256.0, SC("invsx", tg), ALU.add, ALU.mult)
                em.mark("vector")
                aq_marks[tg] = em.cnt("ms_vector")
            for tg in range(QG):
                bk = 6 + tg % 2
                em.wait("tensor", "ms_vector", aq_marks[tg])
                if tg >= 2:
                    em.wait("tensor", "ms_scalar", aq_marks[("e", tg - 2)])
                for kk in range(8):
                    em.trans(pb(bk, kk * 128, (kk + 1) * 128),
                             A("xqTM")[:, tg * 1024 + kk * 128:tg * 1024 + (kk + 1) * 128],
                             A("ident"))
                em.mark("tensor")
                em.join("scalar", "tensor")
                dst = A("xqT").rearrange("p (k t) -> p k t", k=8)[:, :, tg * 128:(tg + 1) * 128]
                em.act(dst, pb(bk).rearrange("p (k c) -> p k c", k=8), AF.Identity)
                em.mark("scalar")
                aq_marks[("e", tg)] = em.cnt("ms_scalar")
            em.join("tensor", "scalar")
            em.table("silu_and_others")
            if tile_idx == 0:
                for s in ("s_twg0", "s_twg1"):
                    em.wait("gpsimd", s, em.cnt(s))
            for e in ("tensor", "vector", "scalar", "gpsimd"):
                for f in ("tensor", "vector", "scalar", "gpsimd"):
                    if e != f:
                        em.join(e, f)

            inner3 = A("inner").rearrange("p (c t) -> p t c", c=32)
            aq2_parts = {4: (0, 5, A("mpos")), 9: (5, 10, A("mneg")),
                         14: (10, 15, A("sqscr")[:, 0:512]),
                         19: (15, 20, A("sqscr")[:, 512:1024]),
                         23: (20, 24, A("sxbc")), 27: (24, 28, A("aq2q")),
                         31: (28, 32, A("aq2p"))}
            bank_free = {}
            sg_free = {}
            for og in range(64):
                segi = seg0 + GU0 + og // 8
                if og % 8 == 0:
                    seg_wait_pe(segi)
                bank = og % 8
                if og >= 8:
                    em.wait("tensor", "ms_vector" if (og - 8) % 2 == 1 else "ms_scalar",
                            bank_free[og - 8])
                for kk in range(8):
                    em.mm(pf(bank),
                          seg_ap(segi, ((og % 8) * 8 + kk) * 128, ((og % 8) * 8 + kk + 1) * 128),
                          A("xqT")[:, kk * T:(kk + 1) * T], start=(kk == 0), stop=(kk == 7))
                em.mark("tensor")
                if og % 8 == 7:
                    seg_done(segi)
                cpair = og // 2
                if og % 2 == 0:
                    em.join("scalar", "tensor")
                    if cpair >= 2:
                        em.wait("scalar", "ms_vector", sg_free[cpair - 2])
                    em.act(A("sg")[:, (cpair % 2) * 512:(cpair % 2 + 1) * 512], pf(bank),
                           AF.Silu, scale=A("mcols")[:, 0:1])
                    em.mark("scalar")
                    bank_free[og] = em.cnt("ms_scalar")
                else:
                    em.join("vector", "scalar")
                    em.join("vector", "tensor")
                    em.tt("vector", A("inner")[:, cpair * T:(cpair + 1) * T],
                          A("sg")[:, (cpair % 2) * 512:(cpair % 2 + 1) * 512], pf(bank), ALU.mult)
                    em.mark("vector")
                    bank_free[og] = em.cnt("ms_vector")
                    sg_free[cpair] = em.cnt("ms_vector")
                    if cpair in aq2_parts:
                        c0, c1, pdst = aq2_parts[cpair]
                        em.red("vector", pdst, inner3[:, :, c0:c1], "max", absval=True)
                        em.mark("vector")

            ck()
            for _, _, pdst in list(aq2_parts.values())[1:]:
                em.tt("vector", A("mpos"), A("mpos"), pdst, ALU.max)
            em.mark("vector")
            em.join("tensor", "vector")
            em.join("tensor", "scalar")
            for tg in range(QG):
                em.trans(pf(0, tg * 128, (tg + 1) * 128),
                         A("mpos")[:, tg * 128:(tg + 1) * 128], A("identf32"))
            em.mark("tensor")
            em.join("vector", "tensor")
            for tg in range(QG):
                em.red("vector", SC("rmax", tg), pf(0, tg * 128, (tg + 1) * 128), "max")
            for tg in range(QG):
                em.ts("vector", SC("rmax", tg), SC("rmax", tg), A("mcols")[:, 2:3], None, ALU.max)
                em.tt("vector", SC("scale2", tg), SC("rmax", tg), A("mcols")[:, 3:4], ALU.mult)
                em.recip("vector", SC("sx2", tg), SC("rmax", tg))
                em.ts("vector", SC("sx2", tg), SC("sx2", tg), 127.0)
            for tg in range(QG):
                em.copy("vector", SC("tmp", tg), SC("sx2", tg))
            em.mark("vector")
            em.join("tensor", "vector")
            for tg in range(QG):
                em.trans(pf(1, tg * 128, (tg + 1) * 128, 0, 1), SC("tmp", tg), A("identf32"))
            em.mark("tensor")
            em.join("vector", "tensor")
            em.copy("vector", A("sxrow")[0:1, 0:512], pf(1, 0, 512, 0, 1))
            em.mark("vector")
            em.join("tensor", "vector")
            em.mm(pf(2), A("rowpack")[0:1, 2048:2176], A("sxrow")[0:1, 0:512],
                  start=True, stop=True)
            em.mark("tensor")
            em.join("vector", "tensor")
            em.copy("vector", A("sxbc"), pf(2))
            em.mark("vector")

            ck()
            em.join("gpsimd", "vector")
            q2_marks = {}
            for kk in range(32):
                segi = seg0 + DN0 + kk // 8
                if kk % 8 == 0:
                    seg_wait_pe(segi)
                sl = kk % 2
                em.join("gpsimd", "vector")
                if kk >= 2:
                    em.wait("gpsimd", "ms_tensor", q2_marks[kk - 2])
                em.tt("gpsimd", A("sg")[:, sl * 512:(sl + 1) * 512],
                      A("inner")[:, kk * T:(kk + 1) * T], A("sxbc"), ALU.mult)
                em.mark("gpsimd")
                em.join("vector", "gpsimd")
                if kk >= 2:
                    em.wait("vector", "ms_tensor", q2_marks[kk - 2])
                em.ts("vector", A("q2i"), A("sg")[:, sl * 512:(sl + 1) * 512],
                      A("mcols")[:, 7:8], None, ALU.add)
                em.ts("vector", A("q2ch")[:, sl * 512:(sl + 1) * 512], A("q2i"), -256.0, None,
                      ALU.add)
                em.mark("vector")
                em.join("tensor", "vector")
                for tg in range(QG):
                    lh = A("q2ch")[:, sl * 512 + tg * 128:sl * 512 + (tg + 1) * 128]
                    em.mm(pf(2 * tg), lh, seg_ap(segi, (kk % 8) * 1024, (kk % 8) * 1024 + 512),
                          start=(kk == 0), stop=(kk == 31))
                    em.mm(pf(2 * tg + 1), lh,
                          seg_ap(segi, (kk % 8) * 1024 + 512, (kk % 8 + 1) * 1024),
                          start=(kk == 0), stop=(kk == 31))
                em.mark("tensor")
                q2_marks[kk] = em.cnt("ms_tensor")
                if kk % 8 == 7:
                    seg_done(segi)

            ck()
            em.join("vector", "tensor")
            l2m = {}
            for tg in range(QG):
                y2 = A("y2")[:, tg * 1024:(tg + 1) * 1024]
                em.stt("vector", y2[:, 0:512], pf(2 * tg), SC("scale2", tg),
                       A("y1n")[:, tg * 1024:tg * 1024 + 512], ALU.mult, ALU.add)
                em.stt("vector", y2[:, 512:1024], pf(2 * tg + 1), SC("scale2", tg),
                       A("y1n")[:, tg * 1024 + 512:(tg + 1) * 1024], ALU.mult, ALU.add)
                em.red("vector", SC("negmean", tg), y2, "sum", neg=True)
                em.ts("vector", SC("negmean", tg), SC("negmean", tg), 1.0 / 1024.0)
                em.mark("vector")
                l2m[tg] = em.cnt("ms_vector")
            if x_loaded[0]:
                em.join("vector", "gpsimd")
                em.wait("vector", "s_x", em.cnt("s_x"))
                cast_mh[0] = {}
                for c in range((QG + 3) // 4):
                    lo, hi = 4 * c * 1024, min(4 * c + 4, QG) * 1024
                    em.copy("vector", A("xbTM")[:, lo:hi], A("x")[:, lo:hi])
                    em.mark("vector")
                    cast_mh[0][c] = em.cnt("ms_vector")
                x_casted[0] = True
            em.wait("scalar", "s_out1", em.cnt("s_out1"))
            for tg in range(QG):
                y2 = A("y2")[:, tg * 1024:(tg + 1) * 1024]
                em.wait("scalar", "ms_vector", l2m[tg])
                em.act(A("sqscr"), y2, AF.Square, bias=SC("negmean", tg), accum=SC("varp", tg))
                em.mark("scalar")
                l2m[("s", tg)] = em.cnt("ms_scalar")
            for tg in range(QG):
                em.wait("vector", "ms_scalar", l2m[("s", tg)])
                em.ts("vector", SC("varp", tg), SC("varp", tg), 1.0 / 1024.0, 1e-5,
                      ALU.mult, ALU.add)
            em.mark("vector")
            if dump == "z":
                for tg in range(QG):
                    em.join("scalar", "vector")
                    em.dma("scalar", out_rows(b, t, tg),
                           A("y2")[:, tg * 1024:(tg + 1) * 1024], "s_out0")
                em.mark("scalar")
                em.join("vector", "scalar")
            em.join("scalar", "vector")
            prev_stats[0] = (b, t)

    except StopEmission:
        pass
    if stop is not None:
        em.wait("sync", "s_out0", em.cnt("s_out0"))
        em.wait("sync", "s_out1", em.cnt("s_out1"))
        with contextlib.ExitStack() as bes:
            sems = {}
            for name in em.counts:
                sems[name] = bes.enter_context(nc.semaphore(name))
            block = bes.enter_context(nc.Block())
            em.replay(block, sems)
        return nc
    em.table("natural_log_exp_and_others")
    pb_, pt_ = prev_stats[0]
    em.join("vector", "gpsimd")
    for tg in range(QG):
        em.act(SC("rstd", tg), SC("varp", tg), AF.Ln)
        em.act(SC("rstd", tg), SC("rstd", tg), AF.Exp, scale=-0.5)
    em.mark("scalar")
    em.join("vector", "scalar")
    outdma = {}
    for tg in range(QG):
        ob = A("outst") if tg % 2 == 0 else A("outst2")
        sem = "s_out%d" % (tg % 2)
        if tg >= 2:
            em.wait("vector", sem, outdma[tg - 2])
        em.ts("vector", ob, A("y2")[:, tg * 1024:(tg + 1) * 1024],
              SC("negmean", tg), SC("rstd", tg), ALU.add, ALU.mult)
        em.mark("vector")
        em.join("scalar", "vector")
        if dump is None:
            em.dma("scalar", out_rows(pb_, pt_, tg), ob, sem)
        outdma[tg] = em.cnt(sem)

    em.wait("sync", "s_out0", em.cnt("s_out0"))
    em.wait("sync", "s_out1", em.cnt("s_out1"))

    with contextlib.ExitStack() as bes:
        sems = {}
        for name in em.counts:
            sems[name] = bes.enter_context(nc.semaphore(name))
        block = bes.enter_context(nc.Block())
        em.replay(block, sems)

    return nc



def kernel(**inputs):
    cfg = make_cfg()
    n_cores = 8
    nc = build_program(cfg)
    in_maps = [prep_core_inputs(ci, n_cores, cfg, inputs) for ci in range(n_cores)]
    res = run_bass_kernel_spmd(nc, in_maps, core_ids=list(range(n_cores)))
    out = np.concatenate([np.asarray(r["yout"]) for r in res.results], axis=0)
    return out.astype(np.float32)


# version 15
# speedup vs baseline: 1.1557x; 1.1009x over previous
import contextlib
import numpy as np
import ml_dtypes
import concourse.bass as bass
import concourse.mybir as mybir
from concourse.bass_utils import run_bass_kernel_spmd

F32 = mybir.dt.float32
BF16 = mybir.dt.bfloat16
I32 = mybir.dt.int32
AF = mybir.ActivationFunctionType
ALU = mybir.AluOpType
AX = mybir.AxisListType
BF = ml_dtypes.bfloat16

D = 1024
H = 16
HD = 64
LH = 4
LHD = 64
LD = 256
CH = 32
INNER = 4096
WIN = 128

SEG_EL = 8192
RING_SLOTS = 2


def make_cfg(B=2, S=2048):
    T = 512 if S >= 512 else S
    return dict(B=B, S=S, T=T, NT=S // T, QG=T // 128, NLAT=S // CH, STI=S // 128)



def build_consts(cfg):
    T, STI, NLAT = cfg["T"], cfg["STI"], cfg["NLAT"]
    c = {}
    c["ident"] = np.eye(128, dtype=BF)
    c["identf32"] = np.eye(128, dtype=np.float32)
    qi = np.arange(128)[:, None]
    kj = np.arange(256)[None, :]
    binB = ((kj >= qi) & (kj <= qi + 127)).astype(BF)
    binA = ((kj >= np.maximum(qi, 128)) & (kj <= qi + 127)).astype(BF)
    c["binB"] = binB
    c["binA"] = binA
    lat = np.zeros((128, STI * NLAT), dtype=BF)
    for sti in range(STI):
        p = np.arange(128)[:, None]
        l = np.arange(NLAT)[None, :]
        lat[:, sti * NLAT:(sti + 1) * NLAT] = (l < (sti * 4 + p // 32)).astype(BF)
    c["latbin"] = lat
    P4 = np.zeros((128, 4), dtype=np.float32)
    P4[np.arange(128), np.arange(128) // 32] = 1.0 / 32.0
    c["P4"] = P4.astype(BF)
    sel = np.zeros((16, 8 * 128), dtype=BF)
    for ci in range(8):
        for p in range(128):
            sel[2 * ci + p // 64, ci * 128 + p] = 1.0
    c["sel"] = sel
    return c



def _pack_ws(w, och_count, k_count):
    d_in, d_out = w.shape
    assert d_in == k_count * 128 and d_out == och_count * 128
    t = w.reshape(k_count, 128, och_count, 128)
    t = np.transpose(t, (1, 2, 0, 3))
    return np.ascontiguousarray(t.reshape(128, och_count * k_count * 128))


def _pack_rhs(w, k_count):
    d_in, d_out = w.shape
    assert d_in == k_count * 128
    t = w.reshape(k_count, 128, d_out)
    t = np.transpose(t, (1, 0, 2))
    return np.ascontiguousarray(t.reshape(128, k_count * d_out))


def prep_core_inputs(ci, n_cores, cfg, inp):
    B, S = cfg["B"], cfg["S"]
    x = np.ascontiguousarray(inp["L0_attn_x"][ci * B:(ci + 1) * B])
    qkv_w = np.asarray(inp["L0_attn_qkv_w"])
    m = {}
    m["xin"] = x
    m["tq"] = _pack_ws(qkv_w[:, 0:1024], 8, 8)
    m["tk"] = _pack_ws(qkv_w[:, 1024:2048], 8, 8)
    m["tv"] = _pack_rhs(qkv_w[:, 2048:3072], 8)
    m["trq"] = _pack_ws(np.asarray(inp["L0_attn_rq_w"]), 2, 8)
    m["trout"] = _pack_ws(np.asarray(inp["L0_attn_rout_w"]), 8, 2)
    m["tow"] = _pack_rhs(np.asarray(inp["L0_attn_out_w"]), 8)
    m["trkv"] = _pack_ws(np.asarray(inp["L0_attn_rkv_w"]), 4, 8)
    wg = np.ascontiguousarray(np.asarray(inp["L0_ffn_w_gate"]).T)
    wu = np.ascontiguousarray(np.asarray(inp["L0_ffn_w_up"]).T)
    wd = np.ascontiguousarray(np.asarray(inp["L0_ffn_w_down"]).T)
    gw = wg.reshape(8, 128, 32, 128)
    uw = wu.reshape(8, 128, 32, 128)
    gu = np.empty((128, 64 * 8 * 128), dtype=np.float32)
    for og in range(64):
        src = gw if og % 2 == 0 else uw
        oc = og // 2
        blk = np.transpose(src[:, :, oc, :], (1, 0, 2)).reshape(128, 8 * 128)
        gu[:, og * 1024:(og + 1) * 1024] = blk
    m["tgu"] = np.ascontiguousarray(gu)
    m["tdn"] = _pack_rhs(wd, 32)
    qkv_b = np.asarray(inp["L0_attn_qkv_b"])
    m["bcol_qk"] = np.ascontiguousarray(qkv_b[0:2048].reshape(16, 128).T)
    m["bcol_rq"] = np.ascontiguousarray(np.asarray(inp["L0_attn_rq_b"]).reshape(2, 128).T)
    m["bcol_rkv"] = np.ascontiguousarray(np.asarray(inp["L0_attn_rkv_b"]).reshape(4, 128).T)
    m["bcol_rout"] = np.ascontiguousarray(np.asarray(inp["L0_attn_rout_b"]).reshape(8, 128).T)
    m["brow_v"] = np.ascontiguousarray(qkv_b[2048:3072].reshape(1, 1024))
    m["brow_out"] = np.ascontiguousarray(np.asarray(inp["L0_attn_out_b"]).reshape(1, 1024))
    m["gl"] = np.ascontiguousarray(np.asarray(inp["L0_attn_gl_w"]).reshape(64, 1))
    m["gr"] = np.ascontiguousarray(np.asarray(inp["L0_attn_gr_w"]).reshape(64, 1))
    return m



ENGS = ("sync", "tensor", "vector", "scalar", "gpsimd")


class Em:
    def __init__(self):
        self.q = {e: [] for e in ENGS}
        self.written = {e: set() for e in ENGS}
        self.readden = {e: set() for e in ENGS}
        self.counts = {}
        self.ranges = {}

    def _overlap(self, s1, s2):
        for a in s1:
            for b in s2:
                if a == b:
                    return True
                ra, rb = self.ranges.get(a), self.ranges.get(b)
                if ra and rb and ra[0] < rb[1] and rb[0] < ra[1]:
                    return True
        return False

    def _names(self, aps):
        out = set()
        for a in aps:
            if a is None or isinstance(a, (int, float)):
                continue
            out.add(a.tensor.name)
        return out

    def op(self, eng, fn, reads=(), writes=()):
        if getattr(self, "frozen", False):
            return
        if eng in ("vector", "scalar", "gpsimd"):
            rd = self._names(reads)
            wr = self._names(writes)
            if (self._overlap(wr, self.written[eng] | self.readden[eng])
                    or self._overlap(rd, self.written[eng])):
                self.q[eng].append(("drain",))
                self.written[eng] = set()
                self.readden[eng] = set()
            self.written[eng] |= wr
            self.readden[eng] |= rd
        self.q[eng].append(("op", fn))

    def inc(self, eng, sem, n=1):
        if getattr(self, "frozen", False):
            return self.counts.get(sem, 0)
        self.counts[sem] = self.counts.get(sem, 0) + n
        q = self.q[eng]
        i = len(q) - 1
        while i >= 0 and q[i][0] == "wait":
            i -= 1
        if i >= 0 and q[i][0] == "inc":
            q.append(("op", lambda e: e.nop(nofuse=True)))
        q.append(("inc", sem, n))
        return self.counts[sem]

    def wait(self, eng, sem, n=None):
        if getattr(self, "frozen", False):
            return
        if n is None:
            n = self.counts.get(sem, 0)
        if n > 0:
            self.q[eng].append(("wait", sem, n))

    def mark(self, eng):
        q = self.q[eng]
        i = len(q) - 1
        while i >= 0 and q[i][0] == "wait":
            i -= 1
        if i >= 0 and q[i][0] == "inc" and q[i][1] == "ms_" + eng:
            return self.counts["ms_" + eng]
        return self.inc(eng, "ms_" + eng, 1)

    def join(self, eng, src):
        self.wait(eng, "ms_" + src)

    def cnt(self, sem):
        return self.counts.get(sem, 0)

    def tt(self, eng, out, a, b, aop):
        self.op(eng, lambda e: e.tensor_tensor(out, a, b, aop), reads=[a, b], writes=[out])

    def ts(self, eng, out, a, s1, s2=None, op0=ALU.mult, op1=ALU.bypass):
        rd = [a] + [x for x in (s1, s2) if hasattr(x, "tensor")]
        self.op(eng, lambda e: e.tensor_scalar(out, a, s1, s2, op0, op1), reads=rd, writes=[out])

    def stt(self, eng, out, in0, scalar, in1, op0, op1):
        rd = [in0, in1] + ([scalar] if hasattr(scalar, "tensor") else [])
        self.op(eng, lambda e: e.scalar_tensor_tensor(out, in0, scalar, in1, op0, op1),
                reads=rd, writes=[out])

    def copy(self, eng, out, a):
        self.op(eng, lambda e: e.tensor_copy(out, a), reads=[a], writes=[out])

    def memset(self, eng, ap, v):
        self.op(eng, lambda e: e.memset(ap, v), reads=[], writes=[ap])

    def red(self, eng, out, a, rop, absval=False, neg=False):
        if rop == "max":
            self.op(eng, lambda e: e.reduce_max(out, a, AX.X, apply_absolute_value=absval,
                                                negate=neg), reads=[a], writes=[out])
        else:
            self.op(eng, lambda e: e.reduce_sum(out, a, AX.X, apply_absolute_value=absval,
                                                negate=neg), reads=[a], writes=[out])

    def recip(self, eng, out, a):
        self.op(eng, lambda e: e.reciprocal(out, a), reads=[a], writes=[out])

    def act(self, out, a, func, bias=0.0, scale=1.0, accum=None):
        rd = [a] + [x for x in (bias, scale) if hasattr(x, "tensor")]
        wr = [out] + ([accum] if accum is not None else [])
        self.op("scalar", lambda e: e.activation(out, a, func, bias=bias, scale=scale,
                                                 accum_out=accum), reads=rd, writes=wr)

    def mm(self, out, lhsT, rhs, start, stop):
        self.op("tensor", lambda e: e.matmul(out, lhsT, rhs, start=start, stop=stop))

    def trans(self, out, a, ident):
        self.op("tensor", lambda e: e.transpose(out, a, ident))

    def dma(self, eng, out, a, sem):
        self.op(eng, lambda e: e.dma_start(out=out, in_=a), reads=[a], writes=[out])
        return self.inc(eng, sem, 16)

    def table(self, set_name):
        if getattr(self, "frozen", False):
            return
        def fn(e):
            from concourse.hw_specs import get_activation_tables
            for idx, (nm, funcs) in enumerate(get_activation_tables(e.bass.m.arch).items()):
                if nm == set_name:
                    return e.add_instruction(mybir.InstLoadActFuncSet(
                        name=e.bass.get_next_instruction_name(), act_func_set_id=idx,
                        ins=[], outs=[]))
            raise ValueError(set_name)
        self.q["scalar"].append(("op", fn))

    def replay(self, block, sems):
        def section(engname):
            def body(engobj):
                last = None
                for item in self.q[engname]:
                    k = item[0]
                    if k == "op":
                        last = item[1](engobj)
                    elif k == "inc":
                        assert last is not None, f"inc with no prior op on {engname}"
                        last.then_inc(sems[item[1]], item[2])
                    elif k == "wait":
                        engobj.wait_ge(sems[item[1]], item[2])
                    elif k == "drain":
                        engobj.drain()
            return body
        block.sync(section("sync"))
        block.tensor(section("tensor"))
        block.vector(section("vector"))
        block.scalar(section("scalar"))
        block.gpsimd(section("gpsimd"))



class StopEmission(Exception):
    pass


def build_program(cfg, stop=None, dump=None):
    B, S, T, NT, QG, NLAT, STI = (cfg["B"], cfg["S"], cfg["T"], cfg["NT"], cfg["QG"],
                                  cfg["NLAT"], cfg["STI"])
    NSEQT = S // 128
    consts = build_consts(cfg)

    nc = bass.Bass("TRN2", target_bir_lowering=False)

    xin = nc.dram_tensor("xin", [B, S, D], F32, kind="ExternalInput").ap()
    yout = nc.dram_tensor("yout", [B, S, D], F32, kind="ExternalOutput").ap()
    hv = {}
    for nm, shape in [("tq", [128, 8192]), ("tk", [128, 8192]), ("tv", [128, 8192]),
                      ("trq", [128, 2048]), ("trout", [128, 2048]), ("tow", [128, 8192]),
                      ("trkv", [128, 4096]), ("tgu", [128, 65536]), ("tdn", [128, 32768]),
                      ("bcol_qk", [128, 16]), ("bcol_rq", [128, 2]), ("bcol_rkv", [128, 4]),
                      ("bcol_rout", [128, 8]), ("brow_v", [1, 1024]), ("brow_out", [1, 1024]),
                      ("gl", [64, 1]), ("gr", [64, 1])]:
        hv[nm] = nc.dram_tensor(nm, shape, F32, kind="ExternalInput").ap()
    cin = {k: nc.inline_tensor(v, "c_" + k).ap() for k, v in consts.items()}
    NSEG_TILE = 5 + 8 + 4
    tape = nc.dram_tensor("tape", [128, NSEG_TILE * SEG_EL], BF16, kind="Internal").ap()

    es = contextlib.ExitStack()
    sb = {}

    def alloc(name, shape, dtype):
        sb[name] = nc.alloc_sbuf_tensor(name, shape, dtype)
        return sb[name]

    alloc("x", [128, 4096], F32)
    alloc("y1n", [128, 4096], F32)
    alloc("y2", [128, 4096], F32)
    alloc("xqT", [128, 8 * T], BF16)
    alloc("kcarry", [128, 1024], BF16)
    alloc("vcarry", [128, 1024], BF16)
    alloc("ring", [128, RING_SLOTS * SEG_EL], BF16)
    alloc("rowpack", [128, 2176], F32)
    alloc("ident", [128, 128], BF16)
    alloc("identf32", [128, 128], F32)
    alloc("binAB", [128, 512], BF16)
    alloc("latbin", [128, STI * NLAT], BF16)
    alloc("P4", [128, 4], BF16)
    alloc("selc", [16, 1024], BF16)
    alloc("glgr", [128, 256], BF16)
    alloc("gl2", [128, 2], F32)
    alloc("rkres", [128, B * 2 * LHD], BF16)
    alloc("rvres", [64, B * LH * LHD], BF16)
    alloc("bcols", [128, 64], F32)
    alloc("stats", [128, 64], F32)
    alloc("mcols", [128, 8], F32)
    alloc("sc1", [1, 16], F32)
    alloc("outst", [128, 1024], F32)
    alloc("onescol", [128, 1], F32)

    AR = {}
    em = Em()
    ar_base = [None]

    def ar(name, shape, dtype, off):
        t = nc.alloc_sbuf_tensor_at(name, shape, dtype, offset=ar_base[0] + off)
        sb[name] = t
        esz = {F32: 4, BF16: 2, I32: 4}[dtype]
        em.ranges[t.name] = (off, off + shape[1] * esz)
        return t


    T2 = 2 * T
    T8 = 8 * T
    ARENA_BYTES = 0x17A00
    ar_base[0] = (nc.sbuf_base + 63) // 64 * 64
    assert nc.sbuf_top - ar_base[0] >= ARENA_BYTES, (
        f"SBUF over budget: base {nc.sbuf_base} top {nc.sbuf_top}")

    ar("xbTM", [128, 4096], BF16, 0x0000)
    ar("xhT", [128, T8], BF16, 0x2000)
    ar("qT", [128, T8], BF16, 0x4000)
    ar("kT", [128, T8], BF16, 0x6000)
    ar("vT", [128, 4096], BF16, 0x8000)
    ar("rqT", [128, T2], BF16, 0xA000)
    ar("probsB", [128, 2048], BF16, 0xA800)
    ar("probsTB", [128, 2048], BF16, 0xB800)
    ar("localT", [128, T8], BF16, 0xC800)
    ar("latoT", [128, T2], BF16, 0xE800)
    ar("latpr", [128, 256], BF16, 0xF000)
    ar("lattnT", [64, 512], BF16, 0xF200)
    ar("remT", [128, T8], BF16, 0xF600)
    ar("mixT", [128, T8], BF16, 0x11600)
    ar("dtmp", [128, 512], F32, 0x13600)
    ar("gatesf", [128, 512], F32, 0x13E00)
    ar("gatesb", [128, 512], BF16, 0x14600)
    ar("aq1i", [128, 1024], I32, 0x10000)
    ar("xqTM", [128, 4096], BF16, 0x15A00)
    ar("inner", [128, 32 * T], F32, 0x0000)
    ar("sg", [128, 1024], F32, 0x10000)
    ar("q2i", [128, 512], I32, 0x11000)
    ar("q2ch", [128, 1024], BF16, 0x11800)
    ar("mpos", [128, 512], F32, 0x12000)
    ar("mneg", [128, 512], F32, 0x12800)
    ar("sxbc", [128, 512], F32, 0x13000)
    ar("sxrow", [1, 512], F32, 0x13800)
    ar("sqscr", [128, 1024], F32, 0x14000)
    ar("aq2p", [128, 512], F32, 0x15000)
    ar("aq2q", [128, 512], F32, 0x11000)
    ar("outst2", [128, 1024], F32, 0x16A00)
    ar("wslab", [128, 8192], F32, 0x0000)
    ar("oslab", [128, 8192], BF16, 0x8000)
    ar("islab", [128, 4096], I32, 0xC000)
    ar("xbP", [128, 4096], BF16, 0x10000)
    ar("pooledP", [128, 512], BF16, 0x12000)
    ar("acc", [128, 8], F32, 0x12400)
    ar("abscr", [128, 1024], F32, 0x12500)
    ar("rvtmp", [128, 128], BF16, 0x13600)
    ar("w1slab", [128, 4096], F32, 0x13800)

    ps_l = [nc.alloc_psum_tensor("ps%d" % i, [128, 512], F32) for i in range(8)]
    psb_l = [tt.bitcast(BF16) for tt in ps_l]

    def pf(i, lo=0, hi=512, p0=0, p1=128):
        return ps_l[i].ap()[p0:p1, lo:hi]

    def pb(i, lo=0, hi=1024, p0=0, p1=128):
        return psb_l[i].ap()[p0:p1, lo:hi]

    A = lambda n: sb[n].ap()
    stage_no = [0]

    def ck():
        stage_no[0] += 1
        if stop is not None and stage_no[0] >= stop:
            em.frozen = True

    ones_row = lambda w: A("rowpack")[0:1, 2048:2048 + w]

    ring_state = dict(next_load=0, loaded_off={}, done_at={}, load_sem={})

    def seg_load(idx):
        slot = idx % RING_SLOTS
        prev = idx - RING_SLOTS
        if prev >= 0:
            em.wait("sync", "ms_tensor", ring_state["done_at"][prev])
        for sem, cntv in ring_state["tape_w"].get(idx % NSEG_TILE, []):
            em.wait("sync", sem, cntv)
        w = 4096 if (idx % NSEG_TILE) == 3 else SEG_EL
        dst = A("ring")[:, slot * SEG_EL:slot * SEG_EL + w]
        src = tape[:, (idx % NSEG_TILE) * SEG_EL:(idx % NSEG_TILE) * SEG_EL + w]
        n = em.dma("sync", dst, src, "s_tape%d" % slot)
        ring_state["load_sem"][idx] = ("s_tape%d" % slot, n)
        ring_state["loaded_off"][idx] = slot * SEG_EL

    def seg_ap(idx, lo, hi):
        off = ring_state["loaded_off"][idx]
        return A("ring")[:, off + lo:off + hi]

    def seg_wait_pe(idx):
        sem, n = ring_state["load_sem"][idx]
        em.wait("tensor", sem, n)

    def seg_done(idx):
        em.mark("tensor")
        ring_state["done_at"][idx] = em.cnt("ms_tensor")
        nl = ring_state["next_load"]
        if nl == idx + RING_SLOTS:
            pass
        want = idx + RING_SLOTS
        if want < ring_state["total"]:
            seg_load(want)
            ring_state["next_load"] = want + 1

    em.dma("sync", A("ident"), cin["ident"], "s_c")
    em.dma("sync", A("identf32"), cin["identf32"], "s_c")
    em.dma("sync", A("binAB")[:, 0:256], cin["binA"], "s_c")
    em.dma("sync", A("binAB")[:, 256:512], cin["binB"], "s_c")
    em.dma("sync", A("latbin"), cin["latbin"], "s_c")
    em.dma("sync", A("P4"), cin["P4"], "s_c")
    em.dma("sync", A("selc")[0:16, 0:1024], cin["sel"], "s_c")
    em.dma("sync", A("bcols")[:, 0:16], hv["bcol_qk"], "s_c")
    em.dma("sync", A("bcols")[:, 16:18], hv["bcol_rq"], "s_c")
    em.dma("sync", A("bcols")[:, 18:22], hv["bcol_rkv"], "s_c")
    em.dma("sync", A("bcols")[:, 22:30], hv["bcol_rout"], "s_c")
    em.dma("sync", A("rowpack")[0:1, 0:1024], hv["brow_v"], "s_c")
    em.dma("sync", A("rowpack")[0:1, 1024:2048], hv["brow_out"], "s_c")
    em.dma("sync", A("gl2")[0:64, 0:1], hv["gl"], "s_c")
    em.dma("sync", A("gl2")[64:128, 0:1], hv["gl"], "s_c")
    em.dma("sync", A("gl2")[0:64, 1:2], hv["gr"], "s_c")
    em.dma("sync", A("gl2")[64:128, 1:2], hv["gr"], "s_c")
    n_c = em.cnt("s_c")

    ck()
    em.table("natural_log_exp_and_others")

    em.wait("vector", "s_c", n_c)
    em.memset("vector", A("rowpack")[0:1, 2048:2176], 1.0)
    em.memset("vector", A("onescol"), 1.0)
    em.memset("vector", A("stats")[:, 60:61], 1e-5)
    em.memset("vector", A("stats")[:, 62:63], 0.2)
    em.ts("vector", A("aq1i")[:, 0:1], A("stats")[:, 62:63], 1.0, 256.5, ALU.mult, ALU.add)
    em.ts("vector", A("stats")[:, 63:64], A("aq1i")[:, 0:1], -256.0, None, ALU.add)
    em.ts("vector", A("mcols")[:, 7:8], A("stats")[:, 63:64], -0.5, 256.5, ALU.mult, ALU.add)
    em.ts("vector", A("bcols")[:, 32:40], A("bcols")[:, 0:8], 0.125)
    em.ts("vector", A("bcols")[:, 40:42], A("bcols")[:, 16:18], 0.125)
    em.memset("vector", A("glgr"), 0.0)
    for cidx in range(8):
        em.copy("vector", A("glgr")[0:64, cidx * 16 + 2 * cidx:cidx * 16 + 2 * cidx + 1],
                A("gl2")[0:64, 0:1])
        em.copy("vector", A("glgr")[64:128, cidx * 16 + 2 * cidx + 1:cidx * 16 + 2 * cidx + 2],
                A("gl2")[64:128, 0:1])
        em.copy("vector", A("glgr")[0:64, 128 + cidx * 16 + 2 * cidx:128 + cidx * 16 + 2 * cidx + 1],
                A("gl2")[0:64, 1:2])
        em.copy("vector",
                A("glgr")[64:128, 128 + cidx * 16 + 2 * cidx + 1:128 + cidx * 16 + 2 * cidx + 2],
                A("gl2")[64:128, 1:2])
    em.mark("vector")

    ck()
    ring_state["tape_w"] = {}
    SEGQ, SEGK, SEGV, SEGRQRO, SEGOW = 0, 1, 2, 3, 4
    tape_steps = []
    for seg, nm, dsto, tot in [(SEGQ, "tq", 0, 8192), (SEGK, "tk", 0, 8192),
                               (SEGV, "tv", 0, 8192), (SEGRQRO, "trq", 0, 2048),
                               (SEGRQRO, "trout", 2048, 2048), (SEGOW, "tow", 0, 8192)]:
        for half in range(0, tot, 4096):
            tape_steps.append((seg, nm, dsto, half, min(4096, tot - half)))
    cast_m = {}
    tw_m = {}

    def tape_step(si):
        seg, nm, dsto, half, w = tape_steps[si]
        sl = si % 2
        wap = A("wslab")[:, sl * 4096:sl * 4096 + w]
        oap = A("oslab")[:, sl * 4096:sl * 4096 + w]
        if si >= 2:
            em.wait("sync", "ms_vector", cast_m[si - 2])
        nl = em.dma("sync", wap, hv[nm][:, half:half + w], "s_wp%d" % sl)
        em.wait("vector", "s_wp%d" % sl, nl)
        if si >= 2:
            em.wait("vector", "s_tw%d" % sl, tw_m[si - 2])
        em.copy("vector", oap, wap)
        em.mark("vector")
        cast_m[si] = em.cnt("ms_vector")
        em.wait("sync", "ms_vector", cast_m[si])
        em.dma("sync", tape[:, seg * SEG_EL + dsto + half:seg * SEG_EL + dsto + half + w],
               oap, "s_tw%d" % sl)
        tw_m[si] = em.cnt("s_tw%d" % sl)
        ring_state["tape_w"].setdefault(seg, []).append(("s_tw%d" % sl, tw_m[si]))

    em.memset("vector", A("acc")[:, 0:8], 0.0)
    em.memset("vector", A("rvtmp"), 0.0)
    em.mark("vector")
    p1_steps = []
    for nm, nsl in (("tgu", 32), ("tdn", 16)):
        for sidx in range(nsl):
            p1_steps.append((nm, sidx))
    abs_m = {}
    addm = {}

    def p1_step(si):
        nm, sidx = p1_steps[si]
        sl = si % 2
        wap = A("w1slab")[:, sl * 2048:(sl + 1) * 2048]
        if si >= 2:
            em.wait("gpsimd", "ms_scalar", abs_m[si - 2])
        nl = em.dma("gpsimd", wap, hv[nm][:, sidx * 2048:(sidx + 1) * 2048], "s_gp%d" % sl)
        em.wait("scalar", "s_gp%d" % sl, nl)
        if si >= 1:
            em.wait("scalar", "ms_vector", addm.get(si - 1, 0))
        for j in range(2):
            em.act(A("abscr"), wap[:, j * 1024:(j + 1) * 1024], AF.Abs,
                   accum=A("stats")[:, 56 + 2 * sl + j:57 + 2 * sl + j])
        em.mark("scalar")
        abs_m[si] = em.cnt("ms_scalar")
        em.wait("vector", "ms_scalar", abs_m[si])
        for j in range(2):
            if nm == "tgu":
                og = sidx * 2 + j
                colidx = 0 if og % 2 == 0 else 1
            else:
                colidx = 6
            em.tt("vector", A("acc")[:, colidx:colidx + 1], A("acc")[:, colidx:colidx + 1],
                  A("stats")[:, 56 + 2 * sl + j:57 + 2 * sl + j], ALU.add)
        em.mark("vector")
        addm[si] = em.cnt("ms_vector")

    for si in range(len(p1_steps)):
        if si < len(tape_steps):
            tape_step(si)
        p1_step(si)
    em.mark("vector")

    em.join("tensor", "vector")
    for i, colidx in enumerate((0, 1, 6)):
        em.mm(pf(5, i, i + 1, 0, 1), A("acc")[:, colidx:colidx + 1],
              A("onescol"), start=True, stop=True)
    em.mark("tensor")
    em.join("vector", "tensor")
    NW = 4096 * 1024
    em.ts("vector", A("sc1")[0:1, 0:3], pf(5, 0, 3, 0, 1), 1.0 / NW, 1e-5, ALU.mult, ALU.max)
    em.recip("vector", A("sc1")[0:1, 4:7], A("sc1")[0:1, 0:3])
    em.ts("vector", A("sc1")[0:1, 8:9], A("sc1")[0:1, 5:6], 1e-5)
    em.tt("vector", A("sc1")[0:1, 9:10], A("sc1")[0:1, 2:3], A("sc1")[0:1, 1:2], ALU.mult)
    em.ts("vector", A("sc1")[0:1, 9:10], A("sc1")[0:1, 9:10], 1.0 / 127.0)
    em.mark("vector")
    em.join("tensor", "vector")
    bc_list = [(0, 0), (1, 1), (2, 8), (3, 9), (4, 4), (5, 5), (6, 6)]
    for mcol, scol in bc_list:
        em.mm(pf(6, mcol, mcol + 1), A("rowpack")[0:1, 2048:2176],
              A("sc1")[0:1, scol:scol + 1], start=True, stop=True)
    em.mark("tensor")
    em.join("vector", "tensor")
    em.copy("vector", A("mcols")[:, 0:7], pf(6, 0, 7))
    em.mark("vector")

    GU0 = 5
    DN0 = 13
    q2_m = {}
    wq_m = {}

    def p2_step(si):
        nm, sidx = p1_steps[si]
        sl = si % 2
        wap = A("wslab")[:, sl * 2048:(sl + 1) * 2048]
        oap = A("oslab")[:, sl * 2048:(sl + 1) * 2048]
        if si >= 2:
            em.wait("sync", "ms_vector", q2_m[si - 2])
        else:
            em.wait("sync", "ms_vector", em.cnt("ms_vector"))
            em.wait("sync", "s_gp%d" % sl, em.cnt("s_gp%d" % sl))
        nl = em.dma("sync", wap, hv[nm][:, sidx * 2048:(sidx + 1) * 2048], "s_g2p%d" % sl)
        em.wait("vector", "s_g2p%d" % sl, nl)
        if si >= 2:
            em.wait("vector", "s_twg%d" % sl, wq_m[si - 2])
        else:
            em.wait("vector", "s_tw0", em.cnt("s_tw0"))
            em.wait("vector", "s_tw1", em.cnt("s_tw1"))
        for j in range(2):
            if nm == "tgu":
                og = sidx * 2 + j
                sw = A("mcols")[:, 4:5] if og % 2 == 0 else A("mcols")[:, 5:6]
            else:
                sw = A("mcols")[:, 6:7]
            blk = wap[:, j * 1024:(j + 1) * 1024]
            fb = sb["islab"].bitcast(F32).ap()[:, (sl * 2 + j) * 1024:(sl * 2 + j + 1) * 1024]
            em.ts("vector", blk, blk, sw, None, ALU.mult)
            em.ts("vector", fb, blk, 0.5, None, ALU.is_gt)
            em.ts("vector", blk, blk, -0.5, None, ALU.is_lt)
            em.tt("vector", oap[:, j * 1024:(j + 1) * 1024], fb, blk, ALU.subtract)
        em.mark("vector")
        q2_m[si] = em.cnt("ms_vector")
        em.wait("scalar", "ms_vector", q2_m[si])
        if nm == "tgu":
            seg = GU0 + sidx // 4
            off = (sidx % 4) * 2048
        else:
            seg = DN0 + sidx // 4
            off = (sidx % 4) * 2048
        em.dma("scalar", tape[:, seg * SEG_EL + off:seg * SEG_EL + off + 2048], oap,
               "s_twg%d" % sl)
        wq_m[si] = em.cnt("s_twg%d" % sl)
        ring_state["tape_w"].setdefault(seg, []).append(("s_twg%d" % sl, wq_m[si]))

    rkvb = sb["w1slab"].bitcast(BF16).ap()
    em.wait("gpsimd", "ms_scalar", em.cnt("ms_scalar"))
    nl = em.dma("gpsimd", rkvb[:, 0:4096], hv["trkv"], "s_gp0")
    rkv_ld = nl
    p2i = [0]

    def pump2(n=2):
        for _ in range(n):
            if p2i[0] < len(p1_steps):
                p2_step(p2i[0])
                p2i[0] += 1

    for b in range(B):
        ck()
        pump2()
        if b == 0:
            em.join("tensor", "vector")
        ngr = (S // 128 + 3) // 4
        for grp in range(ngr):
            t0 = grp * 4
            ntl = min(4, S // 128 - t0)
            em.wait("gpsimd", "ms_tensor", em.cnt("ms_tensor"))
            nl = em.dma("gpsimd", A("xbP").rearrange("p (g d) -> p g d", d=1024)[:, 0:ntl],
                        xin[b][t0 * 128:(t0 + ntl) * 128, :].rearrange("(g p) d -> p g d", p=128),
                        "s_xp")
            em.wait("tensor", "s_xp", nl)
            for tl in range(ntl):
                tg = t0 + tl
                for kk in range(8):
                    em.mm(pf(kk, tg * 4, tg * 4 + 4),
                          A("xbP")[:, tl * 1024 + kk * 128:tl * 1024 + (kk + 1) * 128],
                          A("P4"), start=True, stop=True)
            em.mark("tensor")
            pump2()
        ck()
        em.join("scalar", "tensor")
        for kk in range(8):
            em.act(A("pooledP")[:, kk * NLAT:(kk + 1) * NLAT], pf(kk, 0, NLAT), AF.Identity)
        em.mark("scalar")
        ck()
        em.join("tensor", "scalar")
        em.wait("tensor", "s_gp0", rkv_ld)
        for och in range(4):
            for kk in range(8):
                em.mm(pf(och, 0, NLAT),
                      rkvb[:, (och * 8 + kk) * 128:(och * 8 + kk + 1) * 128],
                      A("pooledP")[:, kk * NLAT:(kk + 1) * NLAT], start=(kk == 0), stop=(kk == 7))
        em.mark("tensor")
        ck()
        pump2()
        em.join("scalar", "tensor")
        for och in range(2):
            em.act(A("rkres")[:, (b * 2 + och) * 64:(b * 2 + och) * 64 + NLAT],
                   pf(och, 0, NLAT), AF.Identity, bias=A("bcols")[:, 18 + och:19 + och])
        for och in range(2, 4):
            em.act(A("rvtmp")[:, 0:NLAT] if och == 2 else A("rvtmp")[:, 64:64 + NLAT],
                   pf(och, 0, NLAT), AF.Identity, bias=A("bcols")[:, 18 + och:19 + och])
        em.mark("scalar")
        ck()
        em.join("tensor", "scalar")
        for half in range(2):
            em.trans(pb(4, half * 256, half * 256 + 128, 0, 64),
                     A("rvtmp")[:, half * 64:(half + 1) * 64], A("ident"))
        em.mark("tensor")
        ck()
        em.join("scalar", "tensor")
        for half in range(2):
            em.act(A("rvres")[0:NLAT, (b * 2 + half) * 128:(b * 2 + half + 1) * 128],
                   pb(4, half * 256, half * 256 + 128, 0, NLAT), AF.Identity)
        em.mark("scalar")
        ck()
        em.join("tensor", "scalar")
        pump2()
    pump2(len(p1_steps))
    ck()
    ring_state["total"] = NSEG_TILE * B * NT
    ring_state["done_at"] = {}
    seg_load(0)
    seg_load(1)
    ring_state["next_load"] = 2

    x_loaded = [False]
    x_casted = [False]
    cast_mh = [None]
    prev_stats = [None]

    def out_rows(b, t, tg):
        return yout[b][t * T + tg * 128:t * T + (tg + 1) * 128, :]

    stc = {"nmax": 0, "rsum": 8, "rr": 16, "negmean": 24, "varp": 28, "rstd": 32,
           "rmax": 36, "sx": 40, "invsx": 44, "scale2": 48, "sx2": 52, "tmp": 56}

    def SC(name, i, w=1):
        return A("stats")[:, stc[name] + i:stc[name] + i + w]

    try:
      for b in range(B):
        for t in range(NT):
            tile_idx = b * NT + t
            seg0 = tile_idx * NSEG_TILE

            em.table("natural_log_exp_and_others")

            if not x_loaded[0]:
                em.wait("sync", "ms_gpsimd", em.cnt("ms_gpsimd"))
                nl = em.dma("sync", A("x").rearrange("p (g d) -> p g d", d=1024),
                            xin[b][t * T:(t + 1) * T, :].rearrange("(g p) d -> p g d", p=128),
                            "s_x")
            x_loaded[0] = False
            if not x_casted[0]:
                em.wait("vector", "s_x", em.cnt("s_x"))
                if tile_idx == 0:
                    for s in ("s_twg0", "s_twg1", "s_gp0", "s_gp1", "s_g2p0", "s_g2p1",
                              "s_wp0", "s_wp1", "s_xp"):
                        em.wait("vector", s, em.cnt(s))
                cast_mh[0] = {}
                for c in range((QG + 3) // 4):
                    lo, hi = 4 * c * 1024, min(4 * c + 4, QG) * 1024
                    em.copy("vector", A("xbTM")[:, lo:hi], A("x")[:, lo:hi])
                    em.mark("vector")
                    cast_mh[0][c] = em.cnt("ms_vector")
            x_casted[0] = False
            cast_m = cast_mh[0]

            if t == 0:
                em.join("gpsimd", "tensor")
                em.memset("gpsimd", A("kcarry"), 0.0)
                em.memset("gpsimd", A("vcarry"), 0.0)
                em.mark("gpsimd")
                em.join("tensor", "gpsimd")

            ck()
            em.join("tensor", "vector")
            a2m = {}
            for tg in range(QG):
                bk = 6 + tg % 2
                em.wait("tensor", "ms_vector", cast_m[tg // 4])
                if tg >= 2:
                    em.wait("tensor", "ms_scalar", a2m[tg - 2])
                for kk in range(8):
                    em.trans(pb(bk, kk * 128, (kk + 1) * 128),
                             A("xbTM")[:, tg * 1024 + kk * 128:tg * 1024 + (kk + 1) * 128],
                             A("ident"))
                em.mark("tensor")
                em.join("scalar", "tensor")
                dst = A("xhT").rearrange("p (k t) -> p k t", k=8)[:, :, tg * 128:(tg + 1) * 128]
                em.act(dst, pb(bk).rearrange("p (k c) -> p k c", k=8), AF.Identity)
                em.mark("scalar")
                a2m[tg] = em.cnt("ms_scalar")
            em.join("tensor", "scalar")

            ck()
            if prev_stats[0] is not None:
                pb_, pt_ = prev_stats[0]
                em.join("vector", "gpsimd")
                for tg in range(QG):
                    em.act(SC("rstd", tg), SC("varp", tg), AF.Ln)
                    em.act(SC("rstd", tg), SC("rstd", tg), AF.Exp, scale=-0.5)
                em.mark("scalar")
                em.join("vector", "scalar")
                outdma = {}
                for tg in range(QG):
                    ob = A("outst") if tg % 2 == 0 else A("outst2")
                    sem = "s_out%d" % (tg % 2)
                    if tg >= 2:
                        em.wait("vector", sem, outdma[tg - 2])
                    em.ts("vector", ob, A("y2")[:, tg * 1024:(tg + 1) * 1024],
                          SC("negmean", tg), SC("rstd", tg), ALU.add, ALU.mult)
                    em.mark("vector")
                    em.join("scalar", "vector")
                    if dump is None:
                        em.dma("scalar", out_rows(pb_, pt_, tg), ob, sem)
                    outdma[tg] = em.cnt(sem)

            seg_wait_pe(seg0 + SEGQ)
            wave_evac = {}

            def proj_wave(name, segi, ochs, banks, dst, dstoff, bias0, scale):
                prev = wave_evac.get(banks[0] // 4)
                if prev is not None:
                    em.wait("tensor", "ms_scalar", prev)
                for j, och in enumerate(ochs):
                    for kk in range(8):
                        em.mm(pf(banks[j]),
                              seg_ap(segi, (och * 8 + kk) * 128, (och * 8 + kk + 1) * 128),
                              A("xhT")[:, kk * T:(kk + 1) * T], start=(kk == 0), stop=(kk == 7))
                em.mark("tensor")
                mmk = em.cnt("ms_tensor")
                em.wait("scalar", "ms_tensor", mmk)
                for j, och in enumerate(ochs):
                    em.act(A(dst)[:, (dstoff + och) * T:(dstoff + och + 1) * T], pf(banks[j]),
                           AF.Identity, bias=A("bcols")[:, bias0 + och:bias0 + och + 1],
                           scale=scale)
                em.mark("scalar")
                wave_evac[banks[0] // 4] = em.cnt("ms_scalar")

            proj_wave("qT", seg0 + SEGQ, range(0, 4), [0, 1, 2, 3], "qT", 0, 32, 0.125)
            proj_wave("qT", seg0 + SEGQ, range(4, 8), [4, 5, 6, 7], "qT", 0, 32, 0.125)
            seg_wait_pe(seg0 + SEGK)
            proj_wave("kT", seg0 + SEGK, range(0, 4), [0, 1, 2, 3], "kT", 0, 8, 1.0)
            proj_wave("kT", seg0 + SEGK, range(4, 8), [4, 5, 6, 7], "kT", 0, 8, 1.0)
            seg_done(seg0 + SEGQ)
            seg_wait_pe(seg0 + SEGV)

            def v_wave(tgs, banks):
                prev = wave_evac.get(banks[0] // 4)
                if prev is not None:
                    em.wait("tensor", "ms_scalar", prev)
                for j, tg in enumerate(tgs):
                    for kk in range(8):
                        lh = A("xhT")[:, kk * T + tg * 128:kk * T + (tg + 1) * 128]
                        em.mm(pf(banks[2 * j]), lh,
                              seg_ap(seg0 + SEGV, kk * 1024, kk * 1024 + 512),
                              start=(kk == 0), stop=False)
                        em.mm(pf(banks[2 * j + 1]), lh,
                              seg_ap(seg0 + SEGV, kk * 1024 + 512, (kk + 1) * 1024),
                              start=(kk == 0), stop=False)
                    em.mm(pf(banks[2 * j]), ones_row(128), A("rowpack")[0:1, 0:512],
                          start=False, stop=True)
                    em.mm(pf(banks[2 * j + 1]), ones_row(128), A("rowpack")[0:1, 512:1024],
                          start=False, stop=True)
                em.mark("tensor")
                mmk = em.cnt("ms_tensor")
                em.wait("scalar", "ms_tensor", mmk)
                for j, tg in enumerate(tgs):
                    em.act(A("vT")[:, tg * 1024:tg * 1024 + 512],
                           pf(banks[2 * j]), AF.Identity)
                    em.act(A("vT")[:, tg * 1024 + 512:(tg + 1) * 1024],
                           pf(banks[2 * j + 1]), AF.Identity)
                em.mark("scalar")
                wave_evac[banks[0] // 4] = em.cnt("ms_scalar")

            v_wave([0, 1], [0, 1, 2, 3])
            v_wave([2, 3], [4, 5, 6, 7])
            seg_done(seg0 + SEGK)
            seg_wait_pe(seg0 + SEGRQRO)
            prev = wave_evac.get(0)
            if prev is not None:
                em.wait("tensor", "ms_scalar", prev)
            for och in range(2):
                for kk in range(8):
                    em.mm(pf(och), seg_ap(seg0 + SEGRQRO, (och * 8 + kk) * 128,
                                          (och * 8 + kk + 1) * 128),
                          A("xhT")[:, kk * T:(kk + 1) * T], start=(kk == 0), stop=(kk == 7))
            em.mark("tensor")
            em.join("scalar", "tensor")
            for och in range(2):
                em.act(A("rqT")[:, och * T:(och + 1) * T], pf(och), AF.Identity,
                       bias=A("bcols")[:, 40 + och:41 + och], scale=0.125)
            em.mark("scalar")
            em.join("tensor", "scalar")
            seg_done(seg0 + SEGV)

            ck()
            ev_marks = {}

            def l_scores(g, m):
                par = m % 2
                key = ("exp", g * 8 + m - 2)
                if key in ev_marks:
                    em.wait("tensor", "ms_scalar", ev_marks[key])
                for i, h in enumerate((m, m + 8)):
                    och = h // 2
                    rlo = (h % 2) * 64
                    qsl = A("qT")[rlo:rlo + 64, och * T + g * 128:och * T + (g + 1) * 128]
                    dst = pf(par, i * 256, (i + 1) * 256)
                    if g == 0:
                        em.mm(pf(par, i * 256, i * 256 + 128), qsl,
                              A("kcarry")[rlo:rlo + 64, och * 128:(och + 1) * 128],
                              start=True, stop=True)
                        em.mm(pf(par, i * 256 + 128, i * 256 + 256), qsl,
                              A("kT")[rlo:rlo + 64, och * T:och * T + 128],
                              start=True, stop=True)
                    else:
                        em.mm(dst, qsl,
                              A("kT")[rlo:rlo + 64, och * T + (g - 1) * 128:och * T + (g + 1) * 128],
                              start=True, stop=True)
                em.mark("tensor")
                ev_marks[("sc", g * 8 + m)] = em.cnt("ms_tensor")

            def l_chain1(g, m):
                par = m % 2
                k = g * 8 + m
                first_g = (t == 0 and g == 0)
                pslot = (k % 4) * 512
                em.wait("vector", "ms_tensor", ev_marks[("sc", k)])
                em.red("vector", SC("nmax", 2 * par, 2),
                       ps_l[par].ap().rearrange("p (bk c) -> p bk c", c=256),
                       "max", neg=True)
                em.mark("vector")
                em.join("scalar", "vector")
                for i in range(2):
                    em.act(A("probsB")[:, pslot + i * 256:pslot + (i + 1) * 256],
                           pf(par, i * 256, (i + 1) * 256), AF.Exp,
                           bias=SC("nmax", i + 2 * par))
                em.mark("scalar")
                ev_marks[("exp", k)] = em.cnt("ms_scalar")
                em.join("gpsimd", "scalar")
                mofs = 0 if first_g else 256
                for i in range(2):
                    pr = A("probsB")[:, pslot + i * 256:pslot + (i + 1) * 256]
                    em.tt("gpsimd", pr, pr, A("binAB")[:, mofs:mofs + 256], ALU.mult)
                em.mark("gpsimd")
                ev_marks[("mask", k)] = em.cnt("ms_gpsimd")

            def l_chain2(g, m):
                par = m % 2
                k = g * 8 + m
                first_g = (t == 0 and g == 0)
                pslot = (k % 4) * 512
                em.wait("vector", "ms_gpsimd", ev_marks[("mask", k)])
                em.red("vector", SC("rsum", 2 * par, 2),
                       A("probsB").rearrange("p (bk c) -> p bk c", c=256)[:, 2 * (k % 4):2 * (k % 4) + 2],
                       "sum")
                if first_g:
                    em.ts("vector", SC("rsum", 2 * par, 2), SC("rsum", 2 * par, 2), 1e-30,
                          None, ALU.max)
                em.recip("vector", SC("rr", 2 * par, 2), SC("rsum", 2 * par, 2))
                for i in range(2):
                    pr = A("probsB")[:, pslot + i * 256:pslot + (i + 1) * 256]
                    em.ts("vector", pr, pr, SC("rr", i + 2 * par))
                em.mark("vector")
                ev_marks[("norm", k)] = em.cnt("ms_vector")

            def l_trans(g, m):
                par = m % 2
                k = g * 8 + m
                pslot = (k % 4) * 512
                em.wait("tensor", "ms_vector", ev_marks[("norm", k)])
                key = ("ptev", k - 1)
                if key in ev_marks:
                    em.wait("tensor", ev_marks[key][0], ev_marks[key][1])
                for i in range(2):
                    pr0 = A("probsB")[:, pslot + i * 256:pslot + i * 256 + 128]
                    pr1 = A("probsB")[:, pslot + i * 256 + 128:pslot + (i + 1) * 256]
                    em.trans(pb(2, par * 512 + i * 256, par * 512 + i * 256 + 128), pr0,
                             A("ident"))
                    em.trans(pb(2, par * 512 + i * 256 + 128, par * 512 + (i + 1) * 256), pr1,
                             A("ident"))
                em.mark("tensor")
                eng = "scalar" if k % 2 == 0 else "vector"
                em.join(eng, "tensor")
                if eng == "scalar":
                    em.act(A("probsTB")[:, pslot:pslot + 512],
                           pb(2, par * 512, par * 512 + 512), AF.Identity)
                else:
                    em.copy("vector", A("probsTB")[:, pslot:pslot + 512],
                            pb(2, par * 512, par * 512 + 512))
                em.mark(eng)
                ev_marks[("ptev", k)] = ("ms_" + eng, em.cnt("ms_" + eng))

            def l_pv(g, m):
                par = m % 2
                k = g * 8 + m
                pslot = (k % 4) * 512
                em.wait("tensor", ev_marks[("ptev", k)][0], ev_marks[("ptev", k)][1])
                key = ("lev", k - 1)
                if key in ev_marks:
                    em.wait("tensor", ev_marks[key][0], ev_marks[key][1])
                for i, h in enumerate((m, m + 8)):
                    vcol = h * 64
                    if g == 0:
                        vprev = A("vcarry")[:, vcol:vcol + 64]
                    else:
                        vprev = A("vT")[:, (g - 1) * 1024 + vcol:(g - 1) * 1024 + vcol + 64]
                    vcur = A("vT")[:, g * 1024 + vcol:g * 1024 + vcol + 64]
                    dst = pf(3, par * 256 + i * 128, par * 256 + (i + 1) * 128, 0, 64)
                    em.mm(dst, vprev, A("probsTB")[:, pslot + i * 256:pslot + i * 256 + 128],
                          start=True, stop=False)
                    em.mm(dst, vcur, A("probsTB")[:, pslot + i * 256 + 128:pslot + (i + 1) * 256],
                          start=False, stop=True)
                em.mark("tensor")
                pvm = em.cnt("ms_tensor")
                em.wait("scalar", "ms_tensor", pvm)
                for i, h in enumerate((m, m + 8)):
                    och = h // 2
                    rlo = (h % 2) * 64
                    em.act(A("localT")[rlo:rlo + 64, och * T + g * 128:och * T + (g + 1) * 128],
                           pf(3, par * 256 + i * 128, par * 256 + (i + 1) * 128, 0, 64),
                           AF.Identity)
                em.mark("scalar")
                ev_marks[("lev", k)] = ("ms_scalar", em.cnt("ms_scalar"))

            def lat_scores(tg):
                key = ("lexp", tg - 2)
                if key in ev_marks:
                    em.wait("tensor", "ms_scalar", ev_marks[key])
                for lhi in range(LH):
                    och = lhi // 2
                    rlo = (lhi % 2) * 64
                    em.mm(pf(4 + lhi % 2, (lhi // 2) * 64, (lhi // 2) * 64 + NLAT),
                          A("rqT")[rlo:rlo + 64, och * T + tg * 128:och * T + (tg + 1) * 128],
                          A("rkres")[rlo:rlo + 64, (b * 2 + och) * 64:(b * 2 + och) * 64 + NLAT],
                          start=True, stop=True)
                em.mark("tensor")
                ev_marks[("lsc", tg)] = em.cnt("ms_tensor")

            def lat_chain(tg):
                sti = t * QG + tg
                em.wait("vector", "ms_tensor", ev_marks[("lsc", tg)])
                for lhi in range(LH):
                    em.red("vector", SC("nmax", 4 + lhi),
                           pf(4 + lhi % 2, (lhi // 2) * 64, (lhi // 2) * 64 + NLAT),
                           "max", neg=True)
                em.mark("vector")
                em.join("scalar", "vector")
                for lhi in range(LH):
                    em.act(A("latpr")[:, lhi * 64:lhi * 64 + NLAT],
                           pf(4 + lhi % 2, (lhi // 2) * 64, (lhi // 2) * 64 + NLAT),
                           AF.Exp, bias=SC("nmax", 4 + lhi))
                em.mark("scalar")
                ev_marks[("lexp", tg)] = em.cnt("ms_scalar")
                em.join("gpsimd", "scalar")
                for lhi in range(LH):
                    pr = A("latpr")[:, lhi * 64:lhi * 64 + NLAT]
                    em.tt("gpsimd", pr, pr, A("latbin")[:, sti * NLAT:(sti + 1) * NLAT], ALU.mult)
                em.mark("gpsimd")
                em.join("vector", "gpsimd")
                for lhi in range(LH):
                    em.red("vector", SC("rsum", 4 + lhi), A("latpr")[:, lhi * 64:lhi * 64 + NLAT],
                           "sum")
                if tg == 0 and t == 0:
                    em.ts("vector", SC("rsum", 4, 4), SC("rsum", 4, 4), 1e-30, None, ALU.max)
                em.recip("vector", SC("rr", 4, 4), SC("rsum", 4, 4))
                for lhi in range(LH):
                    pr = A("latpr")[:, lhi * 64:lhi * 64 + NLAT]
                    em.ts("vector", pr, pr, SC("rr", 4 + lhi))
                em.mark("vector")
                ev_marks[("lnorm", tg)] = em.cnt("ms_vector")

            def lat_pv(tg):
                em.wait("tensor", "ms_vector", ev_marks[("lnorm", tg)])
                key = ("llev", tg - 1)
                if key in ev_marks:
                    em.wait("tensor", "ms_scalar", ev_marks[key])
                for lhi in range(LH):
                    em.trans(pb(6, lhi * 128, lhi * 128 + 128, 0, NLAT),
                             A("latpr")[:, lhi * 64:lhi * 64 + NLAT], A("ident"))
                em.mark("tensor")
                em.join("scalar", "tensor")
                for lhi in range(LH):
                    em.act(A("lattnT")[0:NLAT, lhi * 128:lhi * 128 + 128],
                           pb(6, lhi * 128, lhi * 128 + 128, 0, NLAT), AF.Identity)
                em.mark("scalar")
                em.join("tensor", "scalar")
                for lhi in range(LH):
                    rvoff = (b * 2 + lhi // 2) * 128 + (lhi % 2) * 64
                    em.mm(pf(7, lhi * 128, (lhi + 1) * 128, 0, 64),
                          A("rvres")[0:NLAT, rvoff:rvoff + 64],
                          A("lattnT")[0:NLAT, lhi * 128:(lhi + 1) * 128], start=True, stop=True)
                em.mark("tensor")
                lpv = em.cnt("ms_tensor")
                em.wait("scalar", "ms_tensor", lpv)
                for lhi in range(LH):
                    och = lhi // 2
                    rlo = (lhi % 2) * 64
                    em.act(A("latoT")[rlo:rlo + 64, och * T + tg * 128:och * T + (tg + 1) * 128],
                           pf(7, lhi * 128, (lhi + 1) * 128, 0, 64), AF.Identity)
                em.mark("scalar")
                ev_marks[("llev", tg)] = em.cnt("ms_scalar")

            NB = QG * 8

            def batch(k):
                return (k // 8, k % 8)

            for i in range(NB + 4):
                if i < NB:
                    l_scores(*batch(i))
                if 0 <= i - 1 < NB:
                    l_chain1(*batch(i - 1))
                if 0 <= i - 2 < NB:
                    l_chain2(*batch(i - 2))
                if 0 <= i - 3 < NB:
                    l_trans(*batch(i - 3))
                if 0 <= i - 4 < NB:
                    l_pv(*batch(i - 4))
                if i % 8 == 4 and i // 8 < QG:
                    lat_scores(i // 8)
                if i % 8 == 6 and i // 8 < QG:
                    lat_chain(i // 8)
                if i % 8 == 0 and 0 < i // 8 <= QG:
                    lat_pv(i // 8 - 1)

            em.join("gpsimd", "tensor")
            em.copy("gpsimd",
                    A("kcarry").rearrange("p (k c) -> p k c", k=8),
                    A("kT").rearrange("p (k t) -> p k t", k=8)[:, :, T - 128:T])
            em.copy("gpsimd", A("vcarry"), A("vT")[:, (QG - 1) * 1024:QG * 1024])
            em.mark("gpsimd")
            ck()
            em.join("tensor", "scalar")
            em.join("tensor", "vector")
            em.join("tensor", "gpsimd")
            seg_wait_pe(seg0 + SEGRQRO)
            for grp in range(2):
                for oc in range(grp * 4, grp * 4 + 4):
                    for L in range(2):
                        em.mm(pf(oc % 4),
                              seg_ap(seg0 + SEGRQRO, 2048 + (oc * 2 + L) * 128,
                                     2048 + (oc * 2 + L + 1) * 128),
                              A("latoT")[:, L * T:(L + 1) * T], start=(L == 0), stop=(L == 1))
                em.mark("tensor")
                em.join("scalar", "tensor")
                for oc in range(grp * 4, grp * 4 + 4):
                    em.act(A("remT")[:, oc * T:(oc + 1) * T], pf(oc % 4), AF.Identity,
                           bias=A("bcols")[:, 22 + oc:23 + oc])
                em.mark("scalar")
                em.join("tensor", "scalar")
            seg_done(seg0 + SEGRQRO)
            for cidx in range(8):
                em.mm(pf(4, 0, 512, 0, 16), A("glgr")[:, cidx * 16:(cidx + 1) * 16],
                      A("localT")[:, cidx * T:(cidx + 1) * T], start=(cidx == 0), stop=False)
            for cidx in range(8):
                em.mm(pf(4, 0, 512, 0, 16), A("glgr")[:, 128 + cidx * 16:128 + (cidx + 1) * 16],
                      A("remT")[:, cidx * T:(cidx + 1) * T], start=False, stop=(cidx == 7))
            em.mark("tensor")
            em.join("scalar", "tensor")
            em.act(A("gatesf")[0:16, 0:512], pf(4, 0, 512, 0, 16), AF.Exp, scale=-1.0)
            em.mark("scalar")
            em.join("vector", "scalar")
            em.ts("vector", A("gatesf")[0:16, 0:512], A("gatesf")[0:16, 0:512], 1.0, None, ALU.add)
            em.recip("vector", A("gatesf")[0:16, 0:512], A("gatesf")[0:16, 0:512])
            em.copy("vector", A("gatesb")[0:16, 0:512], A("gatesf")[0:16, 0:512])
            em.mark("vector")
            gate_mark = em.cnt("ms_vector")
            em.wait("tensor", "ms_vector", gate_mark)
            for cidx in range(8):
                gb = 5 + cidx % 2
                if cidx >= 2:
                    em.wait("tensor", "ms_vector", ev_marks.get(("mix", cidx - 2), 0))
                em.mm(pf(gb), A("selc")[0:16, cidx * 128:(cidx + 1) * 128],
                      A("gatesb")[0:16, 0:512], start=True, stop=True)
                em.mark("tensor")
                em.join("vector", "tensor")
                lc = A("localT")[:, cidx * T:(cidx + 1) * T]
                rc = A("remT")[:, cidx * T:(cidx + 1) * T]
                em.tt("vector", A("dtmp"), lc, rc, ALU.subtract)
                em.tt("vector", A("dtmp"), A("dtmp"), pf(gb), ALU.mult)
                em.tt("vector", A("mixT")[:, cidx * T:(cidx + 1) * T], A("dtmp"), rc, ALU.add)
                em.mark("vector")
                ev_marks[("mix", cidx)] = em.cnt("ms_vector")

            ck()
            seg_wait_pe(seg0 + SEGOW)
            em.join("tensor", "vector")
            em.wait("scalar", "s_out0", em.cnt("s_out0"))
            em.wait("scalar", "s_out1", em.cnt("s_out1"))
            mm_marks = {}
            for tg in range(QG):
                for cidx in range(8):
                    lh = A("mixT")[:, cidx * T + tg * 128:cidx * T + (tg + 1) * 128]
                    em.mm(pf(2 * tg), lh, seg_ap(seg0 + SEGOW, cidx * 1024, cidx * 1024 + 512),
                          start=(cidx == 0), stop=False)
                    em.mm(pf(2 * tg + 1), lh,
                          seg_ap(seg0 + SEGOW, cidx * 1024 + 512, (cidx + 1) * 1024),
                          start=(cidx == 0), stop=False)
                em.mm(pf(2 * tg), ones_row(128), A("rowpack")[0:1, 1024:1536],
                      start=False, stop=True)
                em.mm(pf(2 * tg + 1), ones_row(128), A("rowpack")[0:1, 1536:2048],
                      start=False, stop=True)
                em.mark("tensor")
                mm_marks[tg] = em.cnt("ms_tensor")
            seg_done(seg0 + SEGOW)
            for tg in range(QG):
                y1 = A("y1n")[:, tg * 1024:(tg + 1) * 1024]
                em.wait("vector", "ms_tensor", mm_marks[tg])
                em.tt("vector", y1[:, 0:512], pf(2 * tg), A("x")[:, tg * 1024:tg * 1024 + 512],
                      ALU.add)
                em.tt("vector", y1[:, 512:1024], pf(2 * tg + 1),
                      A("x")[:, tg * 1024 + 512:(tg + 1) * 1024], ALU.add)
                em.red("vector", SC("negmean", tg), y1, "sum", neg=True)
                em.ts("vector", SC("negmean", tg), SC("negmean", tg), 1.0 / 1024.0)
                em.mark("vector")
                mm_marks[("r", tg)] = em.cnt("ms_vector")
            for tg in range(QG):
                y1 = A("y1n")[:, tg * 1024:(tg + 1) * 1024]
                em.wait("scalar", "ms_vector", mm_marks[("r", tg)])
                em.act(A("outst"), y1, AF.Square, bias=SC("negmean", tg),
                       accum=SC("varp", tg))
                em.act(SC("rstd", tg), SC("varp", tg), AF.Ln, bias=A("stats")[:, 60:61], scale=1.0 / 1024.0)
                em.act(SC("rstd", tg), SC("rstd", tg), AF.Exp, scale=-0.5)
                em.mark("scalar")
                mm_marks[("s", tg)] = em.cnt("ms_scalar")
            for tg in range(QG):
                y1 = A("y1n")[:, tg * 1024:(tg + 1) * 1024]
                em.wait("vector", "ms_scalar", mm_marks[("s", tg)])
                em.ts("vector", y1, y1, SC("negmean", tg), SC("rstd", tg), ALU.add, ALU.mult)
                em.mark("vector")
            nb, ntile = (b, t + 1) if t + 1 < NT else (b + 1, 0)
            if nb < B:
                em.wait("sync", "ms_vector", em.cnt("ms_vector"))
                em.dma("sync", A("x").rearrange("p (g d) -> p g d", d=1024),
                       xin[nb][ntile * T:(ntile + 1) * T, :].rearrange("(g p) d -> p g d", p=128),
                       "s_x")
                x_loaded[0] = True

            ck()
            em.wait("vector", "s_out0", em.cnt("s_out0"))
            em.wait("vector", "s_out1", em.cnt("s_out1"))
            aq_marks = {}
            for tg in range(QG):
                y1 = A("y1n")[:, tg * 1024:(tg + 1) * 1024]
                em.red("vector", SC("rmax", tg), y1, "max", absval=True)
                em.ts("vector", SC("rmax", tg), SC("rmax", tg), 1e-5, None, ALU.max)
                em.recip("vector", SC("sx", tg), SC("rmax", tg))
                em.ts("vector", SC("sx", tg), SC("sx", tg), 127.0)
                em.ts("vector", SC("invsx", tg), SC("rmax", tg), 1.0 / 127.0)
                em.ts("vector", A("outst"), y1, SC("sx", tg), None, ALU.mult)
                em.ts("vector", A("aq1i"), A("outst"), A("mcols")[:, 7:8], None, ALU.add)
                em.ts("vector", A("xqTM")[:, tg * 1024:(tg + 1) * 1024], A("aq1i"),
                      -256.0, SC("invsx", tg), ALU.add, ALU.mult)
                em.mark("vector")
                aq_marks[tg] = em.cnt("ms_vector")
            for tg in range(QG):
                bk = 6 + tg % 2
                em.wait("tensor", "ms_vector", aq_marks[tg])
                if tg >= 2:
                    em.wait("tensor", "ms_scalar", aq_marks[("e", tg - 2)])
                for kk in range(8):
                    em.trans(pb(bk, kk * 128, (kk + 1) * 128),
                             A("xqTM")[:, tg * 1024 + kk * 128:tg * 1024 + (kk + 1) * 128],
                             A("ident"))
                em.mark("tensor")
                em.join("scalar", "tensor")
                dst = A("xqT").rearrange("p (k t) -> p k t", k=8)[:, :, tg * 128:(tg + 1) * 128]
                em.act(dst, pb(bk).rearrange("p (k c) -> p k c", k=8), AF.Identity)
                em.mark("scalar")
                aq_marks[("e", tg)] = em.cnt("ms_scalar")
            em.join("tensor", "scalar")
            em.table("silu_and_others")
            if tile_idx == 0:
                for s in ("s_twg0", "s_twg1"):
                    em.wait("gpsimd", s, em.cnt(s))
            for e in ("tensor", "vector", "scalar", "gpsimd"):
                for f in ("tensor", "vector", "scalar", "gpsimd"):
                    if e != f:
                        em.join(e, f)

            inner3 = A("inner").rearrange("p (c t) -> p t c", c=32)
            aq2_parts = {4: (0, 5, A("mpos")), 9: (5, 10, A("mneg")),
                         14: (10, 15, A("sqscr")[:, 0:512]),
                         19: (15, 20, A("sqscr")[:, 512:1024]),
                         23: (20, 24, A("sxbc")), 27: (24, 28, A("aq2q")),
                         31: (28, 32, A("aq2p"))}
            bank_free = {}
            sg_free = {}
            for og in range(64):
                segi = seg0 + GU0 + og // 8
                if og % 8 == 0:
                    seg_wait_pe(segi)
                bank = og % 8
                if og >= 8:
                    em.wait("tensor", "ms_vector" if (og - 8) % 2 == 1 else "ms_scalar",
                            bank_free[og - 8])
                for kk in range(8):
                    em.mm(pf(bank),
                          seg_ap(segi, ((og % 8) * 8 + kk) * 128, ((og % 8) * 8 + kk + 1) * 128),
                          A("xqT")[:, kk * T:(kk + 1) * T], start=(kk == 0), stop=(kk == 7))
                em.mark("tensor")
                if og % 8 == 7:
                    seg_done(segi)
                cpair = og // 2
                if og % 2 == 0:
                    em.join("scalar", "tensor")
                    if cpair >= 2:
                        em.wait("scalar", "ms_vector", sg_free[cpair - 2])
                    em.act(A("sg")[:, (cpair % 2) * 512:(cpair % 2 + 1) * 512], pf(bank),
                           AF.Silu, scale=A("mcols")[:, 0:1])
                    em.mark("scalar")
                    bank_free[og] = em.cnt("ms_scalar")
                else:
                    em.join("vector", "scalar")
                    em.join("vector", "tensor")
                    em.tt("vector", A("inner")[:, cpair * T:(cpair + 1) * T],
                          A("sg")[:, (cpair % 2) * 512:(cpair % 2 + 1) * 512], pf(bank), ALU.mult)
                    em.mark("vector")
                    bank_free[og] = em.cnt("ms_vector")
                    sg_free[cpair] = em.cnt("ms_vector")
                    if cpair in aq2_parts:
                        c0, c1, pdst = aq2_parts[cpair]
                        em.red("vector", pdst, inner3[:, :, c0:c1], "max", absval=True)
                        em.mark("vector")

            ck()
            for _, _, pdst in list(aq2_parts.values())[1:]:
                em.tt("vector", A("mpos"), A("mpos"), pdst, ALU.max)
            em.mark("vector")
            em.join("tensor", "vector")
            em.join("tensor", "scalar")
            for tg in range(QG):
                em.trans(pf(0, tg * 128, (tg + 1) * 128),
                         A("mpos")[:, tg * 128:(tg + 1) * 128], A("identf32"))
            em.mark("tensor")
            em.join("vector", "tensor")
            for tg in range(QG):
                em.red("vector", SC("rmax", tg), pf(0, tg * 128, (tg + 1) * 128), "max")
            for tg in range(QG):
                em.ts("vector", SC("rmax", tg), SC("rmax", tg), A("mcols")[:, 2:3], None, ALU.max)
                em.tt("vector", SC("scale2", tg), SC("rmax", tg), A("mcols")[:, 3:4], ALU.mult)
                em.recip("vector", SC("sx2", tg), SC("rmax", tg))
                em.ts("vector", SC("sx2", tg), SC("sx2", tg), 127.0)
            for tg in range(QG):
                em.copy("vector", SC("tmp", tg), SC("sx2", tg))
            em.mark("vector")
            em.join("tensor", "vector")
            for tg in range(QG):
                em.trans(pf(1, tg * 128, (tg + 1) * 128, 0, 1), SC("tmp", tg), A("identf32"))
            em.mark("tensor")
            em.join("vector", "tensor")
            em.copy("vector", A("sxrow")[0:1, 0:512], pf(1, 0, 512, 0, 1))
            em.mark("vector")
            em.join("tensor", "vector")
            em.mm(pf(2), A("rowpack")[0:1, 2048:2176], A("sxrow")[0:1, 0:512],
                  start=True, stop=True)
            em.mark("tensor")
            em.join("vector", "tensor")
            em.copy("vector", A("sxbc"), pf(2))
            em.mark("vector")

            ck()
            em.join("gpsimd", "vector")
            q2_marks = {}
            for kk in range(32):
                segi = seg0 + DN0 + kk // 8
                if kk % 8 == 0:
                    seg_wait_pe(segi)
                sl = kk % 2
                em.join("gpsimd", "vector")
                if kk >= 2:
                    em.wait("gpsimd", "ms_tensor", q2_marks[kk - 2])
                em.tt("gpsimd", A("sg")[:, sl * 512:(sl + 1) * 512],
                      A("inner")[:, kk * T:(kk + 1) * T], A("sxbc"), ALU.mult)
                em.mark("gpsimd")
                em.join("vector", "gpsimd")
                if kk >= 2:
                    em.wait("vector", "ms_tensor", q2_marks[kk - 2])
                em.ts("vector", A("q2i"), A("sg")[:, sl * 512:(sl + 1) * 512],
                      A("mcols")[:, 7:8], None, ALU.add)
                em.ts("vector", A("q2ch")[:, sl * 512:(sl + 1) * 512], A("q2i"), -256.0, None,
                      ALU.add)
                em.mark("vector")
                em.join("tensor", "vector")
                for tg in range(QG):
                    lh = A("q2ch")[:, sl * 512 + tg * 128:sl * 512 + (tg + 1) * 128]
                    em.mm(pf(2 * tg), lh, seg_ap(segi, (kk % 8) * 1024, (kk % 8) * 1024 + 512),
                          start=(kk == 0), stop=(kk == 31))
                    em.mm(pf(2 * tg + 1), lh,
                          seg_ap(segi, (kk % 8) * 1024 + 512, (kk % 8 + 1) * 1024),
                          start=(kk == 0), stop=(kk == 31))
                em.mark("tensor")
                q2_marks[kk] = em.cnt("ms_tensor")
                if kk % 8 == 7:
                    seg_done(segi)

            ck()
            em.join("vector", "tensor")
            l2m = {}
            for tg in range(QG):
                y2 = A("y2")[:, tg * 1024:(tg + 1) * 1024]
                em.stt("vector", y2[:, 0:512], pf(2 * tg), SC("scale2", tg),
                       A("y1n")[:, tg * 1024:tg * 1024 + 512], ALU.mult, ALU.add)
                em.stt("vector", y2[:, 512:1024], pf(2 * tg + 1), SC("scale2", tg),
                       A("y1n")[:, tg * 1024 + 512:(tg + 1) * 1024], ALU.mult, ALU.add)
                em.red("vector", SC("negmean", tg), y2, "sum", neg=True)
                em.ts("vector", SC("negmean", tg), SC("negmean", tg), 1.0 / 1024.0)
                em.mark("vector")
                l2m[tg] = em.cnt("ms_vector")
            if x_loaded[0]:
                em.join("vector", "gpsimd")
                em.wait("vector", "s_x", em.cnt("s_x"))
                cast_mh[0] = {}
                for c in range((QG + 3) // 4):
                    lo, hi = 4 * c * 1024, min(4 * c + 4, QG) * 1024
                    em.copy("vector", A("xbTM")[:, lo:hi], A("x")[:, lo:hi])
                    em.mark("vector")
                    cast_mh[0][c] = em.cnt("ms_vector")
                x_casted[0] = True
            em.wait("scalar", "s_out1", em.cnt("s_out1"))
            for tg in range(QG):
                y2 = A("y2")[:, tg * 1024:(tg + 1) * 1024]
                em.wait("scalar", "ms_vector", l2m[tg])
                em.act(A("sqscr"), y2, AF.Square, bias=SC("negmean", tg), accum=SC("varp", tg))
                em.mark("scalar")
                l2m[("s", tg)] = em.cnt("ms_scalar")
            for tg in range(QG):
                em.wait("vector", "ms_scalar", l2m[("s", tg)])
                em.ts("vector", SC("varp", tg), SC("varp", tg), 1.0 / 1024.0, 1e-5,
                      ALU.mult, ALU.add)
            em.mark("vector")
            if dump == "z":
                for tg in range(QG):
                    em.join("scalar", "vector")
                    em.dma("scalar", out_rows(b, t, tg),
                           A("y2")[:, tg * 1024:(tg + 1) * 1024], "s_out0")
                em.mark("scalar")
                em.join("vector", "scalar")
            em.join("scalar", "vector")
            prev_stats[0] = (b, t)

    except StopEmission:
        pass
    if stop is not None:
        em.wait("sync", "s_out0", em.cnt("s_out0"))
        em.wait("sync", "s_out1", em.cnt("s_out1"))
        with contextlib.ExitStack() as bes:
            sems = {}
            for name in em.counts:
                sems[name] = bes.enter_context(nc.semaphore(name))
            block = bes.enter_context(nc.Block())
            em.replay(block, sems)
        return nc
    em.table("natural_log_exp_and_others")
    pb_, pt_ = prev_stats[0]
    em.join("vector", "gpsimd")
    for tg in range(QG):
        em.act(SC("rstd", tg), SC("varp", tg), AF.Ln)
        em.act(SC("rstd", tg), SC("rstd", tg), AF.Exp, scale=-0.5)
    em.mark("scalar")
    em.join("vector", "scalar")
    outdma = {}
    for tg in range(QG):
        ob = A("outst") if tg % 2 == 0 else A("outst2")
        sem = "s_out%d" % (tg % 2)
        if tg >= 2:
            em.wait("vector", sem, outdma[tg - 2])
        em.ts("vector", ob, A("y2")[:, tg * 1024:(tg + 1) * 1024],
              SC("negmean", tg), SC("rstd", tg), ALU.add, ALU.mult)
        em.mark("vector")
        em.join("scalar", "vector")
        if dump is None:
            em.dma("scalar", out_rows(pb_, pt_, tg), ob, sem)
        outdma[tg] = em.cnt(sem)

    em.wait("sync", "s_out0", em.cnt("s_out0"))
    em.wait("sync", "s_out1", em.cnt("s_out1"))

    with contextlib.ExitStack() as bes:
        sems = {}
        for name in em.counts:
            sems[name] = bes.enter_context(nc.semaphore(name))
        block = bes.enter_context(nc.Block())
        em.replay(block, sems)

    return nc



def kernel(**inputs):
    cfg = make_cfg()
    n_cores = 8
    nc = build_program(cfg)
    in_maps = [prep_core_inputs(ci, n_cores, cfg, inputs) for ci in range(n_cores)]
    res = run_bass_kernel_spmd(nc, in_maps, core_ids=list(range(n_cores)))
    out = np.concatenate([np.asarray(r["yout"]) for r in res.results], axis=0)
    return out.astype(np.float32)
